# version 09
# baseline (speedup 1.0000x reference)
import os
from contextlib import ExitStack

import numpy as np
import ml_dtypes
import concourse.bass as bass
import concourse.mybir as mybir
from concourse.bass_utils import run_bass_kernel_spmd
from concourse.alu_op_type import AluOpType as OP

F32 = mybir.dt.float32
BF16 = mybir.dt.bfloat16
U32 = mybir.dt.uint32
AFT = mybir.ActivationFunctionType
AXX = mybir.AxisListType.X

H = 1024
NS = 16
FF = 4096
II = 512
NE = 8
TC = 2048
TT = 512
NTT = TC // TT
L = 4096
EPS = 1e-5
HOH = 512
AG1R = H + 32
TQ = 1024
BIG_NEG = -1e30
RG8 = [[0, 1, 2, 3, 4, 5, 6, 7]]

KDBG = ()
RACECHECK = bool(os.environ.get("KERNEL_RACECHECK"))
GU_SKEW = 1
DN_SKEW = 1
YMM_SKEW = 1


class Sem:
    def __init__(self, st, nc, name):
        self.h = st.enter_context(nc.semaphore(name))
        self.n = 0

    def inc(self, inst, k=1):
        inst.then_inc(self.h, k)
        self.n += k
        return inst


def _ap_tname(x):
    try:
        t = x.tensor
        return getattr(t, "name", None) or str(id(t))
    except Exception:
        return None


class EW:
    OPS = ("tensor_tensor", "tensor_scalar", "scalar_tensor_tensor",
           "tensor_copy", "tensor_reduce", "reciprocal", "memset",
           "transpose", "tensor_tensor_scan", "activation", "iota")
    RKEYS = ("in_", "in0", "in1", "data0", "data1", "initial", "scalar",
             "scalar1", "scalar2", "bias", "scale", "mask")

    def __init__(self, eng, sem, waited):
        self._e = eng
        self._s = sem
        self._w = waited
        self._tw = {}
        self._tr = {}

    def __getattr__(self, nm):
        f = getattr(self._e, nm)
        if nm in EW.OPS:
            def wrapped(*a, **k):
                if RACECHECK:
                    key = (id(self._e), id(self._s))
                    if self._w.get(key, -1) < self._s.n:
                        self._e.wait_ge(self._s.h, self._s.n)
                        self._w[key] = self._s.n
                else:
                    wt = k.get("out")
                    if wt is None and nm == "memset" and a:
                        wt = a[0]
                    rd = [k[r] for r in EW.RKEYS if r in k]
                    need = 0
                    wtn = _ap_tname(wt) if wt is not None else None
                    for r in rd:
                        t2 = _ap_tname(r)
                        if t2 is not None:
                            need = max(need, self._tw.get(t2, 0))
                    if wtn is not None:
                        need = max(need, self._tw.get(wtn, 0),
                                   self._tr.get(wtn, 0))
                    key = (id(self._e), id(self._s))
                    if need > self._w.get(key, -1):
                        self._e.wait_ge(self._s.h, need)
                        self._w[key] = need
                r2 = self._s.inc(f(*a, **k))
                if not RACECHECK:
                    if wtn is not None:
                        self._tw[wtn] = self._s.n
                    for r in rd:
                        t2 = _ap_tname(r)
                        if t2 is not None:
                            self._tr[t2] = self._s.n
                return r2
            return wrapped
        return f


def build_program(dbg=()):
    nc = bass.Bass("TRN2", target_bir_lowering=False,
                   detect_race_conditions=RACECHECK)
    t, v0, s0, y, g = nc.tensor, nc.vector, nc.scalar, nc.sync, nc.gpsimd

    def din(name, shape, dt=F32):
        return nc.dram_tensor(name, shape, dt, kind="ExternalInput")

    x_fm = din("x_fm", [H, TC])
    wd_t = din("wd_t", [H, H])
    wbc_t = din("wbc_t", [H, 32])
    w1_t = din("w1_t", [H, FF])
    w2_t = din("w2_t", [FF, H])
    rw_t = din("rw_t", [H, NE])
    wg_e = din("wg_e", [NE, H, II])
    wu_e = din("wu_e", [NE, H, II])
    wdn_e = din("wdn_e", [NE, II, H])
    swg = din("swg", [H, II])
    swu = din("swu", [H, II])
    swd = din("swd", [II, H])
    ln1_g = din("ln1_g", [128, 8])
    ln1_b = din("ln1_b", [128, 8])
    ln2_g = din("ln2_g", [128, 8])
    ln2_b = din("ln2_b", [128, 8])
    bd_c = din("bd_c", [128, 8])
    b2_c = din("b2_c", [128, 8])
    d_c = din("d_c", [128, 8])
    b1_c = din("b1_c", [128, 32])
    bbc_c = din("bbc_c", [32, 1])
    alog2 = din("alog2", [128, 64])
    esel_i = din("esel_i", [128, 16 * 128])
    cbt_i = din("cbt_i", [32, 512])
    gidx = din("gidx", [128, 20], U32)
    yidx = din("yidx", [128, 8], U32)

    out_t = nc.dram_tensor("out", [H, TC], F32, kind="ExternalOutput")

    ag1a_in = nc.dram_tensor("ag1a_in", [AG1R, TC], F32, kind="Internal")
    ag1a_out = nc.dram_tensor("ag1a_out", [8 * AG1R, TC], F32, kind="Internal")
    ag1b_in = nc.dram_tensor("ag1b_in", [H, TC], BF16, kind="Internal")
    ag1b_out = nc.dram_tensor("ag1b_out", [8 * H, TC], BF16, kind="Internal")
    ag2_in = nc.dram_tensor("ag2_in", [2 * HOH, TC], F32, kind="Internal")
    ag2_out = nc.dram_tensor("ag2_out", [8 * 2 * HOH, TC], F32, kind="Internal")
    dloc = nc.dram_tensor("dloc", [HOH, L], BF16, kind="Internal")
    dxloc = nc.dram_tensor("dxloc", [HOH, L], BF16, kind="Internal")
    accin_d = nc.dram_tensor("accin_d", [H, TC], F32, kind="Internal")
    moe_d = nc.dram_tensor("moe_d", [H, TC], F32, kind="Internal")
    bc_seq = nc.dram_tensor("bc_seq", [32, L], BF16, kind="Internal")
    stats_d = nc.dram_tensor("stats_d", [8, TC], F32, kind="Internal")
    comb_d = nc.dram_tensor("comb_d", [NE, TC], BF16, kind="Internal")
    w1_b = nc.dram_tensor("w1_b", [H, FF], BF16, kind="Internal")
    w2_b = nc.dram_tensor("w2_b", [FF, H], BF16, kind="Internal")
    wg_b = nc.dram_tensor("wg_b", [NE, H, II], BF16, kind="Internal")
    wu_b = nc.dram_tensor("wu_b", [NE, H, II], BF16, kind="Internal")
    wdn_b = nc.dram_tensor("wdn_b", [NE, II, H], BF16, kind="Internal")
    swg_b = nc.dram_tensor("swg_b", [H, II], BF16, kind="Internal")
    swu_b = nc.dram_tensor("swu_b", [H, II], BF16, kind="Internal")
    swd_b = nc.dram_tensor("swd_b", [II, H], BF16, kind="Internal")

    dbg_t = {}
    for nm in dbg:
        shp, dt = {
            "xn_bf": ([H, TC], BF16), "delta": ([H, TC], F32),
            "bcv": ([32, L], BF16),
            "yssm": ([2 * HOH, TC], F32), "accin": ([H, TC], F32),
            "logits": ([8, TC], F32), "comb": ([NE, TC], BF16),
            "moe": ([H, TC], F32), "accf": ([H, TC], F32),
            "normed": ([H, TC], BF16), "attno": ([H, TC], F32),
        }[nm]
        dbg_t[nm] = nc.dram_tensor("dbg_" + nm, shp, dt, kind="ExternalOutput")

    gst = ExitStack()

    def sb(stk, name, shape, dt):
        return stk.enter_context(nc.sbuf_tensor(name, shape, dt))

    def psum(stk, name, shape):
        return stk.enter_context(nc.psum_tensor(name, shape, F32))

    sems = {n: Sem(gst, nc, "sem_" + n)
            for n in ("ly", "ls", "lg", "te", "ve", "se", "ge", "cc", "so",
                      "xw", "stl", "agst", "bcl", "yst", "crl", "mst",
                      "a2l", "xml",
                      "dld0", "dld1", "dld2", "dld3", "mwl0", "mwl1", "wdl0", "wdl1",
                      "w1l0", "w1l1", "w1l2", "w1l3", "w2l0", "w2l1",
                      "crl0", "crl1", "a2l0", "a2l1", "ygl0", "ygl1",
                      "xml0", "xml1", "dsg0", "dsg1", "acg0", "acg1", "bcg")}
    ly, ls, lg, te, ve, se, ge, cc, so = (
        sems[n] for n in ("ly", "ls", "lg", "te", "ve", "se", "ge", "cc", "so"))
    xw, stl, agst, bcl, yst, crl, mst, a2l, xml = (
        sems[n] for n in ("xw", "stl", "agst", "bcl", "yst", "crl", "mst",
                          "a2l", "xml"))
    dld = [sems["dld0"], sems["dld1"], sems["dld2"], sems["dld3"]]
    crls = [sems["crl0"], sems["crl1"]]
    a2ls = [sems["a2l0"], sems["a2l1"]]
    ygls = [sems["ygl0"], sems["ygl1"]]
    xmls = [sems["xml0"], sems["xml1"]]
    dsg = [sems["dsg0"], sems["dsg1"]]
    acg = [sems["acg0"], sems["acg1"]]
    bcg = sems["bcg"]
    mwl = [sems["mwl0"], sems["mwl1"]]
    wdl = [sems["wdl0"], sems["wdl1"]]
    w1l = [sems["w1l0"], sems["w1l1"], sems["w1l2"], sems["w1l3"]]
    w2l = [sems["w2l0"], sems["w2l1"]]
    waited = {}
    v = EW(v0, ve, waited)
    s = EW(s0, se, waited)

    def wv(eng, sem, val=None):
        val = sem.n if val is None else val
        e0 = eng._e if isinstance(eng, EW) else eng
        key = (id(e0), id(sem))
        if waited.get(key, -1) >= val:
            return
        e0.wait_ge(sem.h, val)
        waited[key] = val

    def ydma(dst, src, sem=None):
        (sem or ly).inc(y.dma_start(out=dst, in_=src), 16)

    def sdma(dst, src, sem=None):
        (sem or ls).inc(s0.dma_start(out=dst, in_=src), 16)

    def gdma(dst, src, sem=None):
        (sem or lg).inc(g.dma_start(out=dst, in_=src), 16)

    g_tw, g_tr = {}, {}

    def gtt(out, in0, in1, op):
        key = (id(g), id(ge))
        if RACECHECK:
            if waited.get(key, -1) < ge.n:
                g.wait_ge(ge.h, ge.n)
                waited[key] = ge.n
        else:
            need = 0
            wtn = _ap_tname(out)
            for r in (in0, in1):
                t2 = _ap_tname(r)
                if t2 is not None:
                    need = max(need, g_tw.get(t2, 0))
            need = max(need, g_tw.get(wtn, 0), g_tr.get(wtn, 0))
            if need > waited.get(key, -1):
                g.wait_ge(ge.h, need)
                waited[key] = need
        r2 = ge.inc(g.tensor_tensor(out=out, in0=in0, in1=in1, op=op))
        if not RACECHECK:
            g_tw[wtn] = ge.n
            for r in (in0, in1):
                t2 = _ap_tname(r)
                if t2 is not None:
                    g_tr[t2] = ge.n
        return r2

    bank_guard = {}

    def bank_wait(*banks):
        for bk in banks:
            for sem, val in bank_guard.get(id(bk), ()):
                wv(t, sem, val)

    def bank_set(bank, sem):
        bank_guard[id(bank)] = ((sem, sem.n),)

    def barrier():
        for eng in (t, v, s, y, g):
            for sem in sems.values():
                if sem.n and sem is not so and sem is not cc:
                    wv(eng, sem)

    ones_bf = sb(gst, "ones_bf", [128, 1], BF16)
    xn_bf = sb(gst, "xn_bf", [128, 8 * TC], BF16)
    negA = sb(gst, "negA", [128, 64], F32)
    esel = sb(gst, "esel", [128, 16 * 128], BF16)
    states = sb(gst, "states", [128, 64], F32)
    p_ln1g = sb(gst, "p_ln1g", [128, 8], F32)
    p_ln1b = sb(gst, "p_ln1b", [128, 8], F32)
    p_ln2g = sb(gst, "p_ln2g", [128, 8], F32)
    p_ln2b = sb(gst, "p_ln2b", [128, 8], F32)
    p_bd = sb(gst, "p_bd", [128, 8], F32)
    p_b2 = sb(gst, "p_b2", [128, 8], F32)
    p_d = sb(gst, "p_d", [128, 8], F32)
    p_b1 = sb(gst, "p_b1", [128, 32], F32)
    p_bbc = sb(gst, "p_bbc", [32, 1], F32)
    p_cbt = sb(gst, "p_cbt", [32, 512], F32)
    ix_g = sb(gst, "ix_g", [128, 20], U32)
    ix_y = sb(gst, "ix_y", [128, 8], U32)
    alog_sb = sb(gst, "alog_sb", [128, 64], F32)

    ps_a = psum(gst, "ps_a", [128, 512])
    ps_b = psum(gst, "ps_b", [128, 512])
    ps_c = psum(gst, "ps_c", [128, 512])
    ps_dd = psum(gst, "ps_d", [128, 512])

    for dst, src in ((p_ln1g, ln1_g), (p_ln1b, ln1_b), (p_ln2g, ln2_g),
                     (p_ln2b, ln2_b), (p_bd, bd_c), (p_b2, b2_c), (p_d, d_c),
                     (p_b1, b1_c), (p_bbc, bbc_c), (p_cbt, cbt_i),
                     (ix_g, gidx), (ix_y, yidx), (alog_sb, alog2)):
        ydma(dst[:], src.ap())
    gdma(esel[:], esel_i.ap())
    v.memset(ones_bf[:], 1.0)
    v.memset(states[:], 0.0)
    wv(s, ly)
    s.activation(out=negA[:], in_=alog_sb[:], func=AFT.Exp)
    wv(v, se)
    v.tensor_scalar(out=negA[:], in0=negA[:], scalar1=-1.0, scalar2=None,
                    op0=OP.mult)

    fer = ExitStack()
    logits32 = sb(fer, "logits32", [32, TC], F32)
    fe = ExitStack()
    v.memset(logits32[:], 0.0)
    fep = ExitStack()
    ps_s = psum(fep, "ps_s", [1, 512])
    ps_q = psum(fep, "ps_q", [1, 512])
    ps_bc = psum(fep, "ps_bc", [32, 512])
    ps_rt = psum(fep, "ps_rt", [8, 512])

    mu_b = sb(fe, "mu_b", [128, TT], F32)
    r_b = sb(fe, "r_b", [128, TT], F32)
    strow = sb(fe, "strow", [1, 3 * TT], F32)
    wdT = sb(fe, "wdT", [128, 8 * H], BF16)
    wbcT = sb(fe, "wbcT", [128, 8 * 32], BF16)
    rwT = sb(fe, "rwT", [128, 8 * NE], BF16)
    for kc in range(8):
        gdma(wdT[:, kc * H:(kc + 1) * H], wd_t.ap()[kc * 128:(kc + 1) * 128, :])
        gdma(wbcT[:, kc * 32:(kc + 1) * 32], wbc_t.ap()[kc * 128:(kc + 1) * 128, :])
        gdma(rwT[:, kc * NE:(kc + 1) * NE], rw_t.ap()[kc * 128:(kc + 1) * 128, :])
    for db, df in ((wg_b, wg_e), (wu_b, wu_e), (wdn_b, wdn_e), (swg_b, swg),
                   (swu_b, swu), (swd_b, swd), (w1_b, w1_t), (w2_b, w2_t)):
        gdma(db.ap(), df.ap())
    lg_precast = lg.n

    xt = sb(fe, "xt", [128, 8 * TT], F32)
    xb_bf = sb(fe, "xb_bf", [128, 8 * TT], BF16)
    xq_bf = sb(fe, "xq_bf", [128, 8 * TT], BF16)
    xn_f = sb(fe, "xn_f", [128, 8 * TT], F32)
    xn2_f = sb(fe, "xn2_f", [128, 8 * TT], F32)
    xn2b = sb(fe, "xn2b", [128, 8 * TT], BF16)
    dstage = sb(fe, "dstage", [128, 2 * TT], F32)
    dxstage = sb(fe, "dxstage", [128, 2 * TT], BF16)
    bcst = sb(fe, "bcst", [32, TT], F32)
    acst = sb(fe, "acst", [128, 2 * TT], F32)

    def ln_stats(tt, src_bf, sq_bf, row, pss, psq, strow, mu_b, r_b):
        wv(t, ve)
        wv(t, se)
        wv(v, stl)
        bank_wait(pss, psq)
        for hc in range(8):
            t.matmul(pss[:, :], ones_bf[:], src_bf[:, hc * TT:(hc + 1) * TT],
                     start=(hc == 0), stop=(hc == 7))
        for hc in range(8):
            mm = t.matmul(psq[:, :], ones_bf[:], sq_bf[:, hc * TT:(hc + 1) * TT],
                          start=(hc == 0), stop=(hc == 7))
        te.inc(mm)
        wv(v, te)
        v.tensor_scalar(out=strow[0:1, 0:TT], in0=pss[0:1, :],
                        scalar1=1.0 / H, scalar2=None, op0=OP.mult)
        v.tensor_scalar(out=strow[0:1, TT:2 * TT], in0=psq[0:1, :],
                        scalar1=1.0 / H, scalar2=None, op0=OP.mult)
        bank_set(pss, ve)
        bank_set(psq, ve)
        v.tensor_tensor(out=strow[0:1, 2 * TT:3 * TT], in0=strow[0:1, 0:TT],
                        in1=strow[0:1, 0:TT], op=OP.mult)
        v.tensor_tensor(out=strow[0:1, 2 * TT:3 * TT],
                        in0=strow[0:1, 2 * TT:3 * TT],
                        in1=strow[0:1, TT:2 * TT], op=OP.subtract)
        v.tensor_scalar(out=strow[0:1, 2 * TT:3 * TT],
                        in0=strow[0:1, 2 * TT:3 * TT], scalar1=-1.0,
                        scalar2=EPS, op0=OP.mult, op1=OP.add)
        wv(s, ve)
        s.activation(out=strow[0:1, TT:2 * TT], in_=strow[0:1, 2 * TT:3 * TT],
                     func=AFT.Ln)
        s.activation(out=strow[0:1, TT:2 * TT], in_=strow[0:1, TT:2 * TT],
                     func=AFT.Exp, scale=-0.5)
        wv(y, se)
        wv(y, ve)
        ydma(stats_d.ap()[row:row + 1, tt * TT:(tt + 1) * TT], strow[0:1, 0:TT],
             stl)
        ydma(stats_d.ap()[row + 1:row + 2, tt * TT:(tt + 1) * TT],
             strow[0:1, TT:2 * TT], stl)
        wv(y, stl)
        ydma(mu_b[:], bass.AP(stats_d, row * TC + tt * TT, [[0, 128], [1, TT]]),
             stl)
        ydma(r_b[:], bass.AP(stats_d, (row + 1) * TC + tt * TT,
                             [[0, 128], [1, TT]]), stl)

    for tt in range(NTT):
        wv(y, ve)
        wv(y, se)
        ydma(xt[:], bass.AP(x_fm, tt * TT, [[TC, 128], [128 * TC, 8], [1, TT]]),
             xw)
        wv(s, xw)
        s.activation(out=xq_bf[:], in_=xt[:], func=AFT.Square)
        wv(v, xw)
        v.tensor_copy(out=xb_bf[:], in_=xt[:])
        ln_stats(tt, xb_bf, xq_bf, 0, ps_s, ps_q, strow, mu_b, r_b)
        wv(v, stl)
        for hc in range(8):
            cs = slice(hc * TT, (hc + 1) * TT)
            v.tensor_tensor(out=xn_f[:, cs], in0=xt[:, cs], in1=mu_b[:],
                            op=OP.subtract)
            v.tensor_tensor(out=xn_f[:, cs], in0=xn_f[:, cs], in1=r_b[:],
                            op=OP.mult)
        for hc in range(8):
            v.tensor_copy(
                out=xn_bf[:, hc * TC + tt * TT: hc * TC + (tt + 1) * TT],
                in_=xn_f[:, hc * TT:(hc + 1) * TT])
        wv(s, ve)
        s.activation(out=xq_bf[:], in_=xn_f[:], func=AFT.Square)
        s.activation(out=xb_bf[:], in_=xn_f[:], func=AFT.Identity)
        ln_stats(tt, xb_bf, xq_bf, 2, ps_s, ps_q, strow, mu_b, r_b)
        wv(v, stl)
        for hc in range(8):
            cs = slice(hc * TT, (hc + 1) * TT)
            v.tensor_tensor(out=xn2_f[:, cs], in0=xn_f[:, cs], in1=mu_b[:],
                            op=OP.subtract)
            v.tensor_tensor(out=xn2_f[:, cs], in0=xn2_f[:, cs], in1=r_b[:],
                            op=OP.mult)
        wv(s, ve)
        wv(s, te)
        for hc in range(8):
            cs = slice(hc * TT, (hc + 1) * TT)
            s.activation(out=xn2b[:, cs], in_=xn2_f[:, cs], func=AFT.Identity,
                         scale=p_ln1g[:, hc:hc + 1], bias=p_ln1b[:, hc:hc + 1])
        wv(v, se)
        for hc in range(8):
            cs = slice(hc * TT, (hc + 1) * TT)
            v.tensor_copy(out=xn2_f[:, cs], in_=xn2b[:, cs])
        for hc in range(8):
            cs = slice(hc * TT, (hc + 1) * TT)
            stg = acst[:, (hc % 2) * TT:(hc % 2 + 1) * TT]
            wv(v, acg[hc % 2])
            v.scalar_tensor_tensor(out=stg, in0=xn2_f[:, cs],
                                   scalar=p_d[:, hc:hc + 1], in1=xn_f[:, cs],
                                   op0=OP.mult, op1=OP.add)
            wv(y, ve)
            ydma(accin_d.ap()[hc * 128:(hc + 1) * 128, tt * TT:(tt + 1) * TT],
                 stg, acg[hc % 2])
        wv(t, lg)
        wv(t, se)
        for mo in range(8):
            pd = ps_a if (mo % 2 == 0) else ps_b
            dstg = dstage[:, (mo % 2) * TT:(mo % 2 + 1) * TT]
            dxstg = dxstage[:, (mo % 2) * TT:(mo % 2 + 1) * TT]
            bank_wait(pd)
            for kc in range(8):
                mm = t.matmul(pd[:, :],
                              wdT[:, kc * H + mo * 128: kc * H + (mo + 1) * 128],
                              xn2b[:, kc * TT:(kc + 1) * TT],
                              start=(kc == 0), stop=(kc == 7))
            te.inc(mm)
            wv(s, te)
            wv(s, dsg[mo % 2])
            s.activation(out=dstg, in_=pd[:, :], func=AFT.Abs,
                         bias=p_bd[:, mo:mo + 1])
            s.activation(out=dstg, in_=dstg, func=AFT.Exp, scale=-1.0)
            s.activation(out=dstg, in_=dstg, func=AFT.Ln, bias=1.0)
            s.activation(out=dxstg, in_=pd[:, :], func=AFT.Relu,
                         bias=p_bd[:, mo:mo + 1])
            bank_set(pd, se)
            wv(v, se)
            wv(v, dsg[mo % 2])
            v.tensor_tensor(out=dstg, in0=dstg, in1=dxstg, op=OP.add)
            v.tensor_tensor(out=dxstg, in0=dstg,
                            in1=xn2_f[:, mo * TT:(mo + 1) * TT], op=OP.mult)
            wv(y, ve)
            ydma(ag1a_in.ap()[mo * 128:(mo + 1) * 128, tt * TT:(tt + 1) * TT],
                 dstg, dsg[mo % 2])
            ydma(ag1b_in.ap()[mo * 128:(mo + 1) * 128, tt * TT:(tt + 1) * TT],
                 dxstg, dsg[mo % 2])
        bank_wait(ps_bc)
        for kc in range(8):
            mm = t.matmul(ps_bc[:, :], wbcT[:, kc * 32:(kc + 1) * 32],
                          xn2b[:, kc * TT:(kc + 1) * TT],
                          start=(kc == 0), stop=(kc == 7))
        te.inc(mm)
        wv(v, te)
        wv(v, bcg)
        v.tensor_scalar(out=bcst[:], in0=ps_bc[:, :], scalar1=p_bbc[:, 0:1],
                        scalar2=None, op0=OP.add)
        bank_set(ps_bc, ve)
        wv(y, ve)
        ydma(ag1a_in.ap()[H:H + 32, tt * TT:(tt + 1) * TT], bcst[:], bcg)
        bank_wait(ps_rt)
        for kc in range(8):
            mm = t.matmul(ps_rt[:, :], rwT[:, kc * NE:(kc + 1) * NE],
                          xn_bf[:, kc * TC + tt * TT: kc * TC + (tt + 1) * TT],
                          start=(kc == 0), stop=(kc == 7))
        te.inc(mm)
        wv(v, te)
        v.tensor_copy(out=logits32[0:8, tt * TT:(tt + 1) * TT], in_=ps_rt[:, :])
        bank_set(ps_rt, ve)

    if "xn_bf" in dbg_t:
        wv(y, ve)
        for hc in range(8):
            ydma(dbg_t["xn_bf"].ap()[hc * 128:(hc + 1) * 128, :],
                 xn_bf[:, hc * TC:(hc + 1) * TC])
    if "logits" in dbg_t:
        wv(y, ve)
        ydma(dbg_t["logits"].ap()[:, :], logits32[0:8, :])
    if "accin" in dbg_t:
        wv(g, ly)
        wv(g, lg)
        gdma(dbg_t["accin"].ap()[:, :], accin_d.ap()[:, :])
    if "delta" in dbg_t:
        wv(g, ly)
        wv(g, lg)
        gdma(dbg_t["delta"].ap()[:, :], ag1a_in.ap()[0:H, :])

    for sem_ in (dsg[0], dsg[1], acg[0], acg[1], bcg):
        wv(g, sem_)
    cc.inc(g.collective_compute("AllGather", OP.bypass, replica_groups=RG8,
                                ins=[ag1a_in.ap()], outs=[ag1a_out.ap()]))
    cc.inc(g.collective_compute("AllGather", OP.bypass, replica_groups=RG8,
                                ins=[ag1b_in.ap()], outs=[ag1b_out.ap()]))
    barrier()
    fe.close()
    fep.close()

    ro = ExitStack()
    R32 = sb(ro, "R32", [32, 64 * 32], F32)
    sco = sb(ro, "sco", [32, 512], F32)
    msc = sb(ro, "msc", [32, 512], F32)
    gtmp = sb(ro, "gtmp", [32, 128], F32)
    gtmp2 = sb(ro, "gtmp2", [32, 128], F32)
    ge0 = sb(ro, "ge0", [32, 64], F32)
    ge1 = sb(ro, "ge1", [32, 64], F32)
    m1c = sb(ro, "m1c", [32, 64], F32)
    m1r = sb(ro, "m1r", [32, 512], F32)
    f1 = sb(ro, "f1", [32, 512], F32)
    f2 = sb(ro, "f2", [32, 512], F32)
    cum = sb(ro, "cum", [32, 512], F32)
    cum2 = sb(ro, "cum2", [32, 512], F32)
    wsc = sb(ro, "wsc", [32, 64], F32)
    w1c = sb(ro, "w1c", [32, 64], F32)
    w2c = sb(ro, "w2c", [32, 64], F32)
    prodw = sb(ro, "prodw", [32, 512], F32)
    padbuf = sb(ro, "padbuf", [32, 64 * 32], F32)
    combT = sb(ro, "combT", [32, TC], F32)

    def rap(tsr, fsz, off, dims, np_=32):
        return bass.AP(tsr, off, [[fsz, np_]] + dims)

    for j in range(64):
        v.transpose(out=R32[0:32, j * 32:(j + 1) * 32],
                    in_=logits32[0:32, j * 32:(j + 1) * 32])
    wv(s, ve)
    s.activation(out=R32[:], in_=R32[:], func=AFT.Tanh, scale=0.5)
    wv(v, se)
    v.tensor_scalar(out=rap(sco, 512, 0, [[8, 64], [1, 8]]),
                    in0=rap(R32, 64 * 32, 0, [[32, 64], [1, 8]]),
                    scalar1=0.5, scalar2=0.5, op0=OP.mult, op1=OP.add)
    wv(v, ly)
    v.tensor_tensor(out=msc[:], in0=sco[:], in1=p_cbt[:], op=OP.add)
    for idx, (i, j) in enumerate([(0, 1), (0, 2), (0, 3), (1, 2), (1, 3),
                                  (2, 3)]):
        dst = gtmp if idx == 0 else gtmp2
        v.tensor_tensor(out=dst[:], in0=rap(msc, 512, i, [[4, 128]]),
                        in1=rap(msc, 512, j, [[4, 128]]), op=OP.add)
        if idx > 0:
            v.tensor_tensor(out=gtmp[:], in0=gtmp[:], in1=gtmp2[:], op=OP.max)
    v.tensor_tensor(out=ge0[:], in0=rap(gtmp, 128, 0, [[2, 64]]),
                    in1=rap(gtmp, 128, 1, [[2, 64]]), op=OP.is_ge)
    v.tensor_tensor(out=ge1[:], in0=rap(gtmp, 128, 1, [[2, 64]]),
                    in1=rap(gtmp, 128, 0, [[2, 64]]), op=OP.is_gt)
    for e in range(4):
        v.tensor_copy(out=rap(m1r, 512, e, [[8, 64]]), in_=ge0[:])
        v.tensor_copy(out=rap(m1r, 512, 4 + e, [[8, 64]]), in_=ge1[:])
    v.tensor_tensor(out=msc[:], in0=msc[:], in1=m1r[:], op=OP.mult)

    def top_select(fdst):
        v.tensor_reduce(out=m1c[:], in_=rap(msc, 512, 0, [[8, 64], [1, 8]]),
                        axis=AXX, op=OP.max)
        for e in range(8):
            v.tensor_copy(out=rap(m1r, 512, e, [[8, 64]]), in_=m1c[:])
        v.tensor_tensor(out=fdst[:], in0=msc[:], in1=m1r[:], op=OP.is_ge)
        v.tensor_copy(out=cum[:], in_=fdst[:])
        src, dst = cum, cum2
        for shd in (1, 2, 4):
            v.tensor_copy(out=dst[:], in_=src[:])
            v.tensor_tensor(
                out=rap(dst, 512, shd, [[8, 64], [1, 8 - shd]]),
                in0=rap(src, 512, shd, [[8, 64], [1, 8 - shd]]),
                in1=rap(src, 512, 0, [[8, 64], [1, 8 - shd]]), op=OP.add)
            src, dst = dst, src
        v.tensor_scalar(out=dst[:], in0=src[:], scalar1=1.0, scalar2=None,
                        op0=OP.is_le)
        v.tensor_tensor(out=fdst[:], in0=fdst[:], in1=dst[:], op=OP.mult)

    top_select(f1)
    v.tensor_tensor(out=prodw[:], in0=f1[:], in1=sco[:], op=OP.mult)
    v.tensor_reduce(out=w1c[:], in_=rap(prodw, 512, 0, [[8, 64], [1, 8]]),
                    axis=AXX, op=OP.add)
    v.scalar_tensor_tensor(out=msc[:], in0=f1[:], scalar=BIG_NEG, in1=msc[:],
                           op0=OP.mult, op1=OP.add)
    top_select(f2)
    v.tensor_tensor(out=prodw[:], in0=f2[:], in1=sco[:], op=OP.mult)
    v.tensor_reduce(out=w2c[:], in_=rap(prodw, 512, 0, [[8, 64], [1, 8]]),
                    axis=AXX, op=OP.add)
    v.tensor_tensor(out=wsc[:], in0=w1c[:], in1=w2c[:], op=OP.add)
    v.tensor_scalar(out=wsc[:], in0=wsc[:], scalar1=2.0, scalar2=2e-20,
                    op0=OP.mult, op1=OP.add)
    v.reciprocal(out=wsc[:], in_=wsc[:])
    for e in range(8):
        v.tensor_copy(out=rap(m1r, 512, e, [[8, 64]]), in_=w1c[:])
    v.tensor_tensor(out=f1[:], in0=f1[:], in1=m1r[:], op=OP.mult)
    for e in range(8):
        v.tensor_copy(out=rap(m1r, 512, e, [[8, 64]]), in_=w2c[:])
    v.tensor_tensor(out=f2[:], in0=f2[:], in1=m1r[:], op=OP.mult)
    v.tensor_tensor(out=f1[:], in0=f1[:], in1=f2[:], op=OP.add)
    for e in range(8):
        v.tensor_copy(out=rap(m1r, 512, e, [[8, 64]]), in_=wsc[:])
    v.tensor_tensor(out=f1[:], in0=f1[:], in1=m1r[:], op=OP.mult)
    v.memset(padbuf[:], 0.0)
    v.tensor_copy(out=rap(padbuf, 64 * 32, 0, [[32, 64], [1, 8]]),
                  in_=rap(f1, 512, 0, [[8, 64], [1, 8]]))
    for j in range(64):
        v.transpose(out=combT[0:32, j * 32:(j + 1) * 32],
                    in_=padbuf[0:32, j * 32:(j + 1) * 32])
    wv(g, ve)
    gdma(comb_d.ap()[:, :], combT[0:8, :])
    lg_comb = lg.n
    if "comb" in dbg_t:
        wv(g, lg)
        gdma(dbg_t["comb"].ap()[:, :], comb_d.ap()[:, :])
    barrier()
    ro.close()
    fer.close()

    sm = ExitStack()
    pro = ExitStack()
    gstg = sb(pro, "gstg", [128, TC], F32)
    gstgb = sb(pro, "gstgb", [128, TC], BF16)
    bcs = sb(pro, "bcs", [32, 2 * TC], F32)
    wv(g, cc, 2)
    for sh in range(2):
        lg.inc(g.indirect_dma_start(
            out=bcs[0:32, sh * TC:(sh + 1) * TC], out_offset=None,
            in_=bass.AP(ag1a_out, 0, [[TC, 8 * AG1R], [1, TC]]),
            in_offset=bass.IndirectOffsetOnAxis(
                ix_g[0:32, 16 + sh:17 + sh], 0)), 16)
    wv(g, lg)
    for sh in range(2):
        gdma(bc_seq.ap()[:, sh * TC:(sh + 1) * TC],
             bcs[0:32, sh * TC:(sh + 1) * TC])
    for sh in range(2):
        for c in range(4):
            wv(g, lg)
            lg.inc(g.indirect_dma_start(
                out=gstg[:, :], out_offset=None,
                in_=bass.AP(ag1a_out, 0, [[TC, 8 * AG1R], [1, TC]]),
                in_offset=bass.IndirectOffsetOnAxis(
                    ix_g[:, sh * 4 + c: sh * 4 + c + 1], 0)), 16)
            wv(g, lg)
            gdma(dloc.ap()[c * 128:(c + 1) * 128, sh * TC:(sh + 1) * TC],
                 gstg[:, :])
    for sh in range(2):
        for c in range(4):
            wv(g, lg)
            lg.inc(g.indirect_dma_start(
                out=gstgb[:, :], out_offset=None,
                in_=bass.AP(ag1b_out, 0, [[TC, 8 * H], [1, TC]]),
                in_offset=bass.IndirectOffsetOnAxis(
                    ix_g[:, 8 + sh * 4 + c: 8 + sh * 4 + c + 1], 0)), 16)
            wv(g, lg)
            gdma(dxloc.ap()[c * 128:(c + 1) * 128, sh * TC:(sh + 1) * TC],
                 gstgb[:, :])
    if "bcv" in dbg_t:
        wv(g, lg)
        gdma(dbg_t["bcv"].ap()[:, :], bc_seq.ap()[:, :])
    pro.close()
    drep = sb(sm, "drep", [128, 4 * TQ], BF16)
    dxrep = sb(sm, "dxrep", [128, 4 * TQ], BF16)
    brepq = sb(sm, "brepq", [128, 2 * TQ], BF16)
    crepq = sb(sm, "crepq", [128, 2 * TQ], BF16)
    dAt = sb(sm, "dAt", [128, 4 * TQ], F32)
    dBxt = sb(sm, "dBxt", [128, 4 * TQ], F32)
    Hsb = sb(sm, "Hsb", [128, TQ], BF16)
    prodb = sb(sm, "prodb", [128, 2 * TQ], BF16)
    ystg = sb(sm, "ystg", [128, TQ], F32)
    ps_y = [psum(sm, f"ps_y{i}", [128, 512]) for i in range(4)]

    wgs = sb(sm, "wgs", [128, 2 * 8 * II], BF16)
    wus = sb(sm, "wus", [128, 2 * 8 * II], BF16)
    wdmo = sb(sm, "wdmo", [128, 2 * 4 * H], BF16)
    wsdm = sb(sm, "wsdm", [128, 2 * 4 * 128], BF16)
    h_all = sb(sm, "h_all", [128, NE * 4 * TT], BF16)
    h_sh = sb(sm, "h_sh", [128, 4 * TT], BF16)
    combr = sb(sm, "combr", [128, 2 * TT], BF16)
    g_bf = sb(sm, "g_bf", [128, 2 * TT], BF16)
    u_bf = sb(sm, "u_bf", [128, 2 * TT], BF16)
    tnh = sb(sm, "tnh", [128, 2 * TT], BF16)
    h1t = sb(sm, "h1t", [128, 2 * TT], BF16)
    mstage = sb(sm, "mstage", [128, 2 * TT], F32)

    dA_done, dbx_done, scan_done, prod_done = {}, {}, {}, {}
    drep_ld, dxrep_ld, ymm_done, evac_done, yq_dma = {}, {}, {}, {}, {}
    brep_ld = {}
    ypend = []
    ypend_flush = {}

    def emit_ymm(item):
        u, par, kt = item
        cidx2 = u // 16
        pyA2, pyB2, chunk2, sh2, twp2 = ypend_flush[cidx2]
        wv(t, ve, prod_done[u])
        if kt == 0:
            bank_wait(pyA2, pyB2)
        t.matmul(pyA2[:, :], esel[:, kt * 128:(kt + 1) * 128],
                 bass.AP(prodb, par * TQ, [[2 * TQ, 128], [1, 512]]),
                 start=(kt == 0), stop=(kt == 15))
        mm = t.matmul(pyB2[:, :], esel[:, kt * 128:(kt + 1) * 128],
                      bass.AP(prodb, par * TQ + 512,
                              [[2 * TQ, 128], [1, 512]]),
                      start=(kt == 0), stop=(kt == 15))
        te.inc(mm)
        ymm_done[u] = te.n
        if kt == 15:
            del ypend_flush[cidx2]
            wv(v, te, ymm_done[u])
            wv(v, yst)
            v.tensor_copy(out=ystg[:, 0:512], in_=pyA2[:, :])
            v.tensor_copy(out=ystg[:, 512:TQ], in_=pyB2[:, :])
            bank_set(pyA2, ve)
            bank_set(pyB2, ve)
            evac_done[cidx2] = ve.n
            wv(y, ve, evac_done[cidx2])
            wv(y, yst)
            ydma(bass.AP(ag2_in, (sh2 * HOH + chunk2 * 128) * TC + twp2,
                         [[TC, 128], [1, TQ]]),
                 ystg[:, 0:TQ], yst)

    def emit_ssm():
        lg_pro = lg.n
        yield
        def unit_loads(u2):
            if u2 >= 256:
                return
            tq2, tile2 = u2 // 64, u2 % 64
            par2 = u2 % 4
            dsl2 = slice(par2 * TQ, (par2 + 1) * TQ)
            wv(y, se, dA_done.get(u2 - 4, 0))
            wv(y, ge, dbx_done.get(u2 - 4, 0))

            ydma(drep[:, dsl2],
                 bass.AP(dloc, (8 * tile2) * L + tq2 * TQ,
                         [[L, 8], [0, 16], [1, TQ]]), dld[par2])
            ydma(dxrep[:, dsl2],
                 bass.AP(dxloc, (8 * tile2) * L + tq2 * TQ,
                         [[L, 8], [0, 16], [1, TQ]]), dld[par2])

        wv(y, lg, lg_pro)
        for uu in range(3):
            unit_loads(uu)
        for tq in range(4):
            sh, twp = tq // 2, (tq % 2) * TQ
            qpar = tq % 2
            qsl = slice(qpar * TQ, (qpar + 1) * TQ)
            wv(y, ve, scan_done.get(("q", tq - 2), 0))
            wv(y, bcl)
            ydma(brepq[:, qsl], bass.AP(bc_seq, tq * TQ,
                                        [[0, 8], [L, 16], [1, TQ]]), bcl)
            ydma(crepq[:, qsl], bass.AP(bc_seq, 16 * L + tq * TQ,
                                        [[0, 8], [L, 16], [1, TQ]]), bcl)
            for chunk in range(4):
                pyA = ps_y[(chunk % 2) * 2]
                pyB = ps_y[(chunk % 2) * 2 + 1]
                ypend_flush[tq * 4 + chunk] = (pyA, pyB, chunk, sh, twp)
                for kt in range(16):
                    tile = chunk * 16 + kt
                    u = tq * 64 + tile
                    lpar = u % 4
                    par = u % 4
                    lsl = slice(lpar * TQ, (lpar + 1) * TQ)
                    dsl = slice(par * TQ, (par + 1) * TQ)
                    unit_loads(u + 3)
                    wv(s, dld[lpar])
                    wv(s, ve, scan_done.get(u - 4, 0))
                    s.activation(out=dAt[:, dsl], in_=drep[:, lsl],
                                 func=AFT.Exp, scale=negA[:, tile:tile + 1])
                    dA_done[u] = se.n
                    wv(g, dld[lpar])
                    wv(g, bcl)
                    wv(g, ve, scan_done.get(u - 4, 0))
                    gtt(dBxt[:, dsl], dxrep[:, lsl], brepq[:, qsl], OP.mult)
                    dbx_done[u] = ge.n
                    wv(v, se, dA_done[u])
                    wv(v, ge, dbx_done[u])
                    v.tensor_tensor_scan(out=Hsb[:, :], data0=dAt[:, dsl],
                                         data1=dBxt[:, dsl],
                                         initial=states[:, tile:tile + 1],
                                         op0=OP.mult, op1=OP.add)
                    v.tensor_copy(out=states[:, tile:tile + 1],
                                  in_=Hsb[:, TQ - 1:TQ])
                    ppar = u % 2
                    psl = slice(ppar * TQ, (ppar + 1) * TQ)
                    wv(v, te, ymm_done.get(u - 2, 0))
                    wv(v, bcl)
                    v.tensor_tensor(out=prodb[:, psl], in0=Hsb[:, :],
                                    in1=crepq[:, qsl], op=OP.mult)
                    scan_done[u] = ve.n
                    if tile == 63:
                        scan_done[("q", tq)] = ve.n
                    prod_done[u] = ve.n
                    ypend.append((u, u % 2, kt))
                    if len(ypend) > YMM_SKEW:
                        emit_ymm(ypend.pop(0))
                yield
            if tq == 1:
                while ypend:
                    emit_ymm(ypend.pop(0))
                wv(g, yst)
                cc.inc(g.collective_compute(
                    "AllGather", OP.bypass, replica_groups=RG8,
                    ins=[ag2_in.ap()[0:HOH, :]],
                    outs=[ag2_out.ap()[0:8 * HOH, :]]))
        while ypend:
            emit_ymm(ypend.pop(0))
        wv(g, yst)
        cc.inc(g.collective_compute(
            "AllGather", OP.bypass, replica_groups=RG8,
            ins=[ag2_in.ap()[HOH:2 * HOH, :]],
            outs=[ag2_out.ap()[8 * HOH:16 * HOH, :]]))
        if "yssm" in dbg_t:
            wv(g, cc)
            wv(g, ly)
            gdma(dbg_t["yssm"].ap()[:, :], ag2_in.ap()[:, :])
        yield

    gu_evac, h_ready, mo_evac, mo_dma, dn_done, hcons = {}, {}, {}, {}, {}, {}
    ld_w, hr_last = {}, {}
    gu_u = [0]
    gupend, dnpend = [], []
    combr_guard = {}

    def moe_w_load(tt, e):
        ex = tt * 9 + (e + 1)
        slot = ex % 2
        ydma(wgs[:, slot * 8 * II:(slot + 1) * 8 * II],
             bass.AP(wg_b if e >= 0 else swg_b, max(e, 0) * H * II,
                     [[II, 128], [128 * II, 8], [1, II]]), mwl[slot])
        ydma(wus[:, slot * 8 * II:(slot + 1) * 8 * II],
             bass.AP(wu_b if e >= 0 else swu_b, max(e, 0) * H * II,
                     [[II, 128], [128 * II, 8], [1, II]]), mwl[slot])
        ld_w[(tt, e)] = (slot, mwl[slot].n)


    def flush_gu(item):
        tt, e, kc4, pg, pu, ug, mmv, crv, cpar = item
        parq = ug % 2
        csl = slice(parq * TT, (parq + 1) * TT)
        wv(s, te, mmv)
        hprev = h_ready.get(ug - 2, (0, 0))
        wv(s, ve, hprev[0])
        wv(s, ge, hprev[1])
        s.activation(out=tnh[:, csl], in_=pg[:, :], func=AFT.Tanh, scale=0.5)
        s.activation(out=g_bf[:, csl], in_=pg[:, :], func=AFT.Identity)
        s.activation(out=u_bf[:, csl], in_=pu[:, :], func=AFT.Identity)
        bank_set(pg, se)
        bank_set(pu, se)
        gu_evac[((tt, e), kc4)] = se.n
        wv(v, se, se.n)
        wv(v, ge, hprev[1])
        v.scalar_tensor_tensor(out=h1t[:, csl], in0=tnh[:, csl],
                               scalar=1.0, in1=u_bf[:, csl],
                               op0=OP.add, op1=OP.mult)
        if e < 0:
            wv(v, te, dn_done.get((tt - 1, 7), 0))
            v.scalar_tensor_tensor(
                out=h_sh[:, kc4 * TT:(kc4 + 1) * TT],
                in0=h1t[:, csl], scalar=0.5, in1=g_bf[:, csl],
                op0=OP.mult, op1=OP.mult)
        else:
            wv(g, ve, ve.n)
            wv(g, crls[cpar], crv)
            wv(g, te, dn_done.get((tt - 1, 7), 0))
            gtt(h1t[:, csl], h1t[:, csl], g_bf[:, csl], OP.mult)
            gtt(h_all[:, (e * 4 + kc4) * TT:(e * 4 + kc4 + 1) * TT],
                h1t[:, csl], combr[:, cpar * TT:(cpar + 1) * TT], OP.mult)
            combr_guard[cpar] = ge.n
        h_ready[ug] = (ve.n, ge.n)
        hr_last[(tt, e)] = (ve.n, ge.n)

    def flush_dn(item):
        tt, mo, pd2 = item
        wv(s, te, dn_done[(tt, mo)])
        wv(s, mst)
        s.activation(out=mstage[:, (mo % 2) * TT:(mo % 2 + 1) * TT],
                     in_=pd2[:, :], func=AFT.Identity)
        bank_set(pd2, se)
        mo_evac[(tt, mo)] = se.n
        wv(y, se, se.n)
        wv(y, mst)
        ydma(moe_d.ap()[mo * 128:(mo + 1) * 128, tt * TT:(tt + 1) * TT],
             mstage[:, (mo % 2) * TT:(mo % 2 + 1) * TT], mst)

    def emit_moe():
        wv(y, lg, lg_precast)
        moe_w_load(0, -1)
        moe_w_load(0, 0)
        for tt in range(NTT):
            for e in range(-1, NE):
                slot = (tt * 9 + e + 1) % 2
                wsl_g = wgs[:, slot * 8 * II:(slot + 1) * 8 * II]
                wsl_u = wus[:, slot * 8 * II:(slot + 1) * 8 * II]
                cpar = (e + 1) % 2
                if e >= 0:
                    wv(y, ge, combr_guard.get(cpar, 0))
                    wv(y, lg, lg_comb)
                    ydma(combr[:, cpar * TT:(cpar + 1) * TT],
                         bass.AP(comb_d, e * TC + tt * TT, [[0, 128], [1, TT]]),
                         crls[cpar])
                    crv = crls[cpar].n
                else:
                    crv = 0
                for kc4 in range(4):
                    pg, pu = (ps_a, ps_b) if kc4 % 2 == 0 else (ps_c, ps_dd)
                    wv(t, mwl[ld_w[(tt, e)][0]], ld_w[(tt, e)][1])
                    bank_wait(pg, pu)
                    for kc in range(8):
                        t.matmul(pg[:, :],
                                 wsl_g[:, kc * II + kc4 * 128:
                                       kc * II + (kc4 + 1) * 128],
                                 xn_bf[:, kc * TC + tt * TT:
                                       kc * TC + (tt + 1) * TT],
                                 start=(kc == 0), stop=(kc == 7))
                    for kc in range(8):
                        mm = t.matmul(pu[:, :],
                                      wsl_u[:, kc * II + kc4 * 128:
                                            kc * II + (kc4 + 1) * 128],
                                      xn_bf[:, kc * TC + tt * TT:
                                            kc * TC + (tt + 1) * TT],
                                      start=(kc == 0), stop=(kc == 7))
                    te.inc(mm)
                    mmv = te.n
                    gu_u[0] += 1
                    ug = gu_u[0]
                    gupend.append((tt, e, kc4, pg, pu, ug, mmv, crv, cpar))
                    if len(gupend) > GU_SKEW:
                        flush_gu(gupend.pop(0))
                    mm_last = te.n
                    yield
                nxt_e, nxt_tt = e + 2, tt
                if nxt_e >= NE:
                    nxt_e -= (NE + 1)
                    nxt_tt += 1
                if nxt_tt < NTT:
                    wv(y, te, mm_last)
                    moe_w_load(nxt_tt, nxt_e)

            while gupend:
                flush_gu(gupend.pop(0))
            for mo in range(8):
                pd2 = ps_a if mo % 2 == 0 else ps_b
                slot = mo % 2
                wv(y, te, dn_done.get((tt, mo - 2), 0))
                ydma(wdmo[:, slot * 4 * H:(slot + 1) * 4 * H],
                     bass.AP(wdn_b, mo * 128,
                             [[H, 128], [II * H, 8], [128 * H, 4], [1, 128]]),
                     wdl[slot])
                ydma(wsdm[:, slot * 4 * 128:(slot + 1) * 4 * 128],
                     bass.AP(swd_b, mo * 128,
                             [[H, 128], [128 * H, 4], [1, 128]]), wdl[slot])
                wv(t, ve, hr_last[(tt, NE - 1)][0])
                wv(t, ge, hr_last[(tt, NE - 1)][1])
                wv(t, ve, hr_last[(tt, -1)][0])
                wv(t, wdl[slot])
                bank_wait(pd2)
                for e in range(NE):
                    for kc in range(4):
                        t.matmul(pd2[:, :],
                                 wdmo[:, slot * 4 * H + (e * 4 + kc) * 128:
                                       slot * 4 * H + (e * 4 + kc + 1) * 128],
                                 h_all[:, (e * 4 + kc) * TT:
                                       (e * 4 + kc + 1) * TT],
                                 start=(e == 0 and kc == 0), stop=False)
                for kc in range(4):
                    mm = t.matmul(pd2[:, :],
                                  wsdm[:, slot * 4 * 128 + kc * 128:
                                        slot * 4 * 128 + (kc + 1) * 128],
                                  h_sh[:, kc * TT:(kc + 1) * TT],
                                  start=False, stop=(kc == 3))
                te.inc(mm)
                dn_done[(tt, mo)] = te.n
                dnpend.append((tt, mo, pd2))
                if len(dnpend) > DN_SKEW:
                    flush_dn(dnpend.pop(0))
                yield
            while dnpend:
                flush_dn(dnpend.pop(0))
        if "moe" in dbg_t:
            wv(g, mst)
            wv(g, lg)
            gdma(dbg_t["moe"].ap()[:, :], moe_d.ap()[:, :])
        yield

    gen_s = emit_ssm()
    gen_m = emit_moe()
    alive_s, alive_m = True, True
    for _ in range(46):
        try:
            next(gen_m)
        except StopIteration:
            alive_m = False
            break
    next(gen_s)
    while alive_s or alive_m:
        if alive_s:
            try:
                next(gen_s)
            except StopIteration:
                alive_s = False
        if alive_m:
            for _ in range(8):
                try:
                    next(gen_m)
                except StopIteration:
                    alive_m = False
                    break
    barrier()
    sm.close()

    p4 = ExitStack()
    p4p = ExitStack()
    ps_s2 = psum(p4p, "ps_s2", [1, 512])
    ps_q2 = psum(p4p, "ps_q2", [1, 512])
    accf = sb(p4, "accf", [128, 8 * TC], F32)
    mu_b = sb(p4, "mu_b2", [128, TT], F32)
    r_b = sb(p4, "r_b2", [128, TT], F32)
    strow = sb(p4, "strow2", [1, 3 * TT], F32)
    p4a = ExitStack()
    ystage = sb(p4a, "ystage", [128, 2 * TT], F32)
    astage = sb(p4a, "astage", [128, 2 * TT], F32)
    ac_bf = sb(p4a, "ac_bf", [128, 8 * TT], BF16)
    aq_bf = sb(p4a, "aq_bf", [128, 8 * TT], BF16)

    yg_cons, hff_done, ots_done = {}, {}, {}
    w1ld, w2ld = {}, {}

    wv(g, cc, 4)
    ag2av = bass.AP(ag2_out, 0, [[TT, 8 * 2 * HOH * 4], [1, TT]])
    for tt in range(NTT):
        for hc in range(8):
            uy = tt * 8 + hc
            ypar = hc % 2
            wv(g, ve, yg_cons.get(uy - 2, 0))
            ygls[ypar].inc(g.indirect_dma_start(
                out=ystage[:, ypar * TT:(ypar + 1) * TT], out_offset=None,
                in_=ag2av,
                in_offset=bass.IndirectOffsetOnAxis(ix_y[:, hc:hc + 1], 0),
                element_offset=tt * TT), 16)
            wv(y, ve, yg_cons.get(uy - 2, 0))
            ydma(astage[:, ypar * TT:(ypar + 1) * TT],
                 accin_d.ap()[hc * 128:(hc + 1) * 128, tt * TT:(tt + 1) * TT],
                 a2ls[ypar])
            wv(v, ygls[ypar])
            wv(v, a2ls[ypar])
            v.tensor_tensor(
                out=accf[:, hc * TC + tt * TT: hc * TC + (tt + 1) * TT],
                in0=astage[:, ypar * TT:(ypar + 1) * TT],
                in1=ystage[:, ypar * TT:(ypar + 1) * TT], op=OP.add)
            yg_cons[uy] = ve.n
    if "accf" in dbg_t:
        wv(y, ve)
        for hc in range(8):
            ydma(dbg_t["accf"].ap()[hc * 128:(hc + 1) * 128, :],
                 accf[:, hc * TC:(hc + 1) * TC])
    for tt in range(NTT):
        wv(s, ve)
        for hc in range(8):
            s.activation(out=aq_bf[:, hc * TT:(hc + 1) * TT],
                         in_=accf[:, hc * TC + tt * TT: hc * TC + (tt + 1) * TT],
                         func=AFT.Square)
            s.activation(out=ac_bf[:, hc * TT:(hc + 1) * TT],
                         in_=accf[:, hc * TC + tt * TT: hc * TC + (tt + 1) * TT],
                         func=AFT.Identity)
        ln_stats(tt, ac_bf, aq_bf, 4, ps_s2, ps_q2, strow, mu_b, r_b)
    barrier()
    p4a.close()
    nstage = sb(p4, "nstage", [128, 8 * TT], BF16)
    hff = sb(p4, "hff", [128, 32 * TT], BF16)
    w1s = sb(p4, "w1s", [128, 4 * 8 * 128], BF16)
    w2s = sb(p4, "w2s", [128, 2 * 32 * 128], BF16)
    xts = sb(p4, "xts", [128, 2 * TT], F32)
    mts = sb(p4, "mts", [128, 2 * TT], F32)
    ots = sb(p4, "ots", [128, 2 * TT], F32)
    astage = sb(p4, "astage2", [128, TT], F32)

    def w1_load(mf):
        slot = mf % 4
        sdma(w1s[:, slot * 8 * 128:(slot + 1) * 8 * 128],
             bass.AP(w1_b, mf * 128, [[FF, 128], [128 * FF, 8], [1, 128]]),
             w1l[slot])
        w1ld[mf] = (slot, w1l[slot].n)

    def w2_load(mo):
        slot = mo % 2
        ydma(w2s[:, slot * 32 * 128:(slot + 1) * 32 * 128],
             bass.AP(w2_b, mo * 128, [[H, 128], [128 * H, 32], [1, 128]]),
             w2l[slot])
        w2ld[mo] = (slot, w2l[slot].n)

    for mf in range(4):
        w1_load(mf)
    w2_load(0)
    w2_load(1)
    for tt in range(NTT):
        wv(y, stl)
        wv(y, ve)
        ydma(mu_b[:], bass.AP(stats_d, 4 * TC + tt * TT, [[0, 128], [1, TT]]),
             stl)
        ydma(r_b[:], bass.AP(stats_d, 5 * TC + tt * TT, [[0, 128], [1, TT]]),
             stl)
        wv(s, te)
        for hc in range(8):
            cs = slice(hc * TT, (hc + 1) * TT)
            wv(v, se)
            wv(v, stl)
            v.tensor_tensor(out=astage[:, 0:TT],
                            in0=accf[:, hc * TC + tt * TT:
                                      hc * TC + (tt + 1) * TT],
                            in1=mu_b[:], op=OP.subtract)
            v.tensor_tensor(out=astage[:, 0:TT], in0=astage[:, 0:TT],
                            in1=r_b[:], op=OP.mult)
            wv(s, ve)
            s.activation(out=nstage[:, cs], in_=astage[:, 0:TT],
                         func=AFT.Identity, scale=p_ln2g[:, hc:hc + 1],
                         bias=p_ln2b[:, hc:hc + 1])
        nsv = se.n
        for mf in range(32):
            pd3 = (ps_a, ps_b, ps_c, ps_dd)[mf % 4]
            slot = mf % 4
            wsl = w1s[:, slot * 8 * 128:(slot + 1) * 8 * 128]
            wv(t, w1l[w1ld[mf][0]], w1ld[mf][1])
            wv(t, se, nsv)
            bank_wait(pd3)
            for kc in range(8):
                mm = t.matmul(pd3[:, :], wsl[:, kc * 128:(kc + 1) * 128],
                              nstage[:, kc * TT:(kc + 1) * TT],
                              start=(kc == 0), stop=(kc == 7))
            te.inc(mm)
            wv(s, te, te.n)
            s.activation(out=hff[:, mf * TT:(mf + 1) * TT], in_=pd3[:, :],
                         func=AFT.Gelu, bias=p_b1[:, mf:mf + 1])
            bank_set(pd3, se)
            hff_done[mf] = se.n
            wv(s0, te, te.n)
            nxt = mf + 4
            if nxt < 32:
                w1_load(nxt)
            elif tt + 1 < NTT:
                w1_load(nxt - 32)
        for mo in range(8):
            pd4 = (ps_a, ps_b, ps_c, ps_dd)[mo % 4]
            slot = mo % 2
            wsl2 = w2s[:, slot * 32 * 128:(slot + 1) * 32 * 128]
            wv(t, w2l[w2ld[mo][0]], w2ld[mo][1])
            wv(t, se, hff_done[31])
            bank_wait(pd4)
            for kc in range(32):
                mm = t.matmul(pd4[:, :], wsl2[:, kc * 128:(kc + 1) * 128],
                              hff[:, kc * TT:(kc + 1) * TT],
                              start=(kc == 0), stop=(kc == 31))
            te.inc(mm)
            w2v = te.n
            opar = mo % 2
            osl = slice(opar * TT, (opar + 1) * TT)
            wv(y, ve, ots_done.get(mo - 2, 0))
            ydma(xts[:, osl], bass.AP(x_fm, mo * 128 * TC + tt * TT,
                                      [[TC, 128], [1, TT]]), xmls[opar])
            ydma(mts[:, osl],
                 moe_d.ap()[mo * 128:(mo + 1) * 128, tt * TT:(tt + 1) * TT],
                 xmls[opar])
            wv(v, te, w2v)
            wv(v, xmls[opar])
            wv(v, so, so.n)
            v.scalar_tensor_tensor(
                out=ots[:, osl], in0=pd4[:, :], scalar=p_b2[:, mo:mo + 1],
                in1=accf[:, mo * TC + tt * TT: mo * TC + (tt + 1) * TT],
                op0=OP.add, op1=OP.add)
            bank_set(pd4, ve)
            if "attno" in dbg_t:
                wv(y, ve)
                ydma(dbg_t["attno"].ap()[mo * 128:(mo + 1) * 128,
                                         tt * TT:(tt + 1) * TT], ots[:, osl])
                wv(v, ly)
            v.tensor_tensor(out=ots[:, osl], in0=ots[:, osl], in1=xts[:, osl],
                            op=OP.add)
            v.tensor_tensor(out=ots[:, osl], in0=ots[:, osl], in1=mts[:, osl],
                            op=OP.add)
            ots_done[mo] = ve.n
            wv(y, ve, ve.n)
            so.inc(y.dma_start(out=out_t.ap()[mo * 128:(mo + 1) * 128,
                                              tt * TT:(tt + 1) * TT],
                               in_=ots[:, osl]), 16)
            wv(y, te, w2v)
            if mo + 2 < 8:
                w2_load(mo + 2)
            elif tt + 1 < NTT:
                w2_load(mo + 2 - 8)
        ots_done.clear()
        hff_done.clear()
    y.wait_ge(so.h, so.n)
    barrier()
    p4.close()
    p4p.close()
    gst.close()
    return nc



def _prep_core(inputs, k):
    q = k % 2
    b = k // 2
    perm = np.arange(H) if q == 0 else np.r_[np.arange(512, 1024), np.arange(512)]
    asc = np.ascontiguousarray
    f32 = np.float32

    x = np.asarray(inputs["L0_attn_x"])
    xk = x[b, q * TC:(q + 1) * TC, :]
    d = {"x_fm": asc(xk.T[perm])}

    Wd = np.asarray(inputs["L0_attn_Wd"])
    d["wd_t"] = asc(Wd[np.ix_(perm, perm)].T)
    d["wbc_t"] = asc(np.asarray(inputs["L0_attn_Wbc"])[:, perm].T)
    d["w1_t"] = asc(np.asarray(inputs["L0_attn_W1"])[:, perm].T)
    d["w2_t"] = asc(np.asarray(inputs["L0_attn_W2"])[perm].T)
    d["rw_t"] = asc(np.asarray(inputs["L0_ffn_router_weight"])[:, perm].T)
    d["wg_e"] = asc(np.asarray(inputs["L0_ffn_w_gate"])[:, perm, :])
    d["wu_e"] = asc(np.asarray(inputs["L0_ffn_w_up"])[:, perm, :])
    d["wdn_e"] = asc(np.asarray(inputs["L0_ffn_w_down"])[:, :, perm])
    d["swg"] = asc(np.asarray(inputs["L0_ffn_sw_gate"])[perm])
    d["swu"] = asc(np.asarray(inputs["L0_ffn_sw_up"])[perm])
    d["swd"] = asc(np.asarray(inputs["L0_ffn_sw_down"])[:, perm])

    def col8(vv):
        return asc(np.asarray(vv)[perm].reshape(8, 128).T.astype(f32))

    d["ln1_g"] = col8(inputs["L0_attn_ln1_g"])
    d["ln1_b"] = col8(inputs["L0_attn_ln1_b"])
    d["ln2_g"] = col8(inputs["L0_attn_ln2_g"])
    d["ln2_b"] = col8(inputs["L0_attn_ln2_b"])
    d["bd_c"] = col8(inputs["L0_attn_bd"])
    d["b2_c"] = col8(inputs["L0_attn_b2"])
    d["d_c"] = col8(inputs["L0_attn_D"])
    d["b1_c"] = asc(np.asarray(inputs["L0_attn_b1"]).reshape(32, 128).T
                    .astype(f32))
    d["bbc_c"] = asc(np.asarray(inputs["L0_attn_bbc"]).reshape(32, 1)
                     .astype(f32))
    alog = np.asarray(inputs["L0_attn_A_log"])[perm[:HOH]]
    a2 = np.empty((128, 64), f32)
    for tile in range(64):
        for hl in range(8):
            for n in range(NS):
                a2[hl * NS + n, tile] = alog[tile * 8 + hl, n]
    d["alog2"] = asc(a2)
    es = np.zeros((128, 16 * 128), f32)
    for ktile in range(16):
        for p in range(128):
            es[p, ktile * 128 + 8 * ktile + p // NS] = 1.0
    d["esel_i"] = asc(es)
    cb = np.asarray(inputs["L0_ffn_correction_bias"]).astype(f32)
    d["cbt_i"] = asc(np.tile(np.tile(cb, 64), (32, 1)))

    gi = np.zeros((128, 20), np.uint32)
    for sh in range(2):
        src = 2 * b + sh
        for c in range(4):
            p = np.arange(128)
            i = c * 128 + p
            gl = q * 512 + i
            pos = gl if sh == 0 else (gl + 512) % H
            gi[:, sh * 4 + c] = src * AG1R + pos
            gi[:, 8 + sh * 4 + c] = src * H + pos
        gi[0:32, 16 + sh] = src * AG1R + H + np.arange(32)
    d["gidx"] = gi
    yi = np.zeros((128, 8), np.uint32)
    for hc in range(8):
        p = np.arange(128)
        i = hc * 128 + p
        row = np.where(i < 512,
                       q * 8 * HOH + k * HOH + i,
                       q * 8 * HOH + (k ^ 1) * HOH + (i - 512))
        yi[:, hc] = row * 4
    d["yidx"] = yi
    d = {kk: (vv if vv.dtype == np.uint32 else vv.astype(np.float32))
         for kk, vv in d.items()}
    return d, perm


LAST_RESULTS = None


def kernel(**inputs):
    global LAST_RESULTS
    in_maps = []
    perms = []
    for k in range(8):
        m, perm = _prep_core(inputs, k)
        in_maps.append(m)
        perms.append(perm)
    nc = build_program(dbg=KDBG)
    res = run_bass_kernel_spmd(nc, in_maps, core_ids=list(range(8)))
    LAST_RESULTS = res.results
    out = np.empty((4, L, H), np.float32)
    for k in range(8):
        q, b = k % 2, k // 2
        o_fm = res.results[k]["out"]
        o_nat = np.empty_like(o_fm)
        o_nat[perms[k]] = o_fm
        out[b, q * TC:(q + 1) * TC, :] = o_nat.T
    return out


if __name__ == "__main__":
    nc = build_program()
    print("built OK")


# version 10
# speedup vs baseline: 1.0450x; 1.0450x over previous
import os
from contextlib import ExitStack

import numpy as np
import ml_dtypes
import concourse.bass as bass
import concourse.mybir as mybir
from concourse.bass_utils import run_bass_kernel_spmd
from concourse.alu_op_type import AluOpType as OP

F32 = mybir.dt.float32
BF16 = mybir.dt.bfloat16
U32 = mybir.dt.uint32
AFT = mybir.ActivationFunctionType
AXX = mybir.AxisListType.X

H = 1024
NS = 16
FF = 4096
II = 512
NE = 8
TC = 2048
TT = 512
NTT = TC // TT
L = 4096
EPS = 1e-5
HOH = 512
AG1R = H + 32
TQ = 1024
BIG_NEG = -1e30
RG8 = [[0, 1, 2, 3, 4, 5, 6, 7]]

KDBG = ()
RACECHECK = bool(os.environ.get("KERNEL_RACECHECK"))
GU_SKEW = 1
DN_SKEW = 1
YMM_SKEW = 1


class Sem:
    def __init__(self, st, nc, name):
        self.h = st.enter_context(nc.semaphore(name))
        self.n = 0

    def inc(self, inst, k=1):
        inst.then_inc(self.h, k)
        self.n += k
        return inst


def _ap_tname(x):
    try:
        t = x.tensor
        return getattr(t, "name", None) or str(id(t))
    except Exception:
        return None


class EW:
    OPS = ("tensor_tensor", "tensor_scalar", "scalar_tensor_tensor",
           "tensor_copy", "tensor_reduce", "reciprocal", "memset",
           "transpose", "tensor_tensor_scan", "activation", "iota")
    RKEYS = ("in_", "in0", "in1", "data0", "data1", "initial", "scalar",
             "scalar1", "scalar2", "bias", "scale", "mask")

    def __init__(self, eng, sem, waited):
        self._e = eng
        self._s = sem
        self._w = waited
        self._tw = {}
        self._tr = {}

    def __getattr__(self, nm):
        f = getattr(self._e, nm)
        if nm in EW.OPS:
            def wrapped(*a, **k):
                if RACECHECK:
                    key = (id(self._e), id(self._s))
                    if self._w.get(key, -1) < self._s.n:
                        self._e.wait_ge(self._s.h, self._s.n)
                        self._w[key] = self._s.n
                else:
                    wt = k.get("out")
                    if wt is None and nm == "memset" and a:
                        wt = a[0]
                    rd = [k[r] for r in EW.RKEYS if r in k]
                    need = 0
                    wtn = _ap_tname(wt) if wt is not None else None
                    for r in rd:
                        t2 = _ap_tname(r)
                        if t2 is not None:
                            need = max(need, self._tw.get(t2, 0))
                    if wtn is not None:
                        need = max(need, self._tw.get(wtn, 0),
                                   self._tr.get(wtn, 0))
                    key = (id(self._e), id(self._s))
                    if need > self._w.get(key, -1):
                        self._e.wait_ge(self._s.h, need)
                        self._w[key] = need
                r2 = self._s.inc(f(*a, **k))
                if not RACECHECK:
                    if wtn is not None:
                        self._tw[wtn] = self._s.n
                    for r in rd:
                        t2 = _ap_tname(r)
                        if t2 is not None:
                            self._tr[t2] = self._s.n
                return r2
            return wrapped
        return f


def build_program(dbg=()):
    nc = bass.Bass("TRN2", target_bir_lowering=False,
                   detect_race_conditions=RACECHECK)
    t, v0, s0, y, g = nc.tensor, nc.vector, nc.scalar, nc.sync, nc.gpsimd

    def din(name, shape, dt=F32):
        return nc.dram_tensor(name, shape, dt, kind="ExternalInput")

    x_fm = din("x_fm", [H, TC])
    wd_t = din("wd_t", [H, H])
    wbc_t = din("wbc_t", [H, 32])
    w1_t = din("w1_t", [H, FF])
    w2_t = din("w2_t", [FF, H])
    rw_t = din("rw_t", [H, NE])
    wg_e = din("wg_e", [NE, H, II])
    wu_e = din("wu_e", [NE, H, II])
    wdn_e = din("wdn_e", [NE, II, H])
    swg = din("swg", [H, II])
    swu = din("swu", [H, II])
    swd = din("swd", [II, H])
    ln1_g = din("ln1_g", [128, 8])
    ln1_b = din("ln1_b", [128, 8])
    ln2_g = din("ln2_g", [128, 8])
    ln2_b = din("ln2_b", [128, 8])
    bd_c = din("bd_c", [128, 8])
    b2_c = din("b2_c", [128, 8])
    d_c = din("d_c", [128, 8])
    b1_c = din("b1_c", [128, 32])
    bbc_c = din("bbc_c", [32, 1])
    alog2 = din("alog2", [128, 64])
    esel_i = din("esel_i", [128, 16 * 128])
    cbt_i = din("cbt_i", [32, 512])
    gidx = din("gidx", [128, 20], U32)
    yidx = din("yidx", [128, 8], U32)

    out_t = nc.dram_tensor("out", [H, TC], F32, kind="ExternalOutput")

    ag1a_in = nc.dram_tensor("ag1a_in", [AG1R, TC], F32, kind="Internal")
    ag1a_out = nc.dram_tensor("ag1a_out", [8 * AG1R, TC], F32, kind="Internal")
    ag1b_in = nc.dram_tensor("ag1b_in", [H, TC], BF16, kind="Internal")
    ag1b_out = nc.dram_tensor("ag1b_out", [8 * H, TC], BF16, kind="Internal")
    ag2_in = nc.dram_tensor("ag2_in", [2 * HOH, TC], F32, kind="Internal")
    ag2_out = nc.dram_tensor("ag2_out", [8 * 2 * HOH, TC], F32, kind="Internal")
    dloc = nc.dram_tensor("dloc", [HOH, L], BF16, kind="Internal")
    dxloc = nc.dram_tensor("dxloc", [HOH, L], BF16, kind="Internal")
    accin_d = nc.dram_tensor("accin_d", [H, TC], F32, kind="Internal")
    moe_d = nc.dram_tensor("moe_d", [H, TC], F32, kind="Internal")
    bc_seq = nc.dram_tensor("bc_seq", [32, L], BF16, kind="Internal")
    stats_d = nc.dram_tensor("stats_d", [8, TC], F32, kind="Internal")
    comb_d = nc.dram_tensor("comb_d", [NE, TC], BF16, kind="Internal")
    w1_b = nc.dram_tensor("w1_b", [H, FF], BF16, kind="Internal")
    w2_b = nc.dram_tensor("w2_b", [FF, H], BF16, kind="Internal")
    wg_b = nc.dram_tensor("wg_b", [NE, H, II], BF16, kind="Internal")
    wu_b = nc.dram_tensor("wu_b", [NE, H, II], BF16, kind="Internal")
    wdn_b = nc.dram_tensor("wdn_b", [NE, II, H], BF16, kind="Internal")
    swg_b = nc.dram_tensor("swg_b", [H, II], BF16, kind="Internal")
    swu_b = nc.dram_tensor("swu_b", [H, II], BF16, kind="Internal")
    swd_b = nc.dram_tensor("swd_b", [II, H], BF16, kind="Internal")

    dbg_t = {}
    for nm in dbg:
        shp, dt = {
            "xn_bf": ([H, TC], BF16), "delta": ([H, TC], F32),
            "bcv": ([32, L], BF16),
            "yssm": ([2 * HOH, TC], F32), "accin": ([H, TC], F32),
            "logits": ([8, TC], F32), "comb": ([NE, TC], BF16),
            "moe": ([H, TC], F32), "accf": ([H, TC], F32),
            "normed": ([H, TC], BF16), "attno": ([H, TC], F32),
        }[nm]
        dbg_t[nm] = nc.dram_tensor("dbg_" + nm, shp, dt, kind="ExternalOutput")

    gst = ExitStack()

    def sb(stk, name, shape, dt):
        return stk.enter_context(nc.sbuf_tensor(name, shape, dt))

    def psum(stk, name, shape):
        return stk.enter_context(nc.psum_tensor(name, shape, F32))

    sems = {n: Sem(gst, nc, "sem_" + n)
            for n in ("ly", "ls", "lg", "te", "ve", "se", "ge", "cc", "so",
                      "xw", "stl", "agst", "bcl", "yst", "crl", "mst",
                      "a2l", "xml",
                      "dld0", "dld1", "dld2", "dld3", "mwl0", "mwl1", "wdl0", "wdl1",
                      "w1l0", "w1l1", "w1l2", "w1l3", "w2l0", "w2l1",
                      "crl0", "crl1", "a2l0", "a2l1", "ygl0", "ygl1",
                      "xml0", "xml1", "dsg0", "dsg1", "acg0", "acg1", "bcg")}
    ly, ls, lg, te, ve, se, ge, cc, so = (
        sems[n] for n in ("ly", "ls", "lg", "te", "ve", "se", "ge", "cc", "so"))
    xw, stl, agst, bcl, yst, crl, mst, a2l, xml = (
        sems[n] for n in ("xw", "stl", "agst", "bcl", "yst", "crl", "mst",
                          "a2l", "xml"))
    dld = [sems["dld0"], sems["dld1"], sems["dld2"], sems["dld3"]]
    crls = [sems["crl0"], sems["crl1"]]
    a2ls = [sems["a2l0"], sems["a2l1"]]
    ygls = [sems["ygl0"], sems["ygl1"]]
    xmls = [sems["xml0"], sems["xml1"]]
    dsg = [sems["dsg0"], sems["dsg1"]]
    acg = [sems["acg0"], sems["acg1"]]
    bcg = sems["bcg"]
    mwl = [sems["mwl0"], sems["mwl1"]]
    wdl = [sems["wdl0"], sems["wdl1"]]
    w1l = [sems["w1l0"], sems["w1l1"], sems["w1l2"], sems["w1l3"]]
    w2l = [sems["w2l0"], sems["w2l1"]]
    waited = {}
    v = EW(v0, ve, waited)
    s = EW(s0, se, waited)

    def wv(eng, sem, val=None):
        val = sem.n if val is None else val
        e0 = eng._e if isinstance(eng, EW) else eng
        key = (id(e0), id(sem))
        if waited.get(key, -1) >= val:
            return
        e0.wait_ge(sem.h, val)
        waited[key] = val

    def ydma(dst, src, sem=None):
        (sem or ly).inc(y.dma_start(out=dst, in_=src), 16)

    def sdma(dst, src, sem=None):
        (sem or ls).inc(s0.dma_start(out=dst, in_=src), 16)

    def gdma(dst, src, sem=None):
        (sem or lg).inc(g.dma_start(out=dst, in_=src), 16)

    g_tw, g_tr = {}, {}

    def gtt(out, in0, in1, op):
        key = (id(g), id(ge))
        if RACECHECK:
            if waited.get(key, -1) < ge.n:
                g.wait_ge(ge.h, ge.n)
                waited[key] = ge.n
        else:
            need = 0
            wtn = _ap_tname(out)
            for r in (in0, in1):
                t2 = _ap_tname(r)
                if t2 is not None:
                    need = max(need, g_tw.get(t2, 0))
            need = max(need, g_tw.get(wtn, 0), g_tr.get(wtn, 0))
            if need > waited.get(key, -1):
                g.wait_ge(ge.h, need)
                waited[key] = need
        r2 = ge.inc(g.tensor_tensor(out=out, in0=in0, in1=in1, op=op))
        if not RACECHECK:
            g_tw[wtn] = ge.n
            for r in (in0, in1):
                t2 = _ap_tname(r)
                if t2 is not None:
                    g_tr[t2] = ge.n
        return r2

    bank_guard = {}

    def bank_wait(*banks):
        for bk in banks:
            for sem, val in bank_guard.get(id(bk), ()):
                wv(t, sem, val)

    def bank_set(bank, sem):
        bank_guard[id(bank)] = ((sem, sem.n),)

    def barrier():
        for eng in (t, v, s, y, g):
            for sem in sems.values():
                if sem.n and sem is not so and sem is not cc:
                    wv(eng, sem)

    ones_bf = sb(gst, "ones_bf", [128, 1], BF16)
    xn_bf = sb(gst, "xn_bf", [128, 8 * TC], BF16)
    negA = sb(gst, "negA", [128, 64], F32)
    esel = sb(gst, "esel", [128, 16 * 128], BF16)
    states = sb(gst, "states", [128, 64], F32)
    p_ln1g = sb(gst, "p_ln1g", [128, 8], F32)
    p_ln1b = sb(gst, "p_ln1b", [128, 8], F32)
    p_ln2g = sb(gst, "p_ln2g", [128, 8], F32)
    p_ln2b = sb(gst, "p_ln2b", [128, 8], F32)
    p_bd = sb(gst, "p_bd", [128, 8], F32)
    p_b2 = sb(gst, "p_b2", [128, 8], F32)
    p_d = sb(gst, "p_d", [128, 8], F32)
    p_b1 = sb(gst, "p_b1", [128, 32], F32)
    p_bbc = sb(gst, "p_bbc", [32, 1], F32)
    p_cbt = sb(gst, "p_cbt", [32, 512], F32)
    ix_g = sb(gst, "ix_g", [128, 20], U32)
    ix_y = sb(gst, "ix_y", [128, 8], U32)
    alog_sb = sb(gst, "alog_sb", [128, 64], F32)

    ps_a = psum(gst, "ps_a", [128, 512])
    ps_b = psum(gst, "ps_b", [128, 512])
    ps_c = psum(gst, "ps_c", [128, 512])
    ps_dd = psum(gst, "ps_d", [128, 512])

    for dst, src in ((p_ln1g, ln1_g), (p_ln1b, ln1_b), (p_ln2g, ln2_g),
                     (p_ln2b, ln2_b), (p_bd, bd_c), (p_b2, b2_c), (p_d, d_c),
                     (p_b1, b1_c), (p_bbc, bbc_c), (p_cbt, cbt_i),
                     (ix_g, gidx), (ix_y, yidx), (alog_sb, alog2)):
        ydma(dst[:], src.ap())
    gdma(esel[:], esel_i.ap())
    v.memset(ones_bf[:], 1.0)
    v.memset(states[:], 0.0)
    wv(s, ly)
    s.activation(out=negA[:], in_=alog_sb[:], func=AFT.Exp)
    wv(v, se)
    v.tensor_scalar(out=negA[:], in0=negA[:], scalar1=-1.0, scalar2=None,
                    op0=OP.mult)

    fer = ExitStack()
    logits32 = sb(fer, "logits32", [32, TC], F32)
    fe = ExitStack()
    v.memset(logits32[:], 0.0)
    fep = ExitStack()
    ps_s = psum(fep, "ps_s", [1, 512])
    ps_q = psum(fep, "ps_q", [1, 512])
    ps_bc = psum(fep, "ps_bc", [32, 512])
    ps_rt = psum(fep, "ps_rt", [8, 512])

    mu_b = sb(fe, "mu_b", [128, TT], F32)
    r_b = sb(fe, "r_b", [128, TT], F32)
    strow = sb(fe, "strow", [1, 3 * TT], F32)
    wdT = sb(fe, "wdT", [128, 8 * H], BF16)
    wbcT = sb(fe, "wbcT", [128, 8 * 32], BF16)
    rwT = sb(fe, "rwT", [128, 8 * NE], BF16)
    for kc in range(8):
        gdma(wdT[:, kc * H:(kc + 1) * H], wd_t.ap()[kc * 128:(kc + 1) * 128, :])
        gdma(wbcT[:, kc * 32:(kc + 1) * 32], wbc_t.ap()[kc * 128:(kc + 1) * 128, :])
        gdma(rwT[:, kc * NE:(kc + 1) * NE], rw_t.ap()[kc * 128:(kc + 1) * 128, :])
    for db, df in ((wg_b, wg_e), (wu_b, wu_e), (wdn_b, wdn_e), (swg_b, swg),
                   (swu_b, swu), (swd_b, swd), (w1_b, w1_t), (w2_b, w2_t)):
        gdma(db.ap(), df.ap())
    lg_precast = lg.n

    xt = sb(fe, "xt", [128, 8 * TT], F32)
    xb_bf = sb(fe, "xb_bf", [128, 8 * TT], BF16)
    xq_bf = sb(fe, "xq_bf", [128, 8 * TT], BF16)
    xn_f = sb(fe, "xn_f", [128, 8 * TT], F32)
    xn2_f = sb(fe, "xn2_f", [128, 8 * TT], F32)
    xn2b = sb(fe, "xn2b", [128, 8 * TT], BF16)
    dstage = sb(fe, "dstage", [128, 2 * TT], F32)
    dxstage = sb(fe, "dxstage", [128, 2 * TT], BF16)
    bcst = sb(fe, "bcst", [32, TT], F32)
    acst = sb(fe, "acst", [128, 2 * TT], F32)

    def ln_stats(tt, src_bf, sq_bf, row, pss, psq, strow, mu_b, r_b):
        wv(t, ve)
        wv(t, se)
        wv(v, stl)
        bank_wait(pss, psq)
        for hc in range(8):
            t.matmul(pss[:, :], ones_bf[:], src_bf[:, hc * TT:(hc + 1) * TT],
                     start=(hc == 0), stop=(hc == 7))
        for hc in range(8):
            mm = t.matmul(psq[:, :], ones_bf[:], sq_bf[:, hc * TT:(hc + 1) * TT],
                          start=(hc == 0), stop=(hc == 7))
        te.inc(mm)
        wv(v, te)
        v.tensor_scalar(out=strow[0:1, 0:TT], in0=pss[0:1, :],
                        scalar1=1.0 / H, scalar2=None, op0=OP.mult)
        v.tensor_scalar(out=strow[0:1, TT:2 * TT], in0=psq[0:1, :],
                        scalar1=1.0 / H, scalar2=None, op0=OP.mult)
        bank_set(pss, ve)
        bank_set(psq, ve)
        v.tensor_tensor(out=strow[0:1, 2 * TT:3 * TT], in0=strow[0:1, 0:TT],
                        in1=strow[0:1, 0:TT], op=OP.mult)
        v.tensor_tensor(out=strow[0:1, 2 * TT:3 * TT],
                        in0=strow[0:1, 2 * TT:3 * TT],
                        in1=strow[0:1, TT:2 * TT], op=OP.subtract)
        v.tensor_scalar(out=strow[0:1, 2 * TT:3 * TT],
                        in0=strow[0:1, 2 * TT:3 * TT], scalar1=-1.0,
                        scalar2=EPS, op0=OP.mult, op1=OP.add)
        wv(s, ve)
        s.activation(out=strow[0:1, TT:2 * TT], in_=strow[0:1, 2 * TT:3 * TT],
                     func=AFT.Ln)
        s.activation(out=strow[0:1, TT:2 * TT], in_=strow[0:1, TT:2 * TT],
                     func=AFT.Exp, scale=-0.5)
        wv(y, se)
        wv(y, ve)
        ydma(stats_d.ap()[row:row + 1, tt * TT:(tt + 1) * TT], strow[0:1, 0:TT],
             stl)
        ydma(stats_d.ap()[row + 1:row + 2, tt * TT:(tt + 1) * TT],
             strow[0:1, TT:2 * TT], stl)
        wv(y, stl)
        ydma(mu_b[:], bass.AP(stats_d, row * TC + tt * TT, [[0, 128], [1, TT]]),
             stl)
        ydma(r_b[:], bass.AP(stats_d, (row + 1) * TC + tt * TT,
                             [[0, 128], [1, TT]]), stl)

    for tt in range(NTT):
        wv(y, ve)
        wv(y, se)
        ydma(xt[:], bass.AP(x_fm, tt * TT, [[TC, 128], [128 * TC, 8], [1, TT]]),
             xw)
        wv(s, xw)
        s.activation(out=xq_bf[:], in_=xt[:], func=AFT.Square)
        wv(v, xw)
        v.tensor_copy(out=xb_bf[:], in_=xt[:])
        ln_stats(tt, xb_bf, xq_bf, 0, ps_s, ps_q, strow, mu_b, r_b)
        wv(v, stl)
        v.tensor_tensor(out=xn_f[:], in0=xt[:],
                        in1=bass.AP(mu_b, 0, [[512, 128], [0, 8], [1, 512]]),
                        op=OP.subtract)
        v.tensor_tensor(out=xn_f[:], in0=xn_f[:],
                        in1=bass.AP(r_b, 0, [[512, 128], [0, 8], [1, 512]]),
                        op=OP.mult)
        v.tensor_copy(
            out=bass.AP(xn_bf, tt * TT, [[8 * TC, 128], [TC, 8], [1, TT]]),
            in_=xn_f[:])
        wv(s, ve)
        s.activation(out=xq_bf[:], in_=xn_f[:], func=AFT.Square)
        s.activation(out=xb_bf[:], in_=xn_f[:], func=AFT.Identity)
        ln_stats(tt, xb_bf, xq_bf, 2, ps_s, ps_q, strow, mu_b, r_b)
        wv(v, stl)
        v.tensor_tensor(out=xn2_f[:], in0=xn_f[:],
                        in1=bass.AP(mu_b, 0, [[512, 128], [0, 8], [1, 512]]),
                        op=OP.subtract)
        v.tensor_tensor(out=xn2_f[:], in0=xn2_f[:],
                        in1=bass.AP(r_b, 0, [[512, 128], [0, 8], [1, 512]]),
                        op=OP.mult)
        wv(s, ve)
        wv(s, te)
        for hc in range(8):
            cs = slice(hc * TT, (hc + 1) * TT)
            s.activation(out=xn2b[:, cs], in_=xn2_f[:, cs], func=AFT.Identity,
                         scale=p_ln1g[:, hc:hc + 1], bias=p_ln1b[:, hc:hc + 1])
        wv(v, se)
        v.tensor_copy(out=xn2_f[:], in_=xn2b[:])
        for hc in range(8):
            cs = slice(hc * TT, (hc + 1) * TT)
            stg = acst[:, (hc % 2) * TT:(hc % 2 + 1) * TT]
            wv(v, acg[hc % 2])
            v.scalar_tensor_tensor(out=stg, in0=xn2_f[:, cs],
                                   scalar=p_d[:, hc:hc + 1], in1=xn_f[:, cs],
                                   op0=OP.mult, op1=OP.add)
            wv(y, ve)
            ydma(accin_d.ap()[hc * 128:(hc + 1) * 128, tt * TT:(tt + 1) * TT],
                 stg, acg[hc % 2])
        wv(t, lg)
        wv(t, se)
        for mo in range(8):
            pd = ps_a if (mo % 2 == 0) else ps_b
            dstg = dstage[:, (mo % 2) * TT:(mo % 2 + 1) * TT]
            dxstg = dxstage[:, (mo % 2) * TT:(mo % 2 + 1) * TT]
            bank_wait(pd)
            for kc in range(8):
                mm = t.matmul(pd[:, :],
                              wdT[:, kc * H + mo * 128: kc * H + (mo + 1) * 128],
                              xn2b[:, kc * TT:(kc + 1) * TT],
                              start=(kc == 0), stop=(kc == 7))
            te.inc(mm)
            wv(s, te)
            wv(s, dsg[mo % 2])
            s.activation(out=dstg, in_=pd[:, :], func=AFT.Abs,
                         bias=p_bd[:, mo:mo + 1])
            s.activation(out=dstg, in_=dstg, func=AFT.Exp, scale=-1.0)
            s.activation(out=dstg, in_=dstg, func=AFT.Ln, bias=1.0)
            s.activation(out=dxstg, in_=pd[:, :], func=AFT.Relu,
                         bias=p_bd[:, mo:mo + 1])
            bank_set(pd, se)
            wv(v, se)
            wv(v, dsg[mo % 2])
            v.tensor_tensor(out=dstg, in0=dstg, in1=dxstg, op=OP.add)
            v.tensor_tensor(out=dxstg, in0=dstg,
                            in1=xn2_f[:, mo * TT:(mo + 1) * TT], op=OP.mult)
            wv(y, ve)
            ydma(ag1a_in.ap()[mo * 128:(mo + 1) * 128, tt * TT:(tt + 1) * TT],
                 dstg, dsg[mo % 2])
            ydma(ag1b_in.ap()[mo * 128:(mo + 1) * 128, tt * TT:(tt + 1) * TT],
                 dxstg, dsg[mo % 2])
        bank_wait(ps_bc)
        for kc in range(8):
            mm = t.matmul(ps_bc[:, :], wbcT[:, kc * 32:(kc + 1) * 32],
                          xn2b[:, kc * TT:(kc + 1) * TT],
                          start=(kc == 0), stop=(kc == 7))
        te.inc(mm)
        wv(v, te)
        wv(v, bcg)
        v.tensor_scalar(out=bcst[:], in0=ps_bc[:, :], scalar1=p_bbc[:, 0:1],
                        scalar2=None, op0=OP.add)
        bank_set(ps_bc, ve)
        wv(y, ve)
        ydma(ag1a_in.ap()[H:H + 32, tt * TT:(tt + 1) * TT], bcst[:], bcg)
        bank_wait(ps_rt)
        for kc in range(8):
            mm = t.matmul(ps_rt[:, :], rwT[:, kc * NE:(kc + 1) * NE],
                          xn_bf[:, kc * TC + tt * TT: kc * TC + (tt + 1) * TT],
                          start=(kc == 0), stop=(kc == 7))
        te.inc(mm)
        wv(v, te)
        v.tensor_copy(out=logits32[0:8, tt * TT:(tt + 1) * TT], in_=ps_rt[:, :])
        bank_set(ps_rt, ve)

    if "xn_bf" in dbg_t:
        wv(y, ve)
        for hc in range(8):
            ydma(dbg_t["xn_bf"].ap()[hc * 128:(hc + 1) * 128, :],
                 xn_bf[:, hc * TC:(hc + 1) * TC])
    if "logits" in dbg_t:
        wv(y, ve)
        ydma(dbg_t["logits"].ap()[:, :], logits32[0:8, :])
    if "accin" in dbg_t:
        wv(g, ly)
        wv(g, lg)
        gdma(dbg_t["accin"].ap()[:, :], accin_d.ap()[:, :])
    if "delta" in dbg_t:
        wv(g, ly)
        wv(g, lg)
        gdma(dbg_t["delta"].ap()[:, :], ag1a_in.ap()[0:H, :])

    for sem_ in (dsg[0], dsg[1], acg[0], acg[1], bcg):
        wv(g, sem_)
    cc.inc(g.collective_compute("AllGather", OP.bypass, replica_groups=RG8,
                                ins=[ag1a_in.ap()], outs=[ag1a_out.ap()]))
    cc.inc(g.collective_compute("AllGather", OP.bypass, replica_groups=RG8,
                                ins=[ag1b_in.ap()], outs=[ag1b_out.ap()]))
    barrier()
    fe.close()
    fep.close()

    ro = ExitStack()
    R32 = sb(ro, "R32", [32, 64 * 32], F32)
    sco = sb(ro, "sco", [32, 512], F32)
    msc = sb(ro, "msc", [32, 512], F32)
    gtmp = sb(ro, "gtmp", [32, 128], F32)
    gtmp2 = sb(ro, "gtmp2", [32, 128], F32)
    ge0 = sb(ro, "ge0", [32, 64], F32)
    ge1 = sb(ro, "ge1", [32, 64], F32)
    m1c = sb(ro, "m1c", [32, 64], F32)
    m1r = sb(ro, "m1r", [32, 512], F32)
    f1 = sb(ro, "f1", [32, 512], F32)
    f2 = sb(ro, "f2", [32, 512], F32)
    cum = sb(ro, "cum", [32, 512], F32)
    cum2 = sb(ro, "cum2", [32, 512], F32)
    wsc = sb(ro, "wsc", [32, 64], F32)
    w1c = sb(ro, "w1c", [32, 64], F32)
    w2c = sb(ro, "w2c", [32, 64], F32)
    prodw = sb(ro, "prodw", [32, 512], F32)
    padbuf = sb(ro, "padbuf", [32, 64 * 32], F32)
    combT = sb(ro, "combT", [32, TC], F32)

    def rap(tsr, fsz, off, dims, np_=32):
        return bass.AP(tsr, off, [[fsz, np_]] + dims)

    for j in range(64):
        v.transpose(out=R32[0:32, j * 32:(j + 1) * 32],
                    in_=logits32[0:32, j * 32:(j + 1) * 32])
    wv(s, ve)
    s.activation(out=R32[:], in_=R32[:], func=AFT.Tanh, scale=0.5)
    wv(v, se)
    v.tensor_scalar(out=rap(sco, 512, 0, [[8, 64], [1, 8]]),
                    in0=rap(R32, 64 * 32, 0, [[32, 64], [1, 8]]),
                    scalar1=0.5, scalar2=0.5, op0=OP.mult, op1=OP.add)
    wv(v, ly)
    v.tensor_tensor(out=msc[:], in0=sco[:], in1=p_cbt[:], op=OP.add)
    for idx, (i, j) in enumerate([(0, 1), (0, 2), (0, 3), (1, 2), (1, 3),
                                  (2, 3)]):
        dst = gtmp if idx == 0 else gtmp2
        v.tensor_tensor(out=dst[:], in0=rap(msc, 512, i, [[4, 128]]),
                        in1=rap(msc, 512, j, [[4, 128]]), op=OP.add)
        if idx > 0:
            v.tensor_tensor(out=gtmp[:], in0=gtmp[:], in1=gtmp2[:], op=OP.max)
    v.tensor_tensor(out=ge0[:], in0=rap(gtmp, 128, 0, [[2, 64]]),
                    in1=rap(gtmp, 128, 1, [[2, 64]]), op=OP.is_ge)
    v.tensor_tensor(out=ge1[:], in0=rap(gtmp, 128, 1, [[2, 64]]),
                    in1=rap(gtmp, 128, 0, [[2, 64]]), op=OP.is_gt)
    for e in range(4):
        v.tensor_copy(out=rap(m1r, 512, e, [[8, 64]]), in_=ge0[:])
        v.tensor_copy(out=rap(m1r, 512, 4 + e, [[8, 64]]), in_=ge1[:])
    v.tensor_tensor(out=msc[:], in0=msc[:], in1=m1r[:], op=OP.mult)

    def top_select(fdst):
        v.tensor_reduce(out=m1c[:], in_=rap(msc, 512, 0, [[8, 64], [1, 8]]),
                        axis=AXX, op=OP.max)
        for e in range(8):
            v.tensor_copy(out=rap(m1r, 512, e, [[8, 64]]), in_=m1c[:])
        v.tensor_tensor(out=fdst[:], in0=msc[:], in1=m1r[:], op=OP.is_ge)
        v.tensor_copy(out=cum[:], in_=fdst[:])
        src, dst = cum, cum2
        for shd in (1, 2, 4):
            v.tensor_copy(out=dst[:], in_=src[:])
            v.tensor_tensor(
                out=rap(dst, 512, shd, [[8, 64], [1, 8 - shd]]),
                in0=rap(src, 512, shd, [[8, 64], [1, 8 - shd]]),
                in1=rap(src, 512, 0, [[8, 64], [1, 8 - shd]]), op=OP.add)
            src, dst = dst, src
        v.tensor_scalar(out=dst[:], in0=src[:], scalar1=1.0, scalar2=None,
                        op0=OP.is_le)
        v.tensor_tensor(out=fdst[:], in0=fdst[:], in1=dst[:], op=OP.mult)

    top_select(f1)
    v.tensor_tensor(out=prodw[:], in0=f1[:], in1=sco[:], op=OP.mult)
    v.tensor_reduce(out=w1c[:], in_=rap(prodw, 512, 0, [[8, 64], [1, 8]]),
                    axis=AXX, op=OP.add)
    v.scalar_tensor_tensor(out=msc[:], in0=f1[:], scalar=BIG_NEG, in1=msc[:],
                           op0=OP.mult, op1=OP.add)
    top_select(f2)
    v.tensor_tensor(out=prodw[:], in0=f2[:], in1=sco[:], op=OP.mult)
    v.tensor_reduce(out=w2c[:], in_=rap(prodw, 512, 0, [[8, 64], [1, 8]]),
                    axis=AXX, op=OP.add)
    v.tensor_tensor(out=wsc[:], in0=w1c[:], in1=w2c[:], op=OP.add)
    v.tensor_scalar(out=wsc[:], in0=wsc[:], scalar1=2.0, scalar2=2e-20,
                    op0=OP.mult, op1=OP.add)
    v.reciprocal(out=wsc[:], in_=wsc[:])
    for e in range(8):
        v.tensor_copy(out=rap(m1r, 512, e, [[8, 64]]), in_=w1c[:])
    v.tensor_tensor(out=f1[:], in0=f1[:], in1=m1r[:], op=OP.mult)
    for e in range(8):
        v.tensor_copy(out=rap(m1r, 512, e, [[8, 64]]), in_=w2c[:])
    v.tensor_tensor(out=f2[:], in0=f2[:], in1=m1r[:], op=OP.mult)
    v.tensor_tensor(out=f1[:], in0=f1[:], in1=f2[:], op=OP.add)
    for e in range(8):
        v.tensor_copy(out=rap(m1r, 512, e, [[8, 64]]), in_=wsc[:])
    v.tensor_tensor(out=f1[:], in0=f1[:], in1=m1r[:], op=OP.mult)
    v.memset(padbuf[:], 0.0)
    v.tensor_copy(out=rap(padbuf, 64 * 32, 0, [[32, 64], [1, 8]]),
                  in_=rap(f1, 512, 0, [[8, 64], [1, 8]]))
    for j in range(64):
        v.transpose(out=combT[0:32, j * 32:(j + 1) * 32],
                    in_=padbuf[0:32, j * 32:(j + 1) * 32])
    wv(g, ve)
    gdma(comb_d.ap()[:, :], combT[0:8, :])
    lg_comb = lg.n
    if "comb" in dbg_t:
        wv(g, lg)
        gdma(dbg_t["comb"].ap()[:, :], comb_d.ap()[:, :])
    barrier()
    ro.close()
    fer.close()

    sm = ExitStack()
    pro = ExitStack()
    gstg = sb(pro, "gstg", [128, TC], F32)
    gstgb = sb(pro, "gstgb", [128, TC], BF16)
    bcs = sb(pro, "bcs", [32, 2 * TC], F32)
    wv(g, cc, 2)
    for sh in range(2):
        lg.inc(g.indirect_dma_start(
            out=bcs[0:32, sh * TC:(sh + 1) * TC], out_offset=None,
            in_=bass.AP(ag1a_out, 0, [[TC, 8 * AG1R], [1, TC]]),
            in_offset=bass.IndirectOffsetOnAxis(
                ix_g[0:32, 16 + sh:17 + sh], 0)), 16)
    wv(g, lg)
    for sh in range(2):
        gdma(bc_seq.ap()[:, sh * TC:(sh + 1) * TC],
             bcs[0:32, sh * TC:(sh + 1) * TC])
    for sh in range(2):
        for c in range(4):
            wv(g, lg)
            lg.inc(g.indirect_dma_start(
                out=gstg[:, :], out_offset=None,
                in_=bass.AP(ag1a_out, 0, [[TC, 8 * AG1R], [1, TC]]),
                in_offset=bass.IndirectOffsetOnAxis(
                    ix_g[:, sh * 4 + c: sh * 4 + c + 1], 0)), 16)
            wv(g, lg)
            gdma(dloc.ap()[c * 128:(c + 1) * 128, sh * TC:(sh + 1) * TC],
                 gstg[:, :])
    for sh in range(2):
        for c in range(4):
            wv(g, lg)
            lg.inc(g.indirect_dma_start(
                out=gstgb[:, :], out_offset=None,
                in_=bass.AP(ag1b_out, 0, [[TC, 8 * H], [1, TC]]),
                in_offset=bass.IndirectOffsetOnAxis(
                    ix_g[:, 8 + sh * 4 + c: 8 + sh * 4 + c + 1], 0)), 16)
            wv(g, lg)
            gdma(dxloc.ap()[c * 128:(c + 1) * 128, sh * TC:(sh + 1) * TC],
                 gstgb[:, :])
    if "bcv" in dbg_t:
        wv(g, lg)
        gdma(dbg_t["bcv"].ap()[:, :], bc_seq.ap()[:, :])
    pro.close()
    drep = sb(sm, "drep", [128, 4 * TQ], BF16)
    dxrep = sb(sm, "dxrep", [128, 4 * TQ], BF16)
    brepq = sb(sm, "brepq", [128, 2 * TQ], BF16)
    crepq = sb(sm, "crepq", [128, 2 * TQ], BF16)
    dAt = sb(sm, "dAt", [128, 4 * TQ], F32)
    dBxt = sb(sm, "dBxt", [128, 4 * TQ], F32)
    Hsb = sb(sm, "Hsb", [128, TQ], BF16)
    prodb = sb(sm, "prodb", [128, 2 * TQ], BF16)
    ystg = sb(sm, "ystg", [128, TQ], F32)
    ps_y = [psum(sm, f"ps_y{i}", [128, 512]) for i in range(4)]

    wgs = sb(sm, "wgs", [128, 2 * 8 * II], BF16)
    wus = sb(sm, "wus", [128, 2 * 8 * II], BF16)
    wdmo = sb(sm, "wdmo", [128, 2 * 4 * H], BF16)
    wsdm = sb(sm, "wsdm", [128, 2 * 4 * 128], BF16)
    h_all = sb(sm, "h_all", [128, NE * 4 * TT], BF16)
    h_sh = sb(sm, "h_sh", [128, 4 * TT], BF16)
    combr = sb(sm, "combr", [128, 2 * TT], BF16)
    g_bf = sb(sm, "g_bf", [128, 2 * TT], BF16)
    u_bf = sb(sm, "u_bf", [128, 2 * TT], BF16)
    tnh = sb(sm, "tnh", [128, 2 * TT], BF16)
    h1t = sb(sm, "h1t", [128, 2 * TT], BF16)
    mstage = sb(sm, "mstage", [128, 2 * TT], F32)

    dA_done, dbx_done, scan_done, prod_done = {}, {}, {}, {}
    drep_ld, dxrep_ld, ymm_done, evac_done, yq_dma = {}, {}, {}, {}, {}
    brep_ld = {}
    ypend = []
    ypend_flush = {}

    def emit_ymm(item):
        u, par, kt = item
        cidx2 = u // 16
        pyA2, pyB2, chunk2, sh2, twp2 = ypend_flush[cidx2]
        wv(t, ve, prod_done[u])
        if kt == 0:
            bank_wait(pyA2, pyB2)
        t.matmul(pyA2[:, :], esel[:, kt * 128:(kt + 1) * 128],
                 bass.AP(prodb, par * TQ, [[2 * TQ, 128], [1, 512]]),
                 start=(kt == 0), stop=(kt == 15))
        mm = t.matmul(pyB2[:, :], esel[:, kt * 128:(kt + 1) * 128],
                      bass.AP(prodb, par * TQ + 512,
                              [[2 * TQ, 128], [1, 512]]),
                      start=(kt == 0), stop=(kt == 15))
        te.inc(mm)
        ymm_done[u] = te.n
        if kt == 15:
            del ypend_flush[cidx2]
            wv(s, te, ymm_done[u])
            wv(s, yst)
            s.activation(out=ystg[:, 0:512], in_=pyA2[:, :], func=AFT.Identity)
            s.activation(out=ystg[:, 512:TQ], in_=pyB2[:, :], func=AFT.Identity)
            bank_set(pyA2, se)
            bank_set(pyB2, se)
            evac_done[cidx2] = se.n
            wv(y, se, evac_done[cidx2])
            wv(y, yst)
            ydma(bass.AP(ag2_in, (sh2 * HOH + chunk2 * 128) * TC + twp2,
                         [[TC, 128], [1, TQ]]),
                 ystg[:, 0:TQ], yst)

    def emit_ssm():
        lg_pro = lg.n
        yield
        def unit_loads(u2):
            if u2 >= 256:
                return
            tq2, tile2 = u2 // 64, u2 % 64
            par2 = u2 % 4
            dsl2 = slice(par2 * TQ, (par2 + 1) * TQ)
            wv(y, se, dA_done.get(u2 - 4, 0))
            wv(y, ge, dbx_done.get(u2 - 4, 0))

            ydma(drep[:, dsl2],
                 bass.AP(dloc, (8 * tile2) * L + tq2 * TQ,
                         [[L, 8], [0, 16], [1, TQ]]), dld[par2])
            ydma(dxrep[:, dsl2],
                 bass.AP(dxloc, (8 * tile2) * L + tq2 * TQ,
                         [[L, 8], [0, 16], [1, TQ]]), dld[par2])

        wv(y, lg, lg_pro)
        for uu in range(3):
            unit_loads(uu)
        for tq in range(4):
            sh, twp = tq // 2, (tq % 2) * TQ
            qpar = tq % 2
            qsl = slice(qpar * TQ, (qpar + 1) * TQ)
            wv(y, ve, scan_done.get(("q", tq - 2), 0))
            wv(y, bcl)
            ydma(brepq[:, qsl], bass.AP(bc_seq, tq * TQ,
                                        [[0, 8], [L, 16], [1, TQ]]), bcl)
            ydma(crepq[:, qsl], bass.AP(bc_seq, 16 * L + tq * TQ,
                                        [[0, 8], [L, 16], [1, TQ]]), bcl)
            for chunk in range(4):
                pyA = ps_y[(chunk % 2) * 2]
                pyB = ps_y[(chunk % 2) * 2 + 1]
                ypend_flush[tq * 4 + chunk] = (pyA, pyB, chunk, sh, twp)
                for kt in range(16):
                    tile = chunk * 16 + kt
                    u = tq * 64 + tile
                    lpar = u % 4
                    par = u % 4
                    lsl = slice(lpar * TQ, (lpar + 1) * TQ)
                    dsl = slice(par * TQ, (par + 1) * TQ)
                    unit_loads(u + 3)
                    wv(s, dld[lpar])
                    wv(s, ve, scan_done.get(u - 4, 0))
                    s.activation(out=dAt[:, dsl], in_=drep[:, lsl],
                                 func=AFT.Exp, scale=negA[:, tile:tile + 1])
                    dA_done[u] = se.n
                    wv(g, dld[lpar])
                    wv(g, bcl)
                    wv(g, ve, scan_done.get(u - 4, 0))
                    gtt(dBxt[:, dsl], dxrep[:, lsl], brepq[:, qsl], OP.mult)
                    dbx_done[u] = ge.n
                    wv(v, se, dA_done[u])
                    wv(v, ge, dbx_done[u])
                    v.tensor_tensor_scan(out=Hsb[:, :], data0=dAt[:, dsl],
                                         data1=dBxt[:, dsl],
                                         initial=states[:, tile:tile + 1],
                                         op0=OP.mult, op1=OP.add)
                    v.tensor_copy(out=states[:, tile:tile + 1],
                                  in_=Hsb[:, TQ - 1:TQ])
                    ppar = u % 2
                    psl = slice(ppar * TQ, (ppar + 1) * TQ)
                    wv(v, te, ymm_done.get(u - 2, 0))
                    wv(v, bcl)
                    v.tensor_tensor(out=prodb[:, psl], in0=Hsb[:, :],
                                    in1=crepq[:, qsl], op=OP.mult)
                    scan_done[u] = ve.n
                    if tile == 63:
                        scan_done[("q", tq)] = ve.n
                    prod_done[u] = ve.n
                    ypend.append((u, u % 2, kt))
                    if len(ypend) > YMM_SKEW:
                        emit_ymm(ypend.pop(0))
                yield
            if tq == 1:
                while ypend:
                    emit_ymm(ypend.pop(0))
                wv(g, yst)
                cc.inc(g.collective_compute(
                    "AllGather", OP.bypass, replica_groups=RG8,
                    ins=[ag2_in.ap()[0:HOH, :]],
                    outs=[ag2_out.ap()[0:8 * HOH, :]]))
        while ypend:
            emit_ymm(ypend.pop(0))
        wv(g, yst)
        cc.inc(g.collective_compute(
            "AllGather", OP.bypass, replica_groups=RG8,
            ins=[ag2_in.ap()[HOH:2 * HOH, :]],
            outs=[ag2_out.ap()[8 * HOH:16 * HOH, :]]))
        if "yssm" in dbg_t:
            wv(g, cc)
            wv(g, ly)
            gdma(dbg_t["yssm"].ap()[:, :], ag2_in.ap()[:, :])
        yield

    gu_evac, h_ready, mo_evac, mo_dma, dn_done, hcons = {}, {}, {}, {}, {}, {}
    ld_w, hr_last = {}, {}
    gu_u = [0]
    gupend, dnpend = [], []
    combr_guard = {}

    def moe_w_load(tt, e):
        ex = tt * 9 + (e + 1)
        slot = ex % 2
        ydma(wgs[:, slot * 8 * II:(slot + 1) * 8 * II],
             bass.AP(wg_b if e >= 0 else swg_b, max(e, 0) * H * II,
                     [[II, 128], [128 * II, 8], [1, II]]), mwl[slot])
        ydma(wus[:, slot * 8 * II:(slot + 1) * 8 * II],
             bass.AP(wu_b if e >= 0 else swu_b, max(e, 0) * H * II,
                     [[II, 128], [128 * II, 8], [1, II]]), mwl[slot])
        ld_w[(tt, e)] = (slot, mwl[slot].n)

    def hr_last_for(tt2, e2):
        if e2 >= -1 and (tt2, e2) in hr_last:
            return hr_last[(tt2, e2)]
        if e2 < -1 and (tt2 - 1, e2 + 9) in hr_last:
            return hr_last[(tt2 - 1, e2 + 9)]
        return 0

    def flush_gu(item):
        tt, e, kc4, pg, pu, ug, mmv, crv, cpar = item
        parq = ug % 2
        csl = slice(parq * TT, (parq + 1) * TT)
        wv(s, te, mmv)
        wv(s, ve, h_ready.get(ug - 2, 0))
        s.activation(out=tnh[:, csl], in_=pg[:, :], func=AFT.Tanh, scale=0.5)
        s.activation(out=g_bf[:, csl], in_=pg[:, :], func=AFT.Identity)
        s.activation(out=u_bf[:, csl], in_=pu[:, :], func=AFT.Identity)
        bank_set(pg, se)
        bank_set(pu, se)
        gu_evac[((tt, e), kc4)] = se.n
        wv(v, se, se.n)
        v.scalar_tensor_tensor(out=h1t[:, csl], in0=tnh[:, csl],
                               scalar=1.0, in1=u_bf[:, csl],
                               op0=OP.add, op1=OP.mult)
        if e < 0:
            wv(v, te, dn_done.get((tt - 1, 7), 0))
            v.scalar_tensor_tensor(
                out=h_sh[:, kc4 * TT:(kc4 + 1) * TT],
                in0=h1t[:, csl], scalar=0.5, in1=g_bf[:, csl],
                op0=OP.mult, op1=OP.mult)
        else:
            wv(v, crls[cpar], crv)
            wv(v, te, dn_done.get((tt - 1, 7), 0))
            v.tensor_tensor(out=h1t[:, csl], in0=h1t[:, csl],
                            in1=g_bf[:, csl], op=OP.mult)
            v.tensor_tensor(
                out=h_all[:, (e * 4 + kc4) * TT:(e * 4 + kc4 + 1) * TT],
                in0=h1t[:, csl],
                in1=combr[:, cpar * TT:(cpar + 1) * TT], op=OP.mult)
            combr_guard[cpar] = ve.n
        h_ready[ug] = ve.n
        hr_last[(tt, e)] = ve.n

    def flush_dn(item):
        tt, mo, pd2 = item
        wv(s, te, dn_done[(tt, mo)])
        wv(s, mst)
        s.activation(out=mstage[:, (mo % 2) * TT:(mo % 2 + 1) * TT],
                     in_=pd2[:, :], func=AFT.Identity)
        bank_set(pd2, se)
        mo_evac[(tt, mo)] = se.n
        wv(y, se, se.n)
        wv(y, mst)
        ydma(moe_d.ap()[mo * 128:(mo + 1) * 128, tt * TT:(tt + 1) * TT],
             mstage[:, (mo % 2) * TT:(mo % 2 + 1) * TT], mst)

    def emit_moe():
        wv(y, lg, lg_precast)
        moe_w_load(0, -1)
        moe_w_load(0, 0)
        for tt in range(NTT):
            for e in range(-1, NE):
                slot = (tt * 9 + e + 1) % 2
                wsl_g = wgs[:, slot * 8 * II:(slot + 1) * 8 * II]
                wsl_u = wus[:, slot * 8 * II:(slot + 1) * 8 * II]
                cpar = (e + 1) % 2
                if e >= 0:
                    wv(y, ve, combr_guard.get(cpar, 0))
                    wv(y, lg, lg_comb)
                    ydma(combr[:, cpar * TT:(cpar + 1) * TT],
                         bass.AP(comb_d, e * TC + tt * TT, [[0, 128], [1, TT]]),
                         crls[cpar])
                    crv = crls[cpar].n
                else:
                    crv = 0
                for kc4 in range(4):
                    pg, pu = (ps_a, ps_b) if kc4 % 2 == 0 else (ps_c, ps_dd)
                    wv(t, mwl[ld_w[(tt, e)][0]], ld_w[(tt, e)][1])
                    bank_wait(pg, pu)
                    for kc in range(8):
                        t.matmul(pg[:, :],
                                 wsl_g[:, kc * II + kc4 * 128:
                                       kc * II + (kc4 + 1) * 128],
                                 xn_bf[:, kc * TC + tt * TT:
                                       kc * TC + (tt + 1) * TT],
                                 start=(kc == 0), stop=(kc == 7))
                    for kc in range(8):
                        mm = t.matmul(pu[:, :],
                                      wsl_u[:, kc * II + kc4 * 128:
                                            kc * II + (kc4 + 1) * 128],
                                      xn_bf[:, kc * TC + tt * TT:
                                            kc * TC + (tt + 1) * TT],
                                      start=(kc == 0), stop=(kc == 7))
                    te.inc(mm)
                    mmv = te.n
                    gu_u[0] += 1
                    ug = gu_u[0]
                    gupend.append((tt, e, kc4, pg, pu, ug, mmv, crv, cpar))
                    if len(gupend) > GU_SKEW:
                        flush_gu(gupend.pop(0))
                    mm_last = te.n
                    yield
                nxt_e, nxt_tt = e + 2, tt
                if nxt_e >= NE:
                    nxt_e -= (NE + 1)
                    nxt_tt += 1
                if nxt_tt < NTT:
                    wv(y, te, mm_last)
                    moe_w_load(nxt_tt, nxt_e)

            while gupend:
                flush_gu(gupend.pop(0))
            for mo in range(8):
                pd2 = ps_a if mo % 2 == 0 else ps_b
                slot = mo % 2
                wv(y, te, dn_done.get((tt, mo - 2), 0))
                ydma(wdmo[:, slot * 4 * H:(slot + 1) * 4 * H],
                     bass.AP(wdn_b, mo * 128,
                             [[H, 128], [II * H, 8], [128 * H, 4], [1, 128]]),
                     wdl[slot])
                ydma(wsdm[:, slot * 4 * 128:(slot + 1) * 4 * 128],
                     bass.AP(swd_b, mo * 128,
                             [[H, 128], [128 * H, 4], [1, 128]]), wdl[slot])
                wv(t, ve, hr_last[(tt, NE - 1)])
                wv(t, ve, hr_last[(tt, -1)])
                wv(t, wdl[slot])
                bank_wait(pd2)
                for e in range(NE):
                    for kc in range(4):
                        t.matmul(pd2[:, :],
                                 wdmo[:, slot * 4 * H + (e * 4 + kc) * 128:
                                       slot * 4 * H + (e * 4 + kc + 1) * 128],
                                 h_all[:, (e * 4 + kc) * TT:
                                       (e * 4 + kc + 1) * TT],
                                 start=(e == 0 and kc == 0), stop=False)
                for kc in range(4):
                    mm = t.matmul(pd2[:, :],
                                  wsdm[:, slot * 4 * 128 + kc * 128:
                                        slot * 4 * 128 + (kc + 1) * 128],
                                  h_sh[:, kc * TT:(kc + 1) * TT],
                                  start=False, stop=(kc == 3))
                te.inc(mm)
                dn_done[(tt, mo)] = te.n
                dnpend.append((tt, mo, pd2))
                if len(dnpend) > DN_SKEW:
                    flush_dn(dnpend.pop(0))
                yield
            while dnpend:
                flush_dn(dnpend.pop(0))
        if "moe" in dbg_t:
            wv(g, mst)
            wv(g, lg)
            gdma(dbg_t["moe"].ap()[:, :], moe_d.ap()[:, :])
        yield

    gen_s = emit_ssm()
    gen_m = emit_moe()
    alive_s, alive_m = True, True
    for _ in range(46):
        try:
            next(gen_m)
        except StopIteration:
            alive_m = False
            break
    next(gen_s)
    while alive_s or alive_m:
        if alive_s:
            try:
                next(gen_s)
            except StopIteration:
                alive_s = False
        if alive_m:
            for _ in range(8):
                try:
                    next(gen_m)
                except StopIteration:
                    alive_m = False
                    break
    barrier()
    sm.close()

    p4 = ExitStack()
    p4p = ExitStack()
    ps_s2 = psum(p4p, "ps_s2", [1, 512])
    ps_q2 = psum(p4p, "ps_q2", [1, 512])
    accf = sb(p4, "accf", [128, 8 * TC], F32)
    mu_b = sb(p4, "mu_b2", [128, TT], F32)
    r_b = sb(p4, "r_b2", [128, TT], F32)
    strow = sb(p4, "strow2", [1, 3 * TT], F32)
    p4a = ExitStack()
    ystage = sb(p4a, "ystage", [128, 2 * TT], F32)
    astage = sb(p4a, "astage", [128, 2 * TT], F32)
    ac_bf = sb(p4a, "ac_bf", [128, 8 * TT], BF16)
    aq_bf = sb(p4a, "aq_bf", [128, 8 * TT], BF16)

    yg_cons, hff_done, ots_done = {}, {}, {}
    w1ld, w2ld = {}, {}

    wv(g, cc, 4)
    ag2av = bass.AP(ag2_out, 0, [[TT, 8 * 2 * HOH * 4], [1, TT]])
    for tt in range(NTT):
        for hc in range(8):
            uy = tt * 8 + hc
            ypar = hc % 2
            wv(g, ve, yg_cons.get(uy - 2, 0))
            ygls[ypar].inc(g.indirect_dma_start(
                out=ystage[:, ypar * TT:(ypar + 1) * TT], out_offset=None,
                in_=ag2av,
                in_offset=bass.IndirectOffsetOnAxis(ix_y[:, hc:hc + 1], 0),
                element_offset=tt * TT), 16)
            wv(y, ve, yg_cons.get(uy - 2, 0))
            ydma(astage[:, ypar * TT:(ypar + 1) * TT],
                 accin_d.ap()[hc * 128:(hc + 1) * 128, tt * TT:(tt + 1) * TT],
                 a2ls[ypar])
            wv(v, ygls[ypar])
            wv(v, a2ls[ypar])
            v.tensor_tensor(
                out=accf[:, hc * TC + tt * TT: hc * TC + (tt + 1) * TT],
                in0=astage[:, ypar * TT:(ypar + 1) * TT],
                in1=ystage[:, ypar * TT:(ypar + 1) * TT], op=OP.add)
            yg_cons[uy] = ve.n
    if "accf" in dbg_t:
        wv(y, ve)
        for hc in range(8):
            ydma(dbg_t["accf"].ap()[hc * 128:(hc + 1) * 128, :],
                 accf[:, hc * TC:(hc + 1) * TC])
    for tt in range(NTT):
        wv(s, ve)
        for hc in range(8):
            s.activation(out=aq_bf[:, hc * TT:(hc + 1) * TT],
                         in_=accf[:, hc * TC + tt * TT: hc * TC + (tt + 1) * TT],
                         func=AFT.Square)
            s.activation(out=ac_bf[:, hc * TT:(hc + 1) * TT],
                         in_=accf[:, hc * TC + tt * TT: hc * TC + (tt + 1) * TT],
                         func=AFT.Identity)
        ln_stats(tt, ac_bf, aq_bf, 4, ps_s2, ps_q2, strow, mu_b, r_b)
    barrier()
    p4a.close()
    nstage = sb(p4, "nstage", [128, 8 * TT], BF16)
    hff = sb(p4, "hff", [128, 32 * TT], BF16)
    w1s = sb(p4, "w1s", [128, 4 * 8 * 128], BF16)
    w2s = sb(p4, "w2s", [128, 2 * 32 * 128], BF16)
    xts = sb(p4, "xts", [128, 2 * TT], F32)
    mts = sb(p4, "mts", [128, 2 * TT], F32)
    ots = sb(p4, "ots", [128, 2 * TT], F32)
    astage = sb(p4, "astage2", [128, TT], F32)
    aplybuf = sb(p4, "aplybuf", [128, 8 * TT], F32)

    def w1_load(mf):
        slot = mf % 4
        sdma(w1s[:, slot * 8 * 128:(slot + 1) * 8 * 128],
             bass.AP(w1_b, mf * 128, [[FF, 128], [128 * FF, 8], [1, 128]]),
             w1l[slot])
        w1ld[mf] = (slot, w1l[slot].n)

    def w2_load(mo):
        slot = mo % 2
        ydma(w2s[:, slot * 32 * 128:(slot + 1) * 32 * 128],
             bass.AP(w2_b, mo * 128, [[H, 128], [128 * H, 32], [1, 128]]),
             w2l[slot])
        w2ld[mo] = (slot, w2l[slot].n)

    for mf in range(4):
        w1_load(mf)
    w2_load(0)
    w2_load(1)
    for tt in range(NTT):
        wv(y, stl)
        wv(y, ve)
        ydma(mu_b[:], bass.AP(stats_d, 4 * TC + tt * TT, [[0, 128], [1, TT]]),
             stl)
        ydma(r_b[:], bass.AP(stats_d, 5 * TC + tt * TT, [[0, 128], [1, TT]]),
             stl)
        wv(s, te)
        wv(v, se)
        wv(v, stl)
        v.tensor_tensor(out=aplybuf[:],
                        in0=bass.AP(accf, tt * TT,
                                    [[8 * TC, 128], [TC, 8], [1, TT]]),
                        in1=bass.AP(mu_b, 0, [[512, 128], [0, 8], [1, 512]]),
                        op=OP.subtract)
        v.tensor_tensor(out=aplybuf[:], in0=aplybuf[:],
                        in1=bass.AP(r_b, 0, [[512, 128], [0, 8], [1, 512]]),
                        op=OP.mult)
        wv(s, ve)
        for hc in range(8):
            cs = slice(hc * TT, (hc + 1) * TT)
            s.activation(out=nstage[:, cs], in_=aplybuf[:, cs],
                         func=AFT.Identity, scale=p_ln2g[:, hc:hc + 1],
                         bias=p_ln2b[:, hc:hc + 1])
        nsv = se.n
        for mf in range(32):
            pd3 = (ps_a, ps_b, ps_c, ps_dd)[mf % 4]
            slot = mf % 4
            wsl = w1s[:, slot * 8 * 128:(slot + 1) * 8 * 128]
            wv(t, w1l[w1ld[mf][0]], w1ld[mf][1])
            wv(t, se, nsv)
            bank_wait(pd3)
            for kc in range(8):
                mm = t.matmul(pd3[:, :], wsl[:, kc * 128:(kc + 1) * 128],
                              nstage[:, kc * TT:(kc + 1) * TT],
                              start=(kc == 0), stop=(kc == 7))
            te.inc(mm)
            wv(s, te, te.n)
            s.activation(out=hff[:, mf * TT:(mf + 1) * TT], in_=pd3[:, :],
                         func=AFT.Gelu, bias=p_b1[:, mf:mf + 1])
            bank_set(pd3, se)
            hff_done[mf] = se.n
            wv(s0, te, te.n)
            nxt = mf + 4
            if nxt < 32:
                w1_load(nxt)
            elif tt + 1 < NTT:
                w1_load(nxt - 32)
        for mo in range(8):
            pd4 = (ps_a, ps_b, ps_c, ps_dd)[mo % 4]
            slot = mo % 2
            wsl2 = w2s[:, slot * 32 * 128:(slot + 1) * 32 * 128]
            wv(t, w2l[w2ld[mo][0]], w2ld[mo][1])
            wv(t, se, hff_done[31])
            bank_wait(pd4)
            for kc in range(32):
                mm = t.matmul(pd4[:, :], wsl2[:, kc * 128:(kc + 1) * 128],
                              hff[:, kc * TT:(kc + 1) * TT],
                              start=(kc == 0), stop=(kc == 31))
            te.inc(mm)
            w2v = te.n
            opar = mo % 2
            osl = slice(opar * TT, (opar + 1) * TT)
            wv(y, ve, ots_done.get(mo - 2, 0))
            ydma(xts[:, osl], bass.AP(x_fm, mo * 128 * TC + tt * TT,
                                      [[TC, 128], [1, TT]]), xmls[opar])
            ydma(mts[:, osl],
                 moe_d.ap()[mo * 128:(mo + 1) * 128, tt * TT:(tt + 1) * TT],
                 xmls[opar])
            wv(v, te, w2v)
            wv(v, xmls[opar])
            wv(v, so, so.n)
            v.scalar_tensor_tensor(
                out=ots[:, osl], in0=pd4[:, :], scalar=p_b2[:, mo:mo + 1],
                in1=accf[:, mo * TC + tt * TT: mo * TC + (tt + 1) * TT],
                op0=OP.add, op1=OP.add)
            bank_set(pd4, ve)
            if "attno" in dbg_t:
                wv(y, ve)
                ydma(dbg_t["attno"].ap()[mo * 128:(mo + 1) * 128,
                                         tt * TT:(tt + 1) * TT], ots[:, osl])
                wv(v, ly)
            v.tensor_tensor(out=ots[:, osl], in0=ots[:, osl], in1=xts[:, osl],
                            op=OP.add)
            v.tensor_tensor(out=ots[:, osl], in0=ots[:, osl], in1=mts[:, osl],
                            op=OP.add)
            ots_done[mo] = ve.n
            wv(y, ve, ve.n)
            so.inc(y.dma_start(out=out_t.ap()[mo * 128:(mo + 1) * 128,
                                              tt * TT:(tt + 1) * TT],
                               in_=ots[:, osl]), 16)
            wv(y, te, w2v)
            if mo + 2 < 8:
                w2_load(mo + 2)
            elif tt + 1 < NTT:
                w2_load(mo + 2 - 8)
        ots_done.clear()
        hff_done.clear()
    y.wait_ge(so.h, so.n)
    barrier()
    p4.close()
    p4p.close()
    gst.close()
    return nc



def _prep_core(inputs, k):
    q = k % 2
    b = k // 2
    perm = np.arange(H) if q == 0 else np.r_[np.arange(512, 1024), np.arange(512)]
    asc = np.ascontiguousarray
    f32 = np.float32

    x = np.asarray(inputs["L0_attn_x"])
    xk = x[b, q * TC:(q + 1) * TC, :]
    d = {"x_fm": asc(xk.T[perm])}

    Wd = np.asarray(inputs["L0_attn_Wd"])
    d["wd_t"] = asc(Wd[np.ix_(perm, perm)].T)
    d["wbc_t"] = asc(np.asarray(inputs["L0_attn_Wbc"])[:, perm].T)
    d["w1_t"] = asc(np.asarray(inputs["L0_attn_W1"])[:, perm].T)
    d["w2_t"] = asc(np.asarray(inputs["L0_attn_W2"])[perm].T)
    d["rw_t"] = asc(np.asarray(inputs["L0_ffn_router_weight"])[:, perm].T)
    d["wg_e"] = asc(np.asarray(inputs["L0_ffn_w_gate"])[:, perm, :])
    d["wu_e"] = asc(np.asarray(inputs["L0_ffn_w_up"])[:, perm, :])
    d["wdn_e"] = asc(np.asarray(inputs["L0_ffn_w_down"])[:, :, perm])
    d["swg"] = asc(np.asarray(inputs["L0_ffn_sw_gate"])[perm])
    d["swu"] = asc(np.asarray(inputs["L0_ffn_sw_up"])[perm])
    d["swd"] = asc(np.asarray(inputs["L0_ffn_sw_down"])[:, perm])

    def col8(vv):
        return asc(np.asarray(vv)[perm].reshape(8, 128).T.astype(f32))

    d["ln1_g"] = col8(inputs["L0_attn_ln1_g"])
    d["ln1_b"] = col8(inputs["L0_attn_ln1_b"])
    d["ln2_g"] = col8(inputs["L0_attn_ln2_g"])
    d["ln2_b"] = col8(inputs["L0_attn_ln2_b"])
    d["bd_c"] = col8(inputs["L0_attn_bd"])
    d["b2_c"] = col8(inputs["L0_attn_b2"])
    d["d_c"] = col8(inputs["L0_attn_D"])
    d["b1_c"] = asc(np.asarray(inputs["L0_attn_b1"]).reshape(32, 128).T
                    .astype(f32))
    d["bbc_c"] = asc(np.asarray(inputs["L0_attn_bbc"]).reshape(32, 1)
                     .astype(f32))
    alog = np.asarray(inputs["L0_attn_A_log"])[perm[:HOH]]
    a2 = np.empty((128, 64), f32)
    for tile in range(64):
        for hl in range(8):
            for n in range(NS):
                a2[hl * NS + n, tile] = alog[tile * 8 + hl, n]
    d["alog2"] = asc(a2)
    es = np.zeros((128, 16 * 128), f32)
    for ktile in range(16):
        for p in range(128):
            es[p, ktile * 128 + 8 * ktile + p // NS] = 1.0
    d["esel_i"] = asc(es)
    cb = np.asarray(inputs["L0_ffn_correction_bias"]).astype(f32)
    d["cbt_i"] = asc(np.tile(np.tile(cb, 64), (32, 1)))

    gi = np.zeros((128, 20), np.uint32)
    for sh in range(2):
        src = 2 * b + sh
        for c in range(4):
            p = np.arange(128)
            i = c * 128 + p
            gl = q * 512 + i
            pos = gl if sh == 0 else (gl + 512) % H
            gi[:, sh * 4 + c] = src * AG1R + pos
            gi[:, 8 + sh * 4 + c] = src * H + pos
        gi[0:32, 16 + sh] = src * AG1R + H + np.arange(32)
    d["gidx"] = gi
    yi = np.zeros((128, 8), np.uint32)
    for hc in range(8):
        p = np.arange(128)
        i = hc * 128 + p
        row = np.where(i < 512,
                       q * 8 * HOH + k * HOH + i,
                       q * 8 * HOH + (k ^ 1) * HOH + (i - 512))
        yi[:, hc] = row * 4
    d["yidx"] = yi
    d = {kk: (vv if vv.dtype == np.uint32 else vv.astype(np.float32))
         for kk, vv in d.items()}
    return d, perm


LAST_RESULTS = None


def kernel(**inputs):
    global LAST_RESULTS
    in_maps = []
    perms = []
    for k in range(8):
        m, perm = _prep_core(inputs, k)
        in_maps.append(m)
        perms.append(perm)
    nc = build_program(dbg=KDBG)
    res = run_bass_kernel_spmd(nc, in_maps, core_ids=list(range(8)))
    LAST_RESULTS = res.results
    out = np.empty((4, L, H), np.float32)
    for k in range(8):
        q, b = k % 2, k // 2
        o_fm = res.results[k]["out"]
        o_nat = np.empty_like(o_fm)
        o_nat[perms[k]] = o_fm
        out[b, q * TC:(q + 1) * TC, :] = o_nat.T
    return out


if __name__ == "__main__":
    nc = build_program()
    print("built OK")


# version 11
# speedup vs baseline: 1.0556x; 1.0101x over previous
import os
from contextlib import ExitStack

import numpy as np
import ml_dtypes
import concourse.bass as bass
import concourse.mybir as mybir
from concourse.bass_utils import run_bass_kernel_spmd
from concourse.alu_op_type import AluOpType as OP

F32 = mybir.dt.float32
BF16 = mybir.dt.bfloat16
U32 = mybir.dt.uint32
AFT = mybir.ActivationFunctionType
AXX = mybir.AxisListType.X

H = 1024
NS = 16
FF = 4096
II = 512
NE = 8
TC = 2048
TT = 512
NTT = TC // TT
L = 4096
EPS = 1e-5
HOH = 512
AG1R = H + 32
TQ = 1024
BIG_NEG = -1e30
RG8 = [[0, 1, 2, 3, 4, 5, 6, 7]]

KDBG = ()
RACECHECK = bool(os.environ.get("KERNEL_RACECHECK"))
GU_SKEW = 1
DN_SKEW = 1
YMM_SKEW = 1


class Sem:
    def __init__(self, st, nc, name):
        self.h = st.enter_context(nc.semaphore(name))
        self.n = 0

    def inc(self, inst, k=1):
        inst.then_inc(self.h, k)
        self.n += k
        return inst


def _ap_tname(x):
    try:
        t = x.tensor
        return getattr(t, "name", None) or str(id(t))
    except Exception:
        return None


class EW:
    OPS = ("tensor_tensor", "tensor_scalar", "scalar_tensor_tensor",
           "tensor_copy", "tensor_reduce", "reciprocal", "memset",
           "transpose", "tensor_tensor_scan", "activation", "iota")
    RKEYS = ("in_", "in0", "in1", "data0", "data1", "initial", "scalar",
             "scalar1", "scalar2", "bias", "scale", "mask")

    def __init__(self, eng, sem, waited):
        self._e = eng
        self._s = sem
        self._w = waited
        self._tw = {}
        self._tr = {}

    def __getattr__(self, nm):
        f = getattr(self._e, nm)
        if nm in EW.OPS:
            def wrapped(*a, **k):
                if RACECHECK:
                    key = (id(self._e), id(self._s))
                    if self._w.get(key, -1) < self._s.n:
                        self._e.wait_ge(self._s.h, self._s.n)
                        self._w[key] = self._s.n
                else:
                    wt = k.get("out")
                    if wt is None and nm == "memset" and a:
                        wt = a[0]
                    rd = [k[r] for r in EW.RKEYS if r in k]
                    need = 0
                    wtn = _ap_tname(wt) if wt is not None else None
                    for r in rd:
                        t2 = _ap_tname(r)
                        if t2 is not None:
                            need = max(need, self._tw.get(t2, 0))
                    if wtn is not None:
                        need = max(need, self._tw.get(wtn, 0),
                                   self._tr.get(wtn, 0))
                    key = (id(self._e), id(self._s))
                    if need > self._w.get(key, -1):
                        self._e.wait_ge(self._s.h, need)
                        self._w[key] = need
                r2 = self._s.inc(f(*a, **k))
                if not RACECHECK:
                    if wtn is not None:
                        self._tw[wtn] = self._s.n
                    for r in rd:
                        t2 = _ap_tname(r)
                        if t2 is not None:
                            self._tr[t2] = self._s.n
                return r2
            return wrapped
        return f


def build_program(dbg=()):
    nc = bass.Bass("TRN2", target_bir_lowering=False,
                   detect_race_conditions=RACECHECK)
    t, v0, s0, y, g = nc.tensor, nc.vector, nc.scalar, nc.sync, nc.gpsimd

    def din(name, shape, dt=F32):
        return nc.dram_tensor(name, shape, dt, kind="ExternalInput")

    x_fm = din("x_fm", [H, TC])
    wd_t = din("wd_t", [H, H])
    wbc_t = din("wbc_t", [H, 32])
    w1_t = din("w1_t", [H, FF])
    w2_t = din("w2_t", [FF, H])
    rw_t = din("rw_t", [H, NE])
    wg_e = din("wg_e", [NE, H, II])
    wu_e = din("wu_e", [NE, H, II])
    wdn_e = din("wdn_e", [NE, II, H])
    swg = din("swg", [H, II])
    swu = din("swu", [H, II])
    swd = din("swd", [II, H])
    ln1_g = din("ln1_g", [128, 8])
    ln1_b = din("ln1_b", [128, 8])
    ln2_g = din("ln2_g", [128, 8])
    ln2_b = din("ln2_b", [128, 8])
    bd_c = din("bd_c", [128, 8])
    b2_c = din("b2_c", [128, 8])
    d_c = din("d_c", [128, 8])
    b1_c = din("b1_c", [128, 32])
    bbc_c = din("bbc_c", [32, 1])
    alog2 = din("alog2", [128, 64])
    esel_i = din("esel_i", [128, 16 * 128])
    cbt_i = din("cbt_i", [32, 512])
    gidx = din("gidx", [128, 20], U32)
    yidx = din("yidx", [128, 8], U32)

    out_t = nc.dram_tensor("out", [H, TC], F32, kind="ExternalOutput")

    ag1a_in = nc.dram_tensor("ag1a_in", [AG1R, TC], F32, kind="Internal")
    ag1a_out = nc.dram_tensor("ag1a_out", [8 * AG1R, TC], F32, kind="Internal")
    ag1b_in = nc.dram_tensor("ag1b_in", [H, TC], BF16, kind="Internal")
    ag1b_out = nc.dram_tensor("ag1b_out", [8 * H, TC], BF16, kind="Internal")
    ag2_in = nc.dram_tensor("ag2_in", [2 * HOH, TC], F32, kind="Internal")
    ag2_out = nc.dram_tensor("ag2_out", [8 * 2 * HOH, TC], F32, kind="Internal")
    dloc = nc.dram_tensor("dloc", [HOH, L], BF16, kind="Internal")
    dxloc = nc.dram_tensor("dxloc", [HOH, L], BF16, kind="Internal")
    accin_d = nc.dram_tensor("accin_d", [H, TC], F32, kind="Internal")
    moe_d = nc.dram_tensor("moe_d", [H, TC], F32, kind="Internal")
    bc_seq = nc.dram_tensor("bc_seq", [32, L], BF16, kind="Internal")
    stats_d = nc.dram_tensor("stats_d", [8, TC], F32, kind="Internal")
    comb_d = nc.dram_tensor("comb_d", [NE, TC], BF16, kind="Internal")
    w1_b = nc.dram_tensor("w1_b", [H, FF], BF16, kind="Internal")
    w2_b = nc.dram_tensor("w2_b", [FF, H], BF16, kind="Internal")
    wg_b = nc.dram_tensor("wg_b", [NE, H, II], BF16, kind="Internal")
    wu_b = nc.dram_tensor("wu_b", [NE, H, II], BF16, kind="Internal")
    wdn_b = nc.dram_tensor("wdn_b", [NE, II, H], BF16, kind="Internal")
    swg_b = nc.dram_tensor("swg_b", [H, II], BF16, kind="Internal")
    swu_b = nc.dram_tensor("swu_b", [H, II], BF16, kind="Internal")
    swd_b = nc.dram_tensor("swd_b", [II, H], BF16, kind="Internal")

    dbg_t = {}
    for nm in dbg:
        shp, dt = {
            "xn_bf": ([H, TC], BF16), "delta": ([H, TC], F32),
            "bcv": ([32, L], BF16),
            "yssm": ([2 * HOH, TC], F32), "accin": ([H, TC], F32),
            "logits": ([8, TC], F32), "comb": ([NE, TC], BF16),
            "moe": ([H, TC], F32), "accf": ([H, TC], F32),
            "normed": ([H, TC], BF16), "attno": ([H, TC], F32),
        }[nm]
        dbg_t[nm] = nc.dram_tensor("dbg_" + nm, shp, dt, kind="ExternalOutput")

    gst = ExitStack()

    def sb(stk, name, shape, dt):
        return stk.enter_context(nc.sbuf_tensor(name, shape, dt))

    def psum(stk, name, shape):
        return stk.enter_context(nc.psum_tensor(name, shape, F32))

    sems = {n: Sem(gst, nc, "sem_" + n)
            for n in ("ly", "ls", "lg", "te", "ve", "se", "ge", "cc", "so",
                      "xw", "stl", "agst", "bcl", "yst", "crl", "mst",
                      "a2l", "xml",
                      "dld0", "dld1", "dld2", "dld3", "mwl0", "mwl1", "wdl0", "wdl1",
                      "w1l0", "w1l1", "w1l2", "w1l3", "w2l0", "w2l1",
                      "crl0", "crl1", "a2l0", "a2l1", "ygl0", "ygl1",
                      "xml0", "xml1", "dsg0", "dsg1", "acg0", "acg1", "bcg")}
    ly, ls, lg, te, ve, se, ge, cc, so = (
        sems[n] for n in ("ly", "ls", "lg", "te", "ve", "se", "ge", "cc", "so"))
    xw, stl, agst, bcl, yst, crl, mst, a2l, xml = (
        sems[n] for n in ("xw", "stl", "agst", "bcl", "yst", "crl", "mst",
                          "a2l", "xml"))
    dld = [sems["dld0"], sems["dld1"], sems["dld2"], sems["dld3"]]
    crls = [sems["crl0"], sems["crl1"]]
    a2ls = [sems["a2l0"], sems["a2l1"]]
    ygls = [sems["ygl0"], sems["ygl1"]]
    xmls = [sems["xml0"], sems["xml1"]]
    dsg = [sems["dsg0"], sems["dsg1"]]
    acg = [sems["acg0"], sems["acg1"]]
    bcg = sems["bcg"]
    mwl = [sems["mwl0"], sems["mwl1"]]
    wdl = [sems["wdl0"], sems["wdl1"]]
    w1l = [sems["w1l0"], sems["w1l1"], sems["w1l2"], sems["w1l3"]]
    w2l = [sems["w2l0"], sems["w2l1"]]
    waited = {}
    v = EW(v0, ve, waited)
    s = EW(s0, se, waited)

    def wv(eng, sem, val=None):
        val = sem.n if val is None else val
        e0 = eng._e if isinstance(eng, EW) else eng
        key = (id(e0), id(sem))
        if waited.get(key, -1) >= val:
            return
        e0.wait_ge(sem.h, val)
        waited[key] = val

    def ydma(dst, src, sem=None):
        (sem or ly).inc(y.dma_start(out=dst, in_=src), 16)

    def sdma(dst, src, sem=None):
        (sem or ls).inc(s0.dma_start(out=dst, in_=src), 16)

    def gdma(dst, src, sem=None):
        (sem or lg).inc(g.dma_start(out=dst, in_=src), 16)

    g_tw, g_tr = {}, {}

    def gtt(out, in0, in1, op):
        key = (id(g), id(ge))
        if RACECHECK:
            if waited.get(key, -1) < ge.n:
                g.wait_ge(ge.h, ge.n)
                waited[key] = ge.n
        else:
            need = 0
            wtn = _ap_tname(out)
            for r in (in0, in1):
                t2 = _ap_tname(r)
                if t2 is not None:
                    need = max(need, g_tw.get(t2, 0))
            need = max(need, g_tw.get(wtn, 0), g_tr.get(wtn, 0))
            if need > waited.get(key, -1):
                g.wait_ge(ge.h, need)
                waited[key] = need
        r2 = ge.inc(g.tensor_tensor(out=out, in0=in0, in1=in1, op=op))
        if not RACECHECK:
            g_tw[wtn] = ge.n
            for r in (in0, in1):
                t2 = _ap_tname(r)
                if t2 is not None:
                    g_tr[t2] = ge.n
        return r2

    bank_guard = {}

    def bank_wait(*banks):
        for bk in banks:
            for sem, val in bank_guard.get(id(bk), ()):
                wv(t, sem, val)

    def bank_set(bank, sem):
        bank_guard[id(bank)] = ((sem, sem.n),)

    def barrier():
        for eng in (t, v, s, y, g):
            for sem in sems.values():
                if sem.n and sem is not so and sem is not cc:
                    wv(eng, sem)

    ones_bf = sb(gst, "ones_bf", [128, 1], BF16)
    ones_f = sb(gst, "ones_f", [1, 128], F32)
    xn_bf = sb(gst, "xn_bf", [128, 8 * TC], BF16)
    negA = sb(gst, "negA", [128, 64], F32)
    esel = sb(gst, "esel", [128, 16 * 128], BF16)
    states = sb(gst, "states", [128, 64], F32)
    p_ln1g = sb(gst, "p_ln1g", [128, 8], F32)
    p_ln1b = sb(gst, "p_ln1b", [128, 8], F32)
    p_ln2g = sb(gst, "p_ln2g", [128, 8], F32)
    p_ln2b = sb(gst, "p_ln2b", [128, 8], F32)
    p_bd = sb(gst, "p_bd", [128, 8], F32)
    p_b2 = sb(gst, "p_b2", [128, 8], F32)
    p_d = sb(gst, "p_d", [128, 8], F32)
    p_b1 = sb(gst, "p_b1", [128, 32], F32)
    p_bbc = sb(gst, "p_bbc", [32, 1], F32)
    p_cbt = sb(gst, "p_cbt", [32, 512], F32)
    ix_g = sb(gst, "ix_g", [128, 20], U32)
    ix_y = sb(gst, "ix_y", [128, 8], U32)
    alog_sb = sb(gst, "alog_sb", [128, 64], F32)

    ps_a = psum(gst, "ps_a", [128, 512])
    ps_b = psum(gst, "ps_b", [128, 512])
    ps_c = psum(gst, "ps_c", [128, 512])
    ps_dd = psum(gst, "ps_d", [128, 512])

    for dst, src in ((p_ln1g, ln1_g), (p_ln1b, ln1_b), (p_ln2g, ln2_g),
                     (p_ln2b, ln2_b), (p_bd, bd_c), (p_b2, b2_c), (p_d, d_c),
                     (p_b1, b1_c), (p_bbc, bbc_c), (p_cbt, cbt_i),
                     (ix_g, gidx), (ix_y, yidx), (alog_sb, alog2)):
        ydma(dst[:], src.ap())
    gdma(esel[:], esel_i.ap())
    v.memset(ones_bf[:], 1.0)
    v.memset(ones_f[0:1, :], 1.0)
    v.memset(states[:], 0.0)
    wv(s, ly)
    s.activation(out=negA[:], in_=alog_sb[:], func=AFT.Exp)
    wv(v, se)
    v.tensor_scalar(out=negA[:], in0=negA[:], scalar1=-1.0, scalar2=None,
                    op0=OP.mult)

    fer = ExitStack()
    logits32 = sb(fer, "logits32", [32, TC], F32)
    fe = ExitStack()
    v.memset(logits32[:], 0.0)
    fep = ExitStack()
    ps_s = psum(fep, "ps_s", [1, 512])
    ps_q = psum(fep, "ps_q", [1, 512])
    ps_bc = psum(fep, "ps_bc", [32, 512])
    ps_rt = psum(fep, "ps_rt", [8, 512])

    mu_b = sb(fe, "mu_b", [128, TT], F32)
    r_b = sb(fe, "r_b", [128, TT], F32)
    strow = sb(fe, "strow", [1, 3 * TT], F32)
    wdT = sb(fe, "wdT", [128, 8 * H], BF16)
    wbcT = sb(fe, "wbcT", [128, 8 * 32], BF16)
    rwT = sb(fe, "rwT", [128, 8 * NE], BF16)
    for kc in range(8):
        gdma(wdT[:, kc * H:(kc + 1) * H], wd_t.ap()[kc * 128:(kc + 1) * 128, :])
        gdma(wbcT[:, kc * 32:(kc + 1) * 32], wbc_t.ap()[kc * 128:(kc + 1) * 128, :])
        gdma(rwT[:, kc * NE:(kc + 1) * NE], rw_t.ap()[kc * 128:(kc + 1) * 128, :])
    for db, df in ((wg_b, wg_e), (wu_b, wu_e), (wdn_b, wdn_e), (swg_b, swg),
                   (swu_b, swu), (swd_b, swd), (w1_b, w1_t), (w2_b, w2_t)):
        gdma(db.ap(), df.ap())
    lg_precast = lg.n

    xt = sb(fe, "xt", [128, 8 * TT], F32)
    xb_bf = sb(fe, "xb_bf", [128, 8 * TT], BF16)
    xq_bf = sb(fe, "xq_bf", [128, 8 * TT], BF16)
    xn_f = sb(fe, "xn_f", [128, 8 * TT], F32)
    xn2_f = sb(fe, "xn2_f", [128, 8 * TT], F32)
    xn2b = sb(fe, "xn2b", [128, 8 * TT], BF16)
    dstage = sb(fe, "dstage", [128, 2 * TT], F32)
    dxstage = sb(fe, "dxstage", [128, 2 * TT], BF16)
    bcst = sb(fe, "bcst", [32, TT], F32)
    acst = sb(fe, "acst", [128, 2 * TT], F32)

    def ln_stats(tt, src_bf, sq_bf, row, pss, psq, strow, mu_b, r_b,
                 store=False):
        wv(t, ve)
        wv(t, se)
        wv(v, stl)
        bank_wait(pss, psq)
        for hc in range(8):
            t.matmul(pss[:, :], ones_bf[:], src_bf[:, hc * TT:(hc + 1) * TT],
                     start=(hc == 0), stop=(hc == 7))
        for hc in range(8):
            mm = t.matmul(psq[:, :], ones_bf[:], sq_bf[:, hc * TT:(hc + 1) * TT],
                          start=(hc == 0), stop=(hc == 7))
        te.inc(mm)
        wv(v, te)
        v.tensor_scalar(out=strow[0:1, 0:TT], in0=pss[0:1, :],
                        scalar1=1.0 / H, scalar2=None, op0=OP.mult)
        v.tensor_scalar(out=strow[0:1, TT:2 * TT], in0=psq[0:1, :],
                        scalar1=1.0 / H, scalar2=None, op0=OP.mult)
        bank_set(pss, ve)
        bank_set(psq, ve)
        v.tensor_tensor(out=strow[0:1, 2 * TT:3 * TT], in0=strow[0:1, 0:TT],
                        in1=strow[0:1, 0:TT], op=OP.mult)
        v.tensor_tensor(out=strow[0:1, 2 * TT:3 * TT],
                        in0=strow[0:1, 2 * TT:3 * TT],
                        in1=strow[0:1, TT:2 * TT], op=OP.subtract)
        v.tensor_scalar(out=strow[0:1, 2 * TT:3 * TT],
                        in0=strow[0:1, 2 * TT:3 * TT], scalar1=-1.0,
                        scalar2=EPS, op0=OP.mult, op1=OP.add)
        wv(s, ve)
        s.activation(out=strow[0:1, TT:2 * TT], in_=strow[0:1, 2 * TT:3 * TT],
                     func=AFT.Ln)
        s.activation(out=strow[0:1, TT:2 * TT], in_=strow[0:1, TT:2 * TT],
                     func=AFT.Exp, scale=-0.5)
        if store:
            wv(y, se)
            wv(y, ve)
            ydma(stats_d.ap()[row:row + 1, tt * TT:(tt + 1) * TT],
                 strow[0:1, 0:TT], stl)
            ydma(stats_d.ap()[row + 1:row + 2, tt * TT:(tt + 1) * TT],
                 strow[0:1, TT:2 * TT], stl)
            return
        bank_wait(ps_c, ps_dd)
        wv(t, se)
        t.matmul(ps_c[:, :], ones_f[0:1, :], strow[0:1, 0:TT],
                 start=True, stop=True)
        mmb = t.matmul(ps_dd[:, :], ones_f[0:1, :], strow[0:1, TT:2 * TT],
                       start=True, stop=True)
        te.inc(mmb)
        wv(s, te)
        s.activation(out=mu_b[:], in_=ps_c[:, :], func=AFT.Identity)
        s.activation(out=r_b[:], in_=ps_dd[:, :], func=AFT.Identity)
        bank_set(ps_c, se)
        bank_set(ps_dd, se)

    for tt in range(NTT):
        wv(y, ve)
        wv(y, se)
        ydma(xt[:], bass.AP(x_fm, tt * TT, [[TC, 128], [128 * TC, 8], [1, TT]]),
             xw)
        wv(s, xw)
        s.activation(out=xq_bf[:], in_=xt[:], func=AFT.Square)
        wv(v, xw)
        v.tensor_copy(out=xb_bf[:], in_=xt[:])
        ln_stats(tt, xb_bf, xq_bf, 0, ps_s, ps_q, strow, mu_b, r_b)
        wv(v, se)
        v.tensor_tensor(out=xn_f[:], in0=xt[:],
                        in1=bass.AP(mu_b, 0, [[512, 128], [0, 8], [1, 512]]),
                        op=OP.subtract)
        v.tensor_tensor(out=xn_f[:], in0=xn_f[:],
                        in1=bass.AP(r_b, 0, [[512, 128], [0, 8], [1, 512]]),
                        op=OP.mult)
        v.tensor_copy(
            out=bass.AP(xn_bf, tt * TT, [[8 * TC, 128], [TC, 8], [1, TT]]),
            in_=xn_f[:])
        wv(s, ve)
        s.activation(out=xq_bf[:], in_=xn_f[:], func=AFT.Square)
        s.activation(out=xb_bf[:], in_=xn_f[:], func=AFT.Identity)
        ln_stats(tt, xb_bf, xq_bf, 2, ps_s, ps_q, strow, mu_b, r_b)
        wv(v, se)
        v.tensor_tensor(out=xn2_f[:], in0=xn_f[:],
                        in1=bass.AP(mu_b, 0, [[512, 128], [0, 8], [1, 512]]),
                        op=OP.subtract)
        v.tensor_tensor(out=xn2_f[:], in0=xn2_f[:],
                        in1=bass.AP(r_b, 0, [[512, 128], [0, 8], [1, 512]]),
                        op=OP.mult)
        wv(s, ve)
        wv(s, te)
        for hc in range(8):
            cs = slice(hc * TT, (hc + 1) * TT)
            s.activation(out=xn2b[:, cs], in_=xn2_f[:, cs], func=AFT.Identity,
                         scale=p_ln1g[:, hc:hc + 1], bias=p_ln1b[:, hc:hc + 1])
        wv(v, se)
        v.tensor_copy(out=xn2_f[:], in_=xn2b[:])
        for hc in range(8):
            cs = slice(hc * TT, (hc + 1) * TT)
            stg = acst[:, (hc % 2) * TT:(hc % 2 + 1) * TT]
            wv(v, acg[hc % 2])
            v.scalar_tensor_tensor(out=stg, in0=xn2_f[:, cs],
                                   scalar=p_d[:, hc:hc + 1], in1=xn_f[:, cs],
                                   op0=OP.mult, op1=OP.add)
            wv(y, ve)
            ydma(accin_d.ap()[hc * 128:(hc + 1) * 128, tt * TT:(tt + 1) * TT],
                 stg, acg[hc % 2])
        wv(t, lg)
        wv(t, se)
        for mo in range(8):
            pd = ps_a if (mo % 2 == 0) else ps_b
            dstg = dstage[:, (mo % 2) * TT:(mo % 2 + 1) * TT]
            dxstg = dxstage[:, (mo % 2) * TT:(mo % 2 + 1) * TT]
            bank_wait(pd)
            for kc in range(8):
                mm = t.matmul(pd[:, :],
                              wdT[:, kc * H + mo * 128: kc * H + (mo + 1) * 128],
                              xn2b[:, kc * TT:(kc + 1) * TT],
                              start=(kc == 0), stop=(kc == 7))
            te.inc(mm)
            wv(s, te)
            wv(s, dsg[mo % 2])
            s.activation(out=dstg, in_=pd[:, :], func=AFT.Abs,
                         bias=p_bd[:, mo:mo + 1])
            s.activation(out=dstg, in_=dstg, func=AFT.Exp, scale=-1.0)
            s.activation(out=dstg, in_=dstg, func=AFT.Ln, bias=1.0)
            s.activation(out=dxstg, in_=pd[:, :], func=AFT.Relu,
                         bias=p_bd[:, mo:mo + 1])
            bank_set(pd, se)
            wv(v, se)
            wv(v, dsg[mo % 2])
            v.tensor_tensor(out=dstg, in0=dstg, in1=dxstg, op=OP.add)
            v.tensor_tensor(out=dxstg, in0=dstg,
                            in1=xn2_f[:, mo * TT:(mo + 1) * TT], op=OP.mult)
            wv(y, ve)
            ydma(ag1a_in.ap()[mo * 128:(mo + 1) * 128, tt * TT:(tt + 1) * TT],
                 dstg, dsg[mo % 2])
            ydma(ag1b_in.ap()[mo * 128:(mo + 1) * 128, tt * TT:(tt + 1) * TT],
                 dxstg, dsg[mo % 2])
        bank_wait(ps_bc)
        for kc in range(8):
            mm = t.matmul(ps_bc[:, :], wbcT[:, kc * 32:(kc + 1) * 32],
                          xn2b[:, kc * TT:(kc + 1) * TT],
                          start=(kc == 0), stop=(kc == 7))
        te.inc(mm)
        wv(v, te)
        wv(v, bcg)
        v.tensor_scalar(out=bcst[:], in0=ps_bc[:, :], scalar1=p_bbc[:, 0:1],
                        scalar2=None, op0=OP.add)
        bank_set(ps_bc, ve)
        wv(y, ve)
        ydma(ag1a_in.ap()[H:H + 32, tt * TT:(tt + 1) * TT], bcst[:], bcg)
        bank_wait(ps_rt)
        for kc in range(8):
            mm = t.matmul(ps_rt[:, :], rwT[:, kc * NE:(kc + 1) * NE],
                          xn_bf[:, kc * TC + tt * TT: kc * TC + (tt + 1) * TT],
                          start=(kc == 0), stop=(kc == 7))
        te.inc(mm)
        wv(v, te)
        v.tensor_copy(out=logits32[0:8, tt * TT:(tt + 1) * TT], in_=ps_rt[:, :])
        bank_set(ps_rt, ve)

    if "xn_bf" in dbg_t:
        wv(y, ve)
        for hc in range(8):
            ydma(dbg_t["xn_bf"].ap()[hc * 128:(hc + 1) * 128, :],
                 xn_bf[:, hc * TC:(hc + 1) * TC])
    if "logits" in dbg_t:
        wv(y, ve)
        ydma(dbg_t["logits"].ap()[:, :], logits32[0:8, :])
    if "accin" in dbg_t:
        wv(g, ly)
        wv(g, lg)
        gdma(dbg_t["accin"].ap()[:, :], accin_d.ap()[:, :])
    if "delta" in dbg_t:
        wv(g, ly)
        wv(g, lg)
        gdma(dbg_t["delta"].ap()[:, :], ag1a_in.ap()[0:H, :])

    for sem_ in (dsg[0], dsg[1], acg[0], acg[1], bcg):
        wv(g, sem_)
    cc.inc(g.collective_compute("AllGather", OP.bypass, replica_groups=RG8,
                                ins=[ag1a_in.ap()], outs=[ag1a_out.ap()]))
    cc.inc(g.collective_compute("AllGather", OP.bypass, replica_groups=RG8,
                                ins=[ag1b_in.ap()], outs=[ag1b_out.ap()]))
    barrier()
    fe.close()
    fep.close()

    ro = ExitStack()
    R32 = sb(ro, "R32", [32, 64 * 32], F32)
    sco = sb(ro, "sco", [32, 512], F32)
    msc = sb(ro, "msc", [32, 512], F32)
    gtmp = sb(ro, "gtmp", [32, 128], F32)
    gtmp2 = sb(ro, "gtmp2", [32, 128], F32)
    ge0 = sb(ro, "ge0", [32, 64], F32)
    ge1 = sb(ro, "ge1", [32, 64], F32)
    m1c = sb(ro, "m1c", [32, 64], F32)
    m1r = sb(ro, "m1r", [32, 512], F32)
    f1 = sb(ro, "f1", [32, 512], F32)
    f2 = sb(ro, "f2", [32, 512], F32)
    cum = sb(ro, "cum", [32, 512], F32)
    cum2 = sb(ro, "cum2", [32, 512], F32)
    wsc = sb(ro, "wsc", [32, 64], F32)
    w1c = sb(ro, "w1c", [32, 64], F32)
    w2c = sb(ro, "w2c", [32, 64], F32)
    prodw = sb(ro, "prodw", [32, 512], F32)
    padbuf = sb(ro, "padbuf", [32, 64 * 32], F32)
    combT = sb(ro, "combT", [32, TC], F32)

    def rap(tsr, fsz, off, dims, np_=32):
        return bass.AP(tsr, off, [[fsz, np_]] + dims)

    for j in range(64):
        v.transpose(out=R32[0:32, j * 32:(j + 1) * 32],
                    in_=logits32[0:32, j * 32:(j + 1) * 32])
    wv(s, ve)
    s.activation(out=R32[:], in_=R32[:], func=AFT.Tanh, scale=0.5)
    wv(v, se)
    v.tensor_scalar(out=rap(sco, 512, 0, [[8, 64], [1, 8]]),
                    in0=rap(R32, 64 * 32, 0, [[32, 64], [1, 8]]),
                    scalar1=0.5, scalar2=0.5, op0=OP.mult, op1=OP.add)
    wv(v, ly)
    v.tensor_tensor(out=msc[:], in0=sco[:], in1=p_cbt[:], op=OP.add)
    for idx, (i, j) in enumerate([(0, 1), (0, 2), (0, 3), (1, 2), (1, 3),
                                  (2, 3)]):
        dst = gtmp if idx == 0 else gtmp2
        v.tensor_tensor(out=dst[:], in0=rap(msc, 512, i, [[4, 128]]),
                        in1=rap(msc, 512, j, [[4, 128]]), op=OP.add)
        if idx > 0:
            v.tensor_tensor(out=gtmp[:], in0=gtmp[:], in1=gtmp2[:], op=OP.max)
    v.tensor_tensor(out=ge0[:], in0=rap(gtmp, 128, 0, [[2, 64]]),
                    in1=rap(gtmp, 128, 1, [[2, 64]]), op=OP.is_ge)
    v.tensor_tensor(out=ge1[:], in0=rap(gtmp, 128, 1, [[2, 64]]),
                    in1=rap(gtmp, 128, 0, [[2, 64]]), op=OP.is_gt)
    for e in range(4):
        v.tensor_copy(out=rap(m1r, 512, e, [[8, 64]]), in_=ge0[:])
        v.tensor_copy(out=rap(m1r, 512, 4 + e, [[8, 64]]), in_=ge1[:])
    v.tensor_tensor(out=msc[:], in0=msc[:], in1=m1r[:], op=OP.mult)

    def top_select(fdst):
        v.tensor_reduce(out=m1c[:], in_=rap(msc, 512, 0, [[8, 64], [1, 8]]),
                        axis=AXX, op=OP.max)
        for e in range(8):
            v.tensor_copy(out=rap(m1r, 512, e, [[8, 64]]), in_=m1c[:])
        v.tensor_tensor(out=fdst[:], in0=msc[:], in1=m1r[:], op=OP.is_ge)
        v.tensor_copy(out=cum[:], in_=fdst[:])
        src, dst = cum, cum2
        for shd in (1, 2, 4):
            v.tensor_copy(out=dst[:], in_=src[:])
            v.tensor_tensor(
                out=rap(dst, 512, shd, [[8, 64], [1, 8 - shd]]),
                in0=rap(src, 512, shd, [[8, 64], [1, 8 - shd]]),
                in1=rap(src, 512, 0, [[8, 64], [1, 8 - shd]]), op=OP.add)
            src, dst = dst, src
        v.tensor_scalar(out=dst[:], in0=src[:], scalar1=1.0, scalar2=None,
                        op0=OP.is_le)
        v.tensor_tensor(out=fdst[:], in0=fdst[:], in1=dst[:], op=OP.mult)

    top_select(f1)
    v.tensor_tensor(out=prodw[:], in0=f1[:], in1=sco[:], op=OP.mult)
    v.tensor_reduce(out=w1c[:], in_=rap(prodw, 512, 0, [[8, 64], [1, 8]]),
                    axis=AXX, op=OP.add)
    v.scalar_tensor_tensor(out=msc[:], in0=f1[:], scalar=BIG_NEG, in1=msc[:],
                           op0=OP.mult, op1=OP.add)
    top_select(f2)
    v.tensor_tensor(out=prodw[:], in0=f2[:], in1=sco[:], op=OP.mult)
    v.tensor_reduce(out=w2c[:], in_=rap(prodw, 512, 0, [[8, 64], [1, 8]]),
                    axis=AXX, op=OP.add)
    v.tensor_tensor(out=wsc[:], in0=w1c[:], in1=w2c[:], op=OP.add)
    v.tensor_scalar(out=wsc[:], in0=wsc[:], scalar1=2.0, scalar2=2e-20,
                    op0=OP.mult, op1=OP.add)
    v.reciprocal(out=wsc[:], in_=wsc[:])
    for e in range(8):
        v.tensor_copy(out=rap(m1r, 512, e, [[8, 64]]), in_=w1c[:])
    v.tensor_tensor(out=f1[:], in0=f1[:], in1=m1r[:], op=OP.mult)
    for e in range(8):
        v.tensor_copy(out=rap(m1r, 512, e, [[8, 64]]), in_=w2c[:])
    v.tensor_tensor(out=f2[:], in0=f2[:], in1=m1r[:], op=OP.mult)
    v.tensor_tensor(out=f1[:], in0=f1[:], in1=f2[:], op=OP.add)
    for e in range(8):
        v.tensor_copy(out=rap(m1r, 512, e, [[8, 64]]), in_=wsc[:])
    v.tensor_tensor(out=f1[:], in0=f1[:], in1=m1r[:], op=OP.mult)
    v.memset(padbuf[:], 0.0)
    v.tensor_copy(out=rap(padbuf, 64 * 32, 0, [[32, 64], [1, 8]]),
                  in_=rap(f1, 512, 0, [[8, 64], [1, 8]]))
    for j in range(64):
        v.transpose(out=combT[0:32, j * 32:(j + 1) * 32],
                    in_=padbuf[0:32, j * 32:(j + 1) * 32])
    wv(g, ve)
    gdma(comb_d.ap()[:, :], combT[0:8, :])
    lg_comb = lg.n
    if "comb" in dbg_t:
        wv(g, lg)
        gdma(dbg_t["comb"].ap()[:, :], comb_d.ap()[:, :])
    barrier()
    ro.close()
    fer.close()

    sm = ExitStack()
    pro = ExitStack()
    gstg = sb(pro, "gstg", [128, TC], F32)
    gstgb = sb(pro, "gstgb", [128, TC], BF16)
    bcs = sb(pro, "bcs", [32, 2 * TC], F32)
    wv(g, cc, 2)
    for sh in range(2):
        lg.inc(g.indirect_dma_start(
            out=bcs[0:32, sh * TC:(sh + 1) * TC], out_offset=None,
            in_=bass.AP(ag1a_out, 0, [[TC, 8 * AG1R], [1, TC]]),
            in_offset=bass.IndirectOffsetOnAxis(
                ix_g[0:32, 16 + sh:17 + sh], 0)), 16)
    wv(g, lg)
    for sh in range(2):
        gdma(bc_seq.ap()[:, sh * TC:(sh + 1) * TC],
             bcs[0:32, sh * TC:(sh + 1) * TC])
    for sh in range(2):
        for c in range(4):
            wv(g, lg)
            lg.inc(g.indirect_dma_start(
                out=gstg[:, :], out_offset=None,
                in_=bass.AP(ag1a_out, 0, [[TC, 8 * AG1R], [1, TC]]),
                in_offset=bass.IndirectOffsetOnAxis(
                    ix_g[:, sh * 4 + c: sh * 4 + c + 1], 0)), 16)
            wv(g, lg)
            gdma(dloc.ap()[c * 128:(c + 1) * 128, sh * TC:(sh + 1) * TC],
                 gstg[:, :])
    for sh in range(2):
        for c in range(4):
            wv(g, lg)
            lg.inc(g.indirect_dma_start(
                out=gstgb[:, :], out_offset=None,
                in_=bass.AP(ag1b_out, 0, [[TC, 8 * H], [1, TC]]),
                in_offset=bass.IndirectOffsetOnAxis(
                    ix_g[:, 8 + sh * 4 + c: 8 + sh * 4 + c + 1], 0)), 16)
            wv(g, lg)
            gdma(dxloc.ap()[c * 128:(c + 1) * 128, sh * TC:(sh + 1) * TC],
                 gstgb[:, :])
    if "bcv" in dbg_t:
        wv(g, lg)
        gdma(dbg_t["bcv"].ap()[:, :], bc_seq.ap()[:, :])
    pro.close()
    drep = sb(sm, "drep", [128, 4 * TQ], BF16)
    dxrep = sb(sm, "dxrep", [128, 4 * TQ], BF16)
    brepq = sb(sm, "brepq", [128, 2 * TQ], BF16)
    crepq = sb(sm, "crepq", [128, 2 * TQ], BF16)
    dAt = sb(sm, "dAt", [128, 4 * TQ], F32)
    dBxt = sb(sm, "dBxt", [128, 4 * TQ], F32)
    Hsb = sb(sm, "Hsb", [128, TQ], BF16)
    prodb = sb(sm, "prodb", [128, 2 * TQ], BF16)
    ystg = sb(sm, "ystg", [128, TQ], F32)
    ps_y = [psum(sm, f"ps_y{i}", [128, 512]) for i in range(4)]

    wgs = sb(sm, "wgs", [128, 2 * 8 * II], BF16)
    wus = sb(sm, "wus", [128, 2 * 8 * II], BF16)
    wdmo = sb(sm, "wdmo", [128, 2 * 4 * H], BF16)
    wsdm = sb(sm, "wsdm", [128, 2 * 4 * 128], BF16)
    h_all = sb(sm, "h_all", [128, NE * 4 * TT], BF16)
    h_sh = sb(sm, "h_sh", [128, 4 * TT], BF16)
    combr = sb(sm, "combr", [128, 2 * TT], BF16)
    g_bf = sb(sm, "g_bf", [128, 2 * TT], BF16)
    u_bf = sb(sm, "u_bf", [128, 2 * TT], BF16)
    tnh = sb(sm, "tnh", [128, 2 * TT], BF16)
    h1t = sb(sm, "h1t", [128, 2 * TT], BF16)
    mstage = sb(sm, "mstage", [128, 2 * TT], F32)

    dA_done, dbx_done, scan_done, prod_done = {}, {}, {}, {}
    drep_ld, dxrep_ld, ymm_done, evac_done, yq_dma = {}, {}, {}, {}, {}
    brep_ld = {}
    ypend = []
    ypend_flush = {}

    def emit_ymm(item):
        u, par, kt = item
        cidx2 = u // 16
        pyA2, pyB2, chunk2, sh2, twp2 = ypend_flush[cidx2]
        wv(t, ve, prod_done[u])
        if kt == 0:
            bank_wait(pyA2, pyB2)
        t.matmul(pyA2[:, :], esel[:, kt * 128:(kt + 1) * 128],
                 bass.AP(prodb, par * TQ, [[2 * TQ, 128], [1, 512]]),
                 start=(kt == 0), stop=(kt == 15))
        mm = t.matmul(pyB2[:, :], esel[:, kt * 128:(kt + 1) * 128],
                      bass.AP(prodb, par * TQ + 512,
                              [[2 * TQ, 128], [1, 512]]),
                      start=(kt == 0), stop=(kt == 15))
        te.inc(mm)
        ymm_done[u] = te.n
        if kt == 15:
            del ypend_flush[cidx2]
            wv(s, te, ymm_done[u])
            wv(s, yst)
            s.activation(out=ystg[:, 0:512], in_=pyA2[:, :], func=AFT.Identity)
            s.activation(out=ystg[:, 512:TQ], in_=pyB2[:, :], func=AFT.Identity)
            bank_set(pyA2, se)
            bank_set(pyB2, se)
            evac_done[cidx2] = se.n
            wv(y, se, evac_done[cidx2])
            wv(y, yst)
            ydma(bass.AP(ag2_in, (sh2 * HOH + chunk2 * 128) * TC + twp2,
                         [[TC, 128], [1, TQ]]),
                 ystg[:, 0:TQ], yst)

    def emit_ssm():
        lg_pro = lg.n
        yield
        def unit_loads(u2):
            if u2 >= 256:
                return
            tq2, tile2 = u2 // 64, u2 % 64
            par2 = u2 % 4
            dsl2 = slice(par2 * TQ, (par2 + 1) * TQ)
            wv(y, se, dA_done.get(u2 - 4, 0))
            wv(y, ge, dbx_done.get(u2 - 4, 0))

            ydma(drep[:, dsl2],
                 bass.AP(dloc, (8 * tile2) * L + tq2 * TQ,
                         [[L, 8], [0, 16], [1, TQ]]), dld[par2])
            ydma(dxrep[:, dsl2],
                 bass.AP(dxloc, (8 * tile2) * L + tq2 * TQ,
                         [[L, 8], [0, 16], [1, TQ]]), dld[par2])

        wv(y, lg, lg_pro)
        for uu in range(3):
            unit_loads(uu)
        for tq in range(4):
            sh, twp = tq // 2, (tq % 2) * TQ
            qpar = tq % 2
            qsl = slice(qpar * TQ, (qpar + 1) * TQ)
            wv(y, ve, scan_done.get(("q", tq - 2), 0))
            wv(y, bcl)
            ydma(brepq[:, qsl], bass.AP(bc_seq, tq * TQ,
                                        [[0, 8], [L, 16], [1, TQ]]), bcl)
            ydma(crepq[:, qsl], bass.AP(bc_seq, 16 * L + tq * TQ,
                                        [[0, 8], [L, 16], [1, TQ]]), bcl)
            for chunk in range(4):
                pyA = ps_y[(chunk % 2) * 2]
                pyB = ps_y[(chunk % 2) * 2 + 1]
                ypend_flush[tq * 4 + chunk] = (pyA, pyB, chunk, sh, twp)
                for kt in range(16):
                    tile = chunk * 16 + kt
                    u = tq * 64 + tile
                    lpar = u % 4
                    par = u % 4
                    lsl = slice(lpar * TQ, (lpar + 1) * TQ)
                    dsl = slice(par * TQ, (par + 1) * TQ)
                    unit_loads(u + 3)
                    wv(s, dld[lpar])
                    wv(s, ve, scan_done.get(u - 4, 0))
                    s.activation(out=dAt[:, dsl], in_=drep[:, lsl],
                                 func=AFT.Exp, scale=negA[:, tile:tile + 1])
                    dA_done[u] = se.n
                    wv(g, dld[lpar])
                    wv(g, bcl)
                    wv(g, ve, scan_done.get(u - 4, 0))
                    gtt(dBxt[:, dsl], dxrep[:, lsl], brepq[:, qsl], OP.mult)
                    dbx_done[u] = ge.n
                    wv(v, se, dA_done[u])
                    wv(v, ge, dbx_done[u])
                    v.tensor_tensor_scan(out=Hsb[:, :], data0=dAt[:, dsl],
                                         data1=dBxt[:, dsl],
                                         initial=states[:, tile:tile + 1],
                                         op0=OP.mult, op1=OP.add)
                    v.tensor_copy(out=states[:, tile:tile + 1],
                                  in_=Hsb[:, TQ - 1:TQ])
                    ppar = u % 2
                    psl = slice(ppar * TQ, (ppar + 1) * TQ)
                    wv(v, te, ymm_done.get(u - 2, 0))
                    wv(v, bcl)
                    v.tensor_tensor(out=prodb[:, psl], in0=Hsb[:, :],
                                    in1=crepq[:, qsl], op=OP.mult)
                    scan_done[u] = ve.n
                    if tile == 63:
                        scan_done[("q", tq)] = ve.n
                    prod_done[u] = ve.n
                    ypend.append((u, u % 2, kt))
                    if len(ypend) > YMM_SKEW:
                        emit_ymm(ypend.pop(0))
                yield
            if tq == 1:
                while ypend:
                    emit_ymm(ypend.pop(0))
                wv(g, yst)
                cc.inc(g.collective_compute(
                    "AllGather", OP.bypass, replica_groups=RG8,
                    ins=[ag2_in.ap()[0:HOH, :]],
                    outs=[ag2_out.ap()[0:8 * HOH, :]]))
        while ypend:
            emit_ymm(ypend.pop(0))
        wv(g, yst)
        cc.inc(g.collective_compute(
            "AllGather", OP.bypass, replica_groups=RG8,
            ins=[ag2_in.ap()[HOH:2 * HOH, :]],
            outs=[ag2_out.ap()[8 * HOH:16 * HOH, :]]))
        if "yssm" in dbg_t:
            wv(g, cc)
            wv(g, ly)
            gdma(dbg_t["yssm"].ap()[:, :], ag2_in.ap()[:, :])
        yield

    gu_evac, h_ready, mo_evac, mo_dma, dn_done, hcons = {}, {}, {}, {}, {}, {}
    ld_w, hr_last = {}, {}
    gu_u = [0]
    gupend, dnpend = [], []
    combr_guard = {}

    def moe_w_load(tt, e):
        ex = tt * 9 + (e + 1)
        slot = ex % 2
        ydma(wgs[:, slot * 8 * II:(slot + 1) * 8 * II],
             bass.AP(wg_b if e >= 0 else swg_b, max(e, 0) * H * II,
                     [[II, 128], [128 * II, 8], [1, II]]), mwl[slot])
        ydma(wus[:, slot * 8 * II:(slot + 1) * 8 * II],
             bass.AP(wu_b if e >= 0 else swu_b, max(e, 0) * H * II,
                     [[II, 128], [128 * II, 8], [1, II]]), mwl[slot])
        ld_w[(tt, e)] = (slot, mwl[slot].n)

    def hr_last_for(tt2, e2):
        if e2 >= -1 and (tt2, e2) in hr_last:
            return hr_last[(tt2, e2)]
        if e2 < -1 and (tt2 - 1, e2 + 9) in hr_last:
            return hr_last[(tt2 - 1, e2 + 9)]
        return 0

    def flush_gu(item):
        tt, e, kc4, pg, pu, ug, mmv, crv, cpar = item
        parq = ug % 2
        csl = slice(parq * TT, (parq + 1) * TT)
        wv(s, te, mmv)
        wv(s, ve, h_ready.get(ug - 2, 0))
        s.activation(out=tnh[:, csl], in_=pg[:, :], func=AFT.Tanh, scale=0.5)
        s.activation(out=g_bf[:, csl], in_=pg[:, :], func=AFT.Identity)
        s.activation(out=u_bf[:, csl], in_=pu[:, :], func=AFT.Identity)
        bank_set(pg, se)
        bank_set(pu, se)
        gu_evac[((tt, e), kc4)] = se.n
        wv(v, se, se.n)
        v.scalar_tensor_tensor(out=h1t[:, csl], in0=tnh[:, csl],
                               scalar=1.0, in1=u_bf[:, csl],
                               op0=OP.add, op1=OP.mult)
        if e < 0:
            wv(v, te, dn_done.get((tt - 1, 7), 0))
            v.scalar_tensor_tensor(
                out=h_sh[:, kc4 * TT:(kc4 + 1) * TT],
                in0=h1t[:, csl], scalar=0.5, in1=g_bf[:, csl],
                op0=OP.mult, op1=OP.mult)
        else:
            wv(v, crls[cpar], crv)
            wv(v, te, dn_done.get((tt - 1, 7), 0))
            v.tensor_tensor(out=h1t[:, csl], in0=h1t[:, csl],
                            in1=g_bf[:, csl], op=OP.mult)
            v.tensor_tensor(
                out=h_all[:, (e * 4 + kc4) * TT:(e * 4 + kc4 + 1) * TT],
                in0=h1t[:, csl],
                in1=combr[:, cpar * TT:(cpar + 1) * TT], op=OP.mult)
            combr_guard[cpar] = ve.n
        h_ready[ug] = ve.n
        hr_last[(tt, e)] = ve.n

    def flush_dn(item):
        tt, mo, pd2 = item
        wv(s, te, dn_done[(tt, mo)])
        wv(s, mst)
        s.activation(out=mstage[:, (mo % 2) * TT:(mo % 2 + 1) * TT],
                     in_=pd2[:, :], func=AFT.Identity)
        bank_set(pd2, se)
        mo_evac[(tt, mo)] = se.n
        wv(y, se, se.n)
        wv(y, mst)
        ydma(moe_d.ap()[mo * 128:(mo + 1) * 128, tt * TT:(tt + 1) * TT],
             mstage[:, (mo % 2) * TT:(mo % 2 + 1) * TT], mst)

    def emit_moe():
        wv(y, lg, lg_precast)
        moe_w_load(0, -1)
        moe_w_load(0, 0)
        for tt in range(NTT):
            for e in range(-1, NE):
                slot = (tt * 9 + e + 1) % 2
                wsl_g = wgs[:, slot * 8 * II:(slot + 1) * 8 * II]
                wsl_u = wus[:, slot * 8 * II:(slot + 1) * 8 * II]
                cpar = (e + 1) % 2
                if e >= 0:
                    wv(y, ve, combr_guard.get(cpar, 0))
                    wv(y, lg, lg_comb)
                    ydma(combr[:, cpar * TT:(cpar + 1) * TT],
                         bass.AP(comb_d, e * TC + tt * TT, [[0, 128], [1, TT]]),
                         crls[cpar])
                    crv = crls[cpar].n
                else:
                    crv = 0
                for kc4 in range(4):
                    pg, pu = (ps_a, ps_b) if kc4 % 2 == 0 else (ps_c, ps_dd)
                    wv(t, mwl[ld_w[(tt, e)][0]], ld_w[(tt, e)][1])
                    bank_wait(pg, pu)
                    for kc in range(8):
                        t.matmul(pg[:, :],
                                 wsl_g[:, kc * II + kc4 * 128:
                                       kc * II + (kc4 + 1) * 128],
                                 xn_bf[:, kc * TC + tt * TT:
                                       kc * TC + (tt + 1) * TT],
                                 start=(kc == 0), stop=(kc == 7))
                    for kc in range(8):
                        mm = t.matmul(pu[:, :],
                                      wsl_u[:, kc * II + kc4 * 128:
                                            kc * II + (kc4 + 1) * 128],
                                      xn_bf[:, kc * TC + tt * TT:
                                            kc * TC + (tt + 1) * TT],
                                      start=(kc == 0), stop=(kc == 7))
                    te.inc(mm)
                    mmv = te.n
                    gu_u[0] += 1
                    ug = gu_u[0]
                    gupend.append((tt, e, kc4, pg, pu, ug, mmv, crv, cpar))
                    if len(gupend) > GU_SKEW:
                        flush_gu(gupend.pop(0))
                    mm_last = te.n
                    yield
                nxt_e, nxt_tt = e + 2, tt
                if nxt_e >= NE:
                    nxt_e -= (NE + 1)
                    nxt_tt += 1
                if nxt_tt < NTT:
                    wv(y, te, mm_last)
                    moe_w_load(nxt_tt, nxt_e)

            while gupend:
                flush_gu(gupend.pop(0))
            for mo in range(8):
                pd2 = ps_a if mo % 2 == 0 else ps_b
                slot = mo % 2
                wv(y, te, dn_done.get((tt, mo - 2), 0))
                ydma(wdmo[:, slot * 4 * H:(slot + 1) * 4 * H],
                     bass.AP(wdn_b, mo * 128,
                             [[H, 128], [II * H, 8], [128 * H, 4], [1, 128]]),
                     wdl[slot])
                ydma(wsdm[:, slot * 4 * 128:(slot + 1) * 4 * 128],
                     bass.AP(swd_b, mo * 128,
                             [[H, 128], [128 * H, 4], [1, 128]]), wdl[slot])
                wv(t, ve, hr_last[(tt, NE - 1)])
                wv(t, ve, hr_last[(tt, -1)])
                wv(t, wdl[slot])
                bank_wait(pd2)
                for e in range(NE):
                    for kc in range(4):
                        t.matmul(pd2[:, :],
                                 wdmo[:, slot * 4 * H + (e * 4 + kc) * 128:
                                       slot * 4 * H + (e * 4 + kc + 1) * 128],
                                 h_all[:, (e * 4 + kc) * TT:
                                       (e * 4 + kc + 1) * TT],
                                 start=(e == 0 and kc == 0), stop=False)
                for kc in range(4):
                    mm = t.matmul(pd2[:, :],
                                  wsdm[:, slot * 4 * 128 + kc * 128:
                                        slot * 4 * 128 + (kc + 1) * 128],
                                  h_sh[:, kc * TT:(kc + 1) * TT],
                                  start=False, stop=(kc == 3))
                te.inc(mm)
                dn_done[(tt, mo)] = te.n
                dnpend.append((tt, mo, pd2))
                if len(dnpend) > DN_SKEW:
                    flush_dn(dnpend.pop(0))
                yield
            while dnpend:
                flush_dn(dnpend.pop(0))
        if "moe" in dbg_t:
            wv(g, mst)
            wv(g, lg)
            gdma(dbg_t["moe"].ap()[:, :], moe_d.ap()[:, :])
        yield

    gen_s = emit_ssm()
    gen_m = emit_moe()
    alive_s, alive_m = True, True
    for _ in range(46):
        try:
            next(gen_m)
        except StopIteration:
            alive_m = False
            break
    next(gen_s)
    while alive_s or alive_m:
        if alive_s:
            try:
                next(gen_s)
            except StopIteration:
                alive_s = False
        if alive_m:
            for _ in range(8):
                try:
                    next(gen_m)
                except StopIteration:
                    alive_m = False
                    break
    barrier()
    sm.close()

    p4 = ExitStack()
    p4p = ExitStack()
    ps_s2 = psum(p4p, "ps_s2", [1, 512])
    ps_q2 = psum(p4p, "ps_q2", [1, 512])
    accf = sb(p4, "accf", [128, 8 * TC], F32)
    mu_b = sb(p4, "mu_b2", [128, TT], F32)
    r_b = sb(p4, "r_b2", [128, TT], F32)
    strow = sb(p4, "strow2", [1, 3 * TT], F32)
    p4a = ExitStack()
    ystage = sb(p4a, "ystage", [128, 2 * TT], F32)
    astage = sb(p4a, "astage", [128, 2 * TT], F32)
    ac_bf = sb(p4a, "ac_bf", [128, 8 * TT], BF16)
    aq_bf = sb(p4a, "aq_bf", [128, 8 * TT], BF16)

    yg_cons, hff_done, ots_done = {}, {}, {}
    w1ld, w2ld = {}, {}

    wv(g, cc, 4)
    ag2av = bass.AP(ag2_out, 0, [[TT, 8 * 2 * HOH * 4], [1, TT]])
    for tt in range(NTT):
        for hc in range(8):
            uy = tt * 8 + hc
            ypar = hc % 2
            wv(g, ve, yg_cons.get(uy - 2, 0))
            ygls[ypar].inc(g.indirect_dma_start(
                out=ystage[:, ypar * TT:(ypar + 1) * TT], out_offset=None,
                in_=ag2av,
                in_offset=bass.IndirectOffsetOnAxis(ix_y[:, hc:hc + 1], 0),
                element_offset=tt * TT), 16)
            wv(y, ve, yg_cons.get(uy - 2, 0))
            ydma(astage[:, ypar * TT:(ypar + 1) * TT],
                 accin_d.ap()[hc * 128:(hc + 1) * 128, tt * TT:(tt + 1) * TT],
                 a2ls[ypar])
            wv(v, ygls[ypar])
            wv(v, a2ls[ypar])
            v.tensor_tensor(
                out=accf[:, hc * TC + tt * TT: hc * TC + (tt + 1) * TT],
                in0=astage[:, ypar * TT:(ypar + 1) * TT],
                in1=ystage[:, ypar * TT:(ypar + 1) * TT], op=OP.add)
            yg_cons[uy] = ve.n
    if "accf" in dbg_t:
        wv(y, ve)
        for hc in range(8):
            ydma(dbg_t["accf"].ap()[hc * 128:(hc + 1) * 128, :],
                 accf[:, hc * TC:(hc + 1) * TC])
    for tt in range(NTT):
        wv(s, ve)
        for hc in range(8):
            s.activation(out=aq_bf[:, hc * TT:(hc + 1) * TT],
                         in_=accf[:, hc * TC + tt * TT: hc * TC + (tt + 1) * TT],
                         func=AFT.Square)
            s.activation(out=ac_bf[:, hc * TT:(hc + 1) * TT],
                         in_=accf[:, hc * TC + tt * TT: hc * TC + (tt + 1) * TT],
                         func=AFT.Identity)
        ln_stats(tt, ac_bf, aq_bf, 4, ps_s2, ps_q2, strow, mu_b, r_b,
                 store=True)
    barrier()
    p4a.close()
    nstage = sb(p4, "nstage", [128, 8 * TT], BF16)
    hff = sb(p4, "hff", [128, 32 * TT], BF16)
    w1s = sb(p4, "w1s", [128, 4 * 8 * 128], BF16)
    w2s = sb(p4, "w2s", [128, 2 * 32 * 128], BF16)
    xts = sb(p4, "xts", [128, 2 * TT], F32)
    mts = sb(p4, "mts", [128, 2 * TT], F32)
    ots = sb(p4, "ots", [128, 2 * TT], F32)
    astage = sb(p4, "astage2", [128, TT], F32)
    aplybuf = sb(p4, "aplybuf", [128, 8 * TT], F32)

    def w1_load(mf):
        slot = mf % 4
        sdma(w1s[:, slot * 8 * 128:(slot + 1) * 8 * 128],
             bass.AP(w1_b, mf * 128, [[FF, 128], [128 * FF, 8], [1, 128]]),
             w1l[slot])
        w1ld[mf] = (slot, w1l[slot].n)

    def w2_load(mo):
        slot = mo % 2
        ydma(w2s[:, slot * 32 * 128:(slot + 1) * 32 * 128],
             bass.AP(w2_b, mo * 128, [[H, 128], [128 * H, 32], [1, 128]]),
             w2l[slot])
        w2ld[mo] = (slot, w2l[slot].n)

    for mf in range(4):
        w1_load(mf)
    w2_load(0)
    w2_load(1)
    for tt in range(NTT):
        wv(y, stl)
        wv(y, ve)
        ydma(mu_b[:], bass.AP(stats_d, 4 * TC + tt * TT, [[0, 128], [1, TT]]),
             stl)
        ydma(r_b[:], bass.AP(stats_d, 5 * TC + tt * TT, [[0, 128], [1, TT]]),
             stl)
        wv(s, te)
        wv(v, se)
        wv(v, stl)
        v.tensor_tensor(out=aplybuf[:],
                        in0=bass.AP(accf, tt * TT,
                                    [[8 * TC, 128], [TC, 8], [1, TT]]),
                        in1=bass.AP(mu_b, 0, [[512, 128], [0, 8], [1, 512]]),
                        op=OP.subtract)
        v.tensor_tensor(out=aplybuf[:], in0=aplybuf[:],
                        in1=bass.AP(r_b, 0, [[512, 128], [0, 8], [1, 512]]),
                        op=OP.mult)
        wv(s, ve)
        for hc in range(8):
            cs = slice(hc * TT, (hc + 1) * TT)
            s.activation(out=nstage[:, cs], in_=aplybuf[:, cs],
                         func=AFT.Identity, scale=p_ln2g[:, hc:hc + 1],
                         bias=p_ln2b[:, hc:hc + 1])
        nsv = se.n
        for mf in range(32):
            pd3 = (ps_a, ps_b, ps_c, ps_dd)[mf % 4]
            slot = mf % 4
            wsl = w1s[:, slot * 8 * 128:(slot + 1) * 8 * 128]
            wv(t, w1l[w1ld[mf][0]], w1ld[mf][1])
            wv(t, se, nsv)
            bank_wait(pd3)
            for kc in range(8):
                mm = t.matmul(pd3[:, :], wsl[:, kc * 128:(kc + 1) * 128],
                              nstage[:, kc * TT:(kc + 1) * TT],
                              start=(kc == 0), stop=(kc == 7))
            te.inc(mm)
            wv(s, te, te.n)
            s.activation(out=hff[:, mf * TT:(mf + 1) * TT], in_=pd3[:, :],
                         func=AFT.Gelu, bias=p_b1[:, mf:mf + 1])
            bank_set(pd3, se)
            hff_done[mf] = se.n
            wv(s0, te, te.n)
            nxt = mf + 4
            if nxt < 32:
                w1_load(nxt)
            elif tt + 1 < NTT:
                w1_load(nxt - 32)
        for mo in range(8):
            pd4 = (ps_a, ps_b, ps_c, ps_dd)[mo % 4]
            slot = mo % 2
            wsl2 = w2s[:, slot * 32 * 128:(slot + 1) * 32 * 128]
            wv(t, w2l[w2ld[mo][0]], w2ld[mo][1])
            wv(t, se, hff_done[31])
            bank_wait(pd4)
            for kc in range(32):
                mm = t.matmul(pd4[:, :], wsl2[:, kc * 128:(kc + 1) * 128],
                              hff[:, kc * TT:(kc + 1) * TT],
                              start=(kc == 0), stop=(kc == 31))
            te.inc(mm)
            w2v = te.n
            opar = mo % 2
            osl = slice(opar * TT, (opar + 1) * TT)
            wv(y, ve, ots_done.get(mo - 2, 0))
            ydma(xts[:, osl], bass.AP(x_fm, mo * 128 * TC + tt * TT,
                                      [[TC, 128], [1, TT]]), xmls[opar])
            ydma(mts[:, osl],
                 moe_d.ap()[mo * 128:(mo + 1) * 128, tt * TT:(tt + 1) * TT],
                 xmls[opar])
            wv(v, te, w2v)
            wv(v, xmls[opar])
            wv(v, so, so.n)
            v.scalar_tensor_tensor(
                out=ots[:, osl], in0=pd4[:, :], scalar=p_b2[:, mo:mo + 1],
                in1=accf[:, mo * TC + tt * TT: mo * TC + (tt + 1) * TT],
                op0=OP.add, op1=OP.add)
            bank_set(pd4, ve)
            if "attno" in dbg_t:
                wv(y, ve)
                ydma(dbg_t["attno"].ap()[mo * 128:(mo + 1) * 128,
                                         tt * TT:(tt + 1) * TT], ots[:, osl])
                wv(v, ly)
            v.tensor_tensor(out=ots[:, osl], in0=ots[:, osl], in1=xts[:, osl],
                            op=OP.add)
            v.tensor_tensor(out=ots[:, osl], in0=ots[:, osl], in1=mts[:, osl],
                            op=OP.add)
            ots_done[mo] = ve.n
            wv(y, ve, ve.n)
            so.inc(y.dma_start(out=out_t.ap()[mo * 128:(mo + 1) * 128,
                                              tt * TT:(tt + 1) * TT],
                               in_=ots[:, osl]), 16)
            wv(y, te, w2v)
            if mo + 2 < 8:
                w2_load(mo + 2)
            elif tt + 1 < NTT:
                w2_load(mo + 2 - 8)
        ots_done.clear()
        hff_done.clear()
    y.wait_ge(so.h, so.n)
    barrier()
    p4.close()
    p4p.close()
    gst.close()
    return nc



def _prep_core(inputs, k):
    q = k % 2
    b = k // 2
    perm = np.arange(H) if q == 0 else np.r_[np.arange(512, 1024), np.arange(512)]
    asc = np.ascontiguousarray
    f32 = np.float32

    x = np.asarray(inputs["L0_attn_x"])
    xk = x[b, q * TC:(q + 1) * TC, :]
    d = {"x_fm": asc(xk.T[perm])}

    Wd = np.asarray(inputs["L0_attn_Wd"])
    d["wd_t"] = asc(Wd[np.ix_(perm, perm)].T)
    d["wbc_t"] = asc(np.asarray(inputs["L0_attn_Wbc"])[:, perm].T)
    d["w1_t"] = asc(np.asarray(inputs["L0_attn_W1"])[:, perm].T)
    d["w2_t"] = asc(np.asarray(inputs["L0_attn_W2"])[perm].T)
    d["rw_t"] = asc(np.asarray(inputs["L0_ffn_router_weight"])[:, perm].T)
    d["wg_e"] = asc(np.asarray(inputs["L0_ffn_w_gate"])[:, perm, :])
    d["wu_e"] = asc(np.asarray(inputs["L0_ffn_w_up"])[:, perm, :])
    d["wdn_e"] = asc(np.asarray(inputs["L0_ffn_w_down"])[:, :, perm])
    d["swg"] = asc(np.asarray(inputs["L0_ffn_sw_gate"])[perm])
    d["swu"] = asc(np.asarray(inputs["L0_ffn_sw_up"])[perm])
    d["swd"] = asc(np.asarray(inputs["L0_ffn_sw_down"])[:, perm])

    def col8(vv):
        return asc(np.asarray(vv)[perm].reshape(8, 128).T.astype(f32))

    d["ln1_g"] = col8(inputs["L0_attn_ln1_g"])
    d["ln1_b"] = col8(inputs["L0_attn_ln1_b"])
    d["ln2_g"] = col8(inputs["L0_attn_ln2_g"])
    d["ln2_b"] = col8(inputs["L0_attn_ln2_b"])
    d["bd_c"] = col8(inputs["L0_attn_bd"])
    d["b2_c"] = col8(inputs["L0_attn_b2"])
    d["d_c"] = col8(inputs["L0_attn_D"])
    d["b1_c"] = asc(np.asarray(inputs["L0_attn_b1"]).reshape(32, 128).T
                    .astype(f32))
    d["bbc_c"] = asc(np.asarray(inputs["L0_attn_bbc"]).reshape(32, 1)
                     .astype(f32))
    alog = np.asarray(inputs["L0_attn_A_log"])[perm[:HOH]]
    a2 = np.empty((128, 64), f32)
    for tile in range(64):
        for hl in range(8):
            for n in range(NS):
                a2[hl * NS + n, tile] = alog[tile * 8 + hl, n]
    d["alog2"] = asc(a2)
    es = np.zeros((128, 16 * 128), f32)
    for ktile in range(16):
        for p in range(128):
            es[p, ktile * 128 + 8 * ktile + p // NS] = 1.0
    d["esel_i"] = asc(es)
    cb = np.asarray(inputs["L0_ffn_correction_bias"]).astype(f32)
    d["cbt_i"] = asc(np.tile(np.tile(cb, 64), (32, 1)))

    gi = np.zeros((128, 20), np.uint32)
    for sh in range(2):
        src = 2 * b + sh
        for c in range(4):
            p = np.arange(128)
            i = c * 128 + p
            gl = q * 512 + i
            pos = gl if sh == 0 else (gl + 512) % H
            gi[:, sh * 4 + c] = src * AG1R + pos
            gi[:, 8 + sh * 4 + c] = src * H + pos
        gi[0:32, 16 + sh] = src * AG1R + H + np.arange(32)
    d["gidx"] = gi
    yi = np.zeros((128, 8), np.uint32)
    for hc in range(8):
        p = np.arange(128)
        i = hc * 128 + p
        row = np.where(i < 512,
                       q * 8 * HOH + k * HOH + i,
                       q * 8 * HOH + (k ^ 1) * HOH + (i - 512))
        yi[:, hc] = row * 4
    d["yidx"] = yi
    d = {kk: (vv if vv.dtype == np.uint32 else vv.astype(np.float32))
         for kk, vv in d.items()}
    return d, perm


LAST_RESULTS = None


def kernel(**inputs):
    global LAST_RESULTS
    in_maps = []
    perms = []
    for k in range(8):
        m, perm = _prep_core(inputs, k)
        in_maps.append(m)
        perms.append(perm)
    nc = build_program(dbg=KDBG)
    res = run_bass_kernel_spmd(nc, in_maps, core_ids=list(range(8)))
    LAST_RESULTS = res.results
    out = np.empty((4, L, H), np.float32)
    for k in range(8):
        q, b = k % 2, k // 2
        o_fm = res.results[k]["out"]
        o_nat = np.empty_like(o_fm)
        o_nat[perms[k]] = o_fm
        out[b, q * TC:(q + 1) * TC, :] = o_nat.T
    return out


if __name__ == "__main__":
    nc = build_program()
    print("built OK")


# version 12
# speedup vs baseline: 1.0887x; 1.0314x over previous
import os
from contextlib import ExitStack

import numpy as np
import ml_dtypes
import concourse.bass as bass
import concourse.mybir as mybir
from concourse.bass_utils import run_bass_kernel_spmd
from concourse.alu_op_type import AluOpType as OP

F32 = mybir.dt.float32
BF16 = mybir.dt.bfloat16
U32 = mybir.dt.uint32
AFT = mybir.ActivationFunctionType
AXX = mybir.AxisListType.X

H = 1024
NS = 16
FF = 4096
II = 512
NE = 8
TC = 2048
TT = 512
NTT = TC // TT
L = 4096
EPS = 1e-5
HOH = 512
AG1R = H + 32
TQ = 1024
BIG_NEG = -1e30
RG8 = [[0, 1, 2, 3, 4, 5, 6, 7]]

KDBG = ()
RACECHECK = bool(os.environ.get("KERNEL_RACECHECK"))
GU_SKEW = 1
DN_SKEW = 1
YMM_SKEW = 1


class Sem:
    def __init__(self, st, nc, name):
        self.h = st.enter_context(nc.semaphore(name))
        self.n = 0

    def inc(self, inst, k=1):
        inst.then_inc(self.h, k)
        self.n += k
        return inst


def _ap_tname(x):
    try:
        t = x.tensor
        return getattr(t, "name", None) or str(id(t))
    except Exception:
        return None


class EW:
    OPS = ("tensor_tensor", "tensor_scalar", "scalar_tensor_tensor",
           "tensor_copy", "tensor_reduce", "reciprocal", "memset",
           "transpose", "tensor_tensor_scan", "activation", "iota")
    RKEYS = ("in_", "in0", "in1", "data0", "data1", "initial", "scalar",
             "scalar1", "scalar2", "bias", "scale", "mask")

    def __init__(self, eng, sem, waited):
        self._e = eng
        self._s = sem
        self._w = waited
        self._tw = {}
        self._tr = {}

    def __getattr__(self, nm):
        f = getattr(self._e, nm)
        if nm in EW.OPS:
            def wrapped(*a, **k):
                if RACECHECK:
                    key = (id(self._e), id(self._s))
                    if self._w.get(key, -1) < self._s.n:
                        self._e.wait_ge(self._s.h, self._s.n)
                        self._w[key] = self._s.n
                else:
                    wt = k.get("out")
                    if wt is None and nm == "memset" and a:
                        wt = a[0]
                    rd = [k[r] for r in EW.RKEYS if r in k]
                    need = 0
                    wtn = _ap_tname(wt) if wt is not None else None
                    for r in rd:
                        t2 = _ap_tname(r)
                        if t2 is not None:
                            need = max(need, self._tw.get(t2, 0))
                    if wtn is not None:
                        need = max(need, self._tw.get(wtn, 0),
                                   self._tr.get(wtn, 0))
                    key = (id(self._e), id(self._s))
                    if need > self._w.get(key, -1):
                        self._e.wait_ge(self._s.h, need)
                        self._w[key] = need
                r2 = self._s.inc(f(*a, **k))
                if not RACECHECK:
                    if wtn is not None:
                        self._tw[wtn] = self._s.n
                    for r in rd:
                        t2 = _ap_tname(r)
                        if t2 is not None:
                            self._tr[t2] = self._s.n
                return r2
            return wrapped
        return f


def build_program(dbg=()):
    nc = bass.Bass("TRN2", target_bir_lowering=False,
                   detect_race_conditions=RACECHECK)
    t, v0, s0, y, g = nc.tensor, nc.vector, nc.scalar, nc.sync, nc.gpsimd

    def din(name, shape, dt=F32):
        return nc.dram_tensor(name, shape, dt, kind="ExternalInput")

    x_fm = din("x_fm", [H, TC])
    wd_t = din("wd_t", [H, H])
    wbc_t = din("wbc_t", [H, 32])
    w1_t = din("w1_t", [H, FF])
    w2_t = din("w2_t", [FF, H])
    rw_t = din("rw_t", [H, NE])
    wg_e = din("wg_e", [NE, H, II])
    wu_e = din("wu_e", [NE, H, II])
    wdn_e = din("wdn_e", [NE, II, H])
    swg = din("swg", [H, II])
    swu = din("swu", [H, II])
    swd = din("swd", [II, H])
    ln1_g = din("ln1_g", [128, 8])
    ln1_b = din("ln1_b", [128, 8])
    ln2_g = din("ln2_g", [128, 8])
    ln2_b = din("ln2_b", [128, 8])
    bd_c = din("bd_c", [128, 8])
    b2_c = din("b2_c", [128, 8])
    d_c = din("d_c", [128, 8])
    b1_c = din("b1_c", [128, 32])
    bbc_c = din("bbc_c", [32, 1])
    alog2 = din("alog2", [128, 64])
    esel_i = din("esel_i", [128, 16 * 128])
    cbt_i = din("cbt_i", [32, 512])
    gidx = din("gidx", [128, 20], U32)
    yidx = din("yidx", [128, 8], U32)

    out_t = nc.dram_tensor("out", [H, TC], F32, kind="ExternalOutput")

    ag1a_in = nc.dram_tensor("ag1a_in", [AG1R, TC], F32, kind="Internal")
    ag1a_out = nc.dram_tensor("ag1a_out", [8 * AG1R, TC], F32, kind="Internal")
    ag1b_in = nc.dram_tensor("ag1b_in", [H, TC], BF16, kind="Internal")
    ag1b_out = nc.dram_tensor("ag1b_out", [8 * H, TC], BF16, kind="Internal")
    ag2_in = nc.dram_tensor("ag2_in", [2 * HOH, TC], F32, kind="Internal")
    ag2_out = nc.dram_tensor("ag2_out", [8 * 2 * HOH, TC], F32, kind="Internal")
    dloc = nc.dram_tensor("dloc", [HOH, L], BF16, kind="Internal")
    dxloc = nc.dram_tensor("dxloc", [HOH, L], BF16, kind="Internal")
    accin_d = nc.dram_tensor("accin_d", [H, TC], F32, kind="Internal")
    moe_d = nc.dram_tensor("moe_d", [H, TC], F32, kind="Internal")
    bc_seq = nc.dram_tensor("bc_seq", [32, L], BF16, kind="Internal")
    stats_d = nc.dram_tensor("stats_d", [8, TC], F32, kind="Internal")
    comb_d = nc.dram_tensor("comb_d", [NE, TC], BF16, kind="Internal")
    w1_b = nc.dram_tensor("w1_b", [H, FF], BF16, kind="Internal")
    w2_b = nc.dram_tensor("w2_b", [FF, H], BF16, kind="Internal")
    wg_b = nc.dram_tensor("wg_b", [NE, H, II], BF16, kind="Internal")
    wu_b = nc.dram_tensor("wu_b", [NE, H, II], BF16, kind="Internal")
    wdn_b = nc.dram_tensor("wdn_b", [NE, II, H], BF16, kind="Internal")
    swg_b = nc.dram_tensor("swg_b", [H, II], BF16, kind="Internal")
    swu_b = nc.dram_tensor("swu_b", [H, II], BF16, kind="Internal")
    swd_b = nc.dram_tensor("swd_b", [II, H], BF16, kind="Internal")

    dbg_t = {}
    for nm in dbg:
        shp, dt = {
            "xn_bf": ([H, TC], BF16), "delta": ([H, TC], F32),
            "bcv": ([32, L], BF16),
            "yssm": ([2 * HOH, TC], F32), "accin": ([H, TC], F32),
            "logits": ([8, TC], F32), "comb": ([NE, TC], BF16),
            "moe": ([H, TC], F32), "accf": ([H, TC], F32),
            "normed": ([H, TC], BF16), "attno": ([H, TC], F32),
        }[nm]
        dbg_t[nm] = nc.dram_tensor("dbg_" + nm, shp, dt, kind="ExternalOutput")

    gst = ExitStack()

    def sb(stk, name, shape, dt):
        return stk.enter_context(nc.sbuf_tensor(name, shape, dt))

    def psum(stk, name, shape):
        return stk.enter_context(nc.psum_tensor(name, shape, F32))

    sems = {n: Sem(gst, nc, "sem_" + n)
            for n in ("ly", "ls", "lg", "te", "ve", "se", "ge", "cc", "so",
                      "xw", "stl", "agst", "bcl", "yst", "crl", "mst",
                      "a2l", "xml",
                      "dld0", "dld1", "dld2", "dld3", "mwl0", "mwl1", "wdl0", "wdl1",
                      "w1l0", "w1l1", "w1l2", "w1l3", "w2l0", "w2l1",
                      "crl0", "crl1", "a2l0", "a2l1", "ygl0", "ygl1",
                      "xml0", "xml1", "dsg0", "dsg1", "acg0", "acg1", "bcg")}
    ly, ls, lg, te, ve, se, ge, cc, so = (
        sems[n] for n in ("ly", "ls", "lg", "te", "ve", "se", "ge", "cc", "so"))
    xw, stl, agst, bcl, yst, crl, mst, a2l, xml = (
        sems[n] for n in ("xw", "stl", "agst", "bcl", "yst", "crl", "mst",
                          "a2l", "xml"))
    dld = [sems["dld0"], sems["dld1"], sems["dld2"], sems["dld3"]]
    crls = [sems["crl0"], sems["crl1"]]
    a2ls = [sems["a2l0"], sems["a2l1"]]
    ygls = [sems["ygl0"], sems["ygl1"]]
    xmls = [sems["xml0"], sems["xml1"]]
    dsg = [sems["dsg0"], sems["dsg1"]]
    acg = [sems["acg0"], sems["acg1"]]
    bcg = sems["bcg"]
    mwl = [sems["mwl0"], sems["mwl1"]]
    wdl = [sems["wdl0"], sems["wdl1"]]
    w1l = [sems["w1l0"], sems["w1l1"], sems["w1l2"], sems["w1l3"]]
    w2l = [sems["w2l0"], sems["w2l1"]]
    waited = {}
    v = EW(v0, ve, waited)
    s = EW(s0, se, waited)

    def wv(eng, sem, val=None):
        val = sem.n if val is None else val
        e0 = eng._e if isinstance(eng, EW) else eng
        key = (id(e0), id(sem))
        if waited.get(key, -1) >= val:
            return
        e0.wait_ge(sem.h, val)
        waited[key] = val

    def ydma(dst, src, sem=None):
        (sem or ly).inc(y.dma_start(out=dst, in_=src), 16)

    def sdma(dst, src, sem=None):
        (sem or ls).inc(s0.dma_start(out=dst, in_=src), 16)

    def gdma(dst, src, sem=None):
        (sem or lg).inc(g.dma_start(out=dst, in_=src), 16)

    g_tw, g_tr = {}, {}

    def gtt(out, in0, in1, op):
        key = (id(g), id(ge))
        if RACECHECK:
            if waited.get(key, -1) < ge.n:
                g.wait_ge(ge.h, ge.n)
                waited[key] = ge.n
        else:
            need = 0
            wtn = _ap_tname(out)
            for r in (in0, in1):
                t2 = _ap_tname(r)
                if t2 is not None:
                    need = max(need, g_tw.get(t2, 0))
            need = max(need, g_tw.get(wtn, 0), g_tr.get(wtn, 0))
            if need > waited.get(key, -1):
                g.wait_ge(ge.h, need)
                waited[key] = need
        r2 = ge.inc(g.tensor_tensor(out=out, in0=in0, in1=in1, op=op))
        if not RACECHECK:
            g_tw[wtn] = ge.n
            for r in (in0, in1):
                t2 = _ap_tname(r)
                if t2 is not None:
                    g_tr[t2] = ge.n
        return r2

    bank_guard = {}

    def bank_wait(*banks):
        for bk in banks:
            for sem, val in bank_guard.get(id(bk), ()):
                wv(t, sem, val)

    def bank_set(bank, sem):
        bank_guard[id(bank)] = ((sem, sem.n),)

    def barrier():
        for eng in (t, v, s, y, g):
            for sem in sems.values():
                if sem.n and sem is not so and sem is not cc:
                    wv(eng, sem)

    ones_bf = sb(gst, "ones_bf", [128, 1], BF16)
    ones_f = sb(gst, "ones_f", [1, 128], F32)
    xn_bf = sb(gst, "xn_bf", [128, 8 * TC], BF16)
    negA = sb(gst, "negA", [128, 64], F32)
    esel = sb(gst, "esel", [128, 16 * 128], BF16)
    states = sb(gst, "states", [128, 64], F32)
    p_ln1g = sb(gst, "p_ln1g", [128, 8], F32)
    p_ln1b = sb(gst, "p_ln1b", [128, 8], F32)
    p_ln2g = sb(gst, "p_ln2g", [128, 8], F32)
    p_ln2b = sb(gst, "p_ln2b", [128, 8], F32)
    p_bd = sb(gst, "p_bd", [128, 8], F32)
    p_b2 = sb(gst, "p_b2", [128, 8], F32)
    p_d = sb(gst, "p_d", [128, 8], F32)
    p_b1 = sb(gst, "p_b1", [128, 32], F32)
    p_bbc = sb(gst, "p_bbc", [32, 1], F32)
    p_cbt = sb(gst, "p_cbt", [32, 512], F32)
    ix_g = sb(gst, "ix_g", [128, 20], U32)
    ix_y = sb(gst, "ix_y", [128, 8], U32)
    alog_sb = sb(gst, "alog_sb", [128, 64], F32)

    ps_a = psum(gst, "ps_a", [128, 512])
    ps_b = psum(gst, "ps_b", [128, 512])
    ps_c = psum(gst, "ps_c", [128, 512])
    ps_dd = psum(gst, "ps_d", [128, 512])

    for dst, src in ((p_ln1g, ln1_g), (p_ln1b, ln1_b), (p_ln2g, ln2_g),
                     (p_ln2b, ln2_b), (p_bd, bd_c), (p_b2, b2_c), (p_d, d_c),
                     (p_b1, b1_c), (p_bbc, bbc_c), (p_cbt, cbt_i),
                     (ix_g, gidx), (ix_y, yidx), (alog_sb, alog2)):
        ydma(dst[:], src.ap())
    gdma(esel[:], esel_i.ap())
    v.memset(ones_bf[:], 1.0)
    v.memset(ones_f[0:1, :], 1.0)
    v.memset(states[:], 0.0)
    wv(s, ly)
    s.activation(out=negA[:], in_=alog_sb[:], func=AFT.Exp)
    wv(v, se)
    v.tensor_scalar(out=negA[:], in0=negA[:], scalar1=-1.0, scalar2=None,
                    op0=OP.mult)

    fer = ExitStack()
    logits32 = sb(fer, "logits32", [32, TC], F32)
    fe = ExitStack()
    v.memset(logits32[:], 0.0)
    fep = ExitStack()
    ps_s = psum(fep, "ps_s", [1, 512])
    ps_q = psum(fep, "ps_q", [1, 512])
    ps_bc = psum(fep, "ps_bc", [32, 512])
    ps_rt = psum(fep, "ps_rt", [8, 512])

    mu_b = sb(fe, "mu_b", [128, TT], F32)
    r_b = sb(fe, "r_b", [128, TT], F32)
    strow = sb(fe, "strow", [1, 3 * TT], F32)
    wdT = sb(fe, "wdT", [128, 8 * H], BF16)
    wbcT = sb(fe, "wbcT", [128, 8 * 32], BF16)
    rwT = sb(fe, "rwT", [128, 8 * NE], BF16)
    for kc in range(8):
        gdma(wdT[:, kc * H:(kc + 1) * H], wd_t.ap()[kc * 128:(kc + 1) * 128, :])
        gdma(wbcT[:, kc * 32:(kc + 1) * 32], wbc_t.ap()[kc * 128:(kc + 1) * 128, :])
        gdma(rwT[:, kc * NE:(kc + 1) * NE], rw_t.ap()[kc * 128:(kc + 1) * 128, :])
    for db, df in ((wg_b, wg_e), (wu_b, wu_e), (wdn_b, wdn_e), (swg_b, swg),
                   (swu_b, swu), (swd_b, swd), (w1_b, w1_t), (w2_b, w2_t)):
        gdma(db.ap(), df.ap())
    lg_precast = lg.n

    xt = sb(fe, "xt", [128, 8 * TT], F32)
    xb_bf = sb(fe, "xb_bf", [128, 8 * TT], BF16)
    xq_bf = sb(fe, "xq_bf", [128, 8 * TT], BF16)
    xn_f = sb(fe, "xn_f", [128, 8 * TT], F32)
    xn2_f = sb(fe, "xn2_f", [128, 8 * TT], F32)
    xn2b = sb(fe, "xn2b", [128, 8 * TT], BF16)
    dstage = sb(fe, "dstage", [128, 2 * TT], F32)
    dxstage = sb(fe, "dxstage", [128, 2 * TT], BF16)
    bcst = sb(fe, "bcst", [32, TT], F32)
    acst = sb(fe, "acst", [128, 2 * TT], F32)

    def ln_stats(tt, src_bf, sq_bf, row, pss, psq, strow, mu_b, r_b,
                 store=False):
        wv(t, ve)
        wv(t, se)
        wv(v, stl)
        bank_wait(pss, psq)
        for hc in range(8):
            t.matmul(pss[:, :], ones_bf[:], src_bf[:, hc * TT:(hc + 1) * TT],
                     start=(hc == 0), stop=(hc == 7))
        for hc in range(8):
            mm = t.matmul(psq[:, :], ones_bf[:], sq_bf[:, hc * TT:(hc + 1) * TT],
                          start=(hc == 0), stop=(hc == 7))
        te.inc(mm)
        wv(v, te)
        v.tensor_scalar(out=strow[0:1, 0:TT], in0=pss[0:1, :],
                        scalar1=1.0 / H, scalar2=None, op0=OP.mult)
        v.tensor_scalar(out=strow[0:1, TT:2 * TT], in0=psq[0:1, :],
                        scalar1=1.0 / H, scalar2=None, op0=OP.mult)
        bank_set(pss, ve)
        bank_set(psq, ve)
        v.tensor_tensor(out=strow[0:1, 2 * TT:3 * TT], in0=strow[0:1, 0:TT],
                        in1=strow[0:1, 0:TT], op=OP.mult)
        v.tensor_tensor(out=strow[0:1, 2 * TT:3 * TT],
                        in0=strow[0:1, 2 * TT:3 * TT],
                        in1=strow[0:1, TT:2 * TT], op=OP.subtract)
        v.tensor_scalar(out=strow[0:1, 2 * TT:3 * TT],
                        in0=strow[0:1, 2 * TT:3 * TT], scalar1=-1.0,
                        scalar2=EPS, op0=OP.mult, op1=OP.add)
        wv(s, ve)
        s.activation(out=strow[0:1, TT:2 * TT], in_=strow[0:1, 2 * TT:3 * TT],
                     func=AFT.Ln)
        s.activation(out=strow[0:1, TT:2 * TT], in_=strow[0:1, TT:2 * TT],
                     func=AFT.Exp, scale=-0.5)
        if store:
            wv(y, se)
            wv(y, ve)
            ydma(stats_d.ap()[row:row + 1, tt * TT:(tt + 1) * TT],
                 strow[0:1, 0:TT], stl)
            ydma(stats_d.ap()[row + 1:row + 2, tt * TT:(tt + 1) * TT],
                 strow[0:1, TT:2 * TT], stl)
            return
        bank_wait(ps_c, ps_dd)
        wv(t, se)
        t.matmul(ps_c[:, :], ones_f[0:1, :], strow[0:1, 0:TT],
                 start=True, stop=True)
        mmb = t.matmul(ps_dd[:, :], ones_f[0:1, :], strow[0:1, TT:2 * TT],
                       start=True, stop=True)
        te.inc(mmb)
        wv(s, te)
        s.activation(out=mu_b[:], in_=ps_c[:, :], func=AFT.Identity)
        s.activation(out=r_b[:], in_=ps_dd[:, :], func=AFT.Identity)
        bank_set(ps_c, se)
        bank_set(ps_dd, se)

    for tt in range(NTT):
        wv(y, ve)
        wv(y, se)
        ydma(xt[:], bass.AP(x_fm, tt * TT, [[TC, 128], [128 * TC, 8], [1, TT]]),
             xw)
        wv(s, xw)
        s.activation(out=xq_bf[:], in_=xt[:], func=AFT.Square)
        wv(v, xw)
        v.tensor_copy(out=xb_bf[:], in_=xt[:])
        ln_stats(tt, xb_bf, xq_bf, 0, ps_s, ps_q, strow, mu_b, r_b)
        wv(v, se)
        v.tensor_tensor(out=xn_f[:], in0=xt[:],
                        in1=bass.AP(mu_b, 0, [[512, 128], [0, 8], [1, 512]]),
                        op=OP.subtract)
        v.tensor_tensor(out=xn_f[:], in0=xn_f[:],
                        in1=bass.AP(r_b, 0, [[512, 128], [0, 8], [1, 512]]),
                        op=OP.mult)
        v.tensor_copy(
            out=bass.AP(xn_bf, tt * TT, [[8 * TC, 128], [TC, 8], [1, TT]]),
            in_=xn_f[:])
        wv(s, ve)
        s.activation(out=xq_bf[:], in_=xn_f[:], func=AFT.Square)
        s.activation(out=xb_bf[:], in_=xn_f[:], func=AFT.Identity)
        ln_stats(tt, xb_bf, xq_bf, 2, ps_s, ps_q, strow, mu_b, r_b)
        wv(v, se)
        v.tensor_tensor(out=xn2_f[:], in0=xn_f[:],
                        in1=bass.AP(mu_b, 0, [[512, 128], [0, 8], [1, 512]]),
                        op=OP.subtract)
        v.tensor_tensor(out=xn2_f[:], in0=xn2_f[:],
                        in1=bass.AP(r_b, 0, [[512, 128], [0, 8], [1, 512]]),
                        op=OP.mult)
        wv(s, ve)
        wv(s, te)
        for hc in range(8):
            cs = slice(hc * TT, (hc + 1) * TT)
            s.activation(out=xn2b[:, cs], in_=xn2_f[:, cs], func=AFT.Identity,
                         scale=p_ln1g[:, hc:hc + 1], bias=p_ln1b[:, hc:hc + 1])
        wv(v, se)
        v.tensor_copy(out=xn2_f[:], in_=xn2b[:])
        for hc in range(8):
            cs = slice(hc * TT, (hc + 1) * TT)
            stg = acst[:, (hc % 2) * TT:(hc % 2 + 1) * TT]
            wv(v, acg[hc % 2])
            v.scalar_tensor_tensor(out=stg, in0=xn2_f[:, cs],
                                   scalar=p_d[:, hc:hc + 1], in1=xn_f[:, cs],
                                   op0=OP.mult, op1=OP.add)
            wv(y, ve)
            ydma(accin_d.ap()[hc * 128:(hc + 1) * 128, tt * TT:(tt + 1) * TT],
                 stg, acg[hc % 2])
        wv(t, lg)
        wv(t, se)
        for mo in range(8):
            pd = ps_a if (mo % 2 == 0) else ps_b
            dstg = dstage[:, (mo % 2) * TT:(mo % 2 + 1) * TT]
            dxstg = dxstage[:, (mo % 2) * TT:(mo % 2 + 1) * TT]
            bank_wait(pd)
            for kc in range(8):
                mm = t.matmul(pd[:, :],
                              wdT[:, kc * H + mo * 128: kc * H + (mo + 1) * 128],
                              xn2b[:, kc * TT:(kc + 1) * TT],
                              start=(kc == 0), stop=(kc == 7))
            te.inc(mm)
            wv(s, te)
            wv(s, dsg[mo % 2])
            s.activation(out=dstg, in_=pd[:, :], func=AFT.Abs,
                         bias=p_bd[:, mo:mo + 1])
            s.activation(out=dstg, in_=dstg, func=AFT.Exp, scale=-1.0)
            s.activation(out=dstg, in_=dstg, func=AFT.Ln, bias=1.0)
            s.activation(out=dxstg, in_=pd[:, :], func=AFT.Relu,
                         bias=p_bd[:, mo:mo + 1])
            bank_set(pd, se)
            wv(v, se)
            wv(v, dsg[mo % 2])
            v.tensor_tensor(out=dstg, in0=dstg, in1=dxstg, op=OP.add)
            v.tensor_tensor(out=dxstg, in0=dstg,
                            in1=xn2_f[:, mo * TT:(mo + 1) * TT], op=OP.mult)
            wv(y, ve)
            ydma(ag1a_in.ap()[mo * 128:(mo + 1) * 128, tt * TT:(tt + 1) * TT],
                 dstg, dsg[mo % 2])
            ydma(ag1b_in.ap()[mo * 128:(mo + 1) * 128, tt * TT:(tt + 1) * TT],
                 dxstg, dsg[mo % 2])
        bank_wait(ps_bc)
        for kc in range(8):
            mm = t.matmul(ps_bc[:, :], wbcT[:, kc * 32:(kc + 1) * 32],
                          xn2b[:, kc * TT:(kc + 1) * TT],
                          start=(kc == 0), stop=(kc == 7))
        te.inc(mm)
        wv(v, te)
        wv(v, bcg)
        v.tensor_scalar(out=bcst[:], in0=ps_bc[:, :], scalar1=p_bbc[:, 0:1],
                        scalar2=None, op0=OP.add)
        bank_set(ps_bc, ve)
        wv(y, ve)
        ydma(ag1a_in.ap()[H:H + 32, tt * TT:(tt + 1) * TT], bcst[:], bcg)
        bank_wait(ps_rt)
        for kc in range(8):
            mm = t.matmul(ps_rt[:, :], rwT[:, kc * NE:(kc + 1) * NE],
                          xn_bf[:, kc * TC + tt * TT: kc * TC + (tt + 1) * TT],
                          start=(kc == 0), stop=(kc == 7))
        te.inc(mm)
        wv(v, te)
        v.tensor_copy(out=logits32[0:8, tt * TT:(tt + 1) * TT], in_=ps_rt[:, :])
        bank_set(ps_rt, ve)

    if "xn_bf" in dbg_t:
        wv(y, ve)
        for hc in range(8):
            ydma(dbg_t["xn_bf"].ap()[hc * 128:(hc + 1) * 128, :],
                 xn_bf[:, hc * TC:(hc + 1) * TC])
    if "logits" in dbg_t:
        wv(y, ve)
        ydma(dbg_t["logits"].ap()[:, :], logits32[0:8, :])
    if "accin" in dbg_t:
        wv(g, ly)
        wv(g, lg)
        gdma(dbg_t["accin"].ap()[:, :], accin_d.ap()[:, :])
    if "delta" in dbg_t:
        wv(g, ly)
        wv(g, lg)
        gdma(dbg_t["delta"].ap()[:, :], ag1a_in.ap()[0:H, :])

    for sem_ in (dsg[0], dsg[1], acg[0], acg[1], bcg):
        wv(g, sem_)
    cc.inc(g.collective_compute("AllGather", OP.bypass, replica_groups=RG8,
                                ins=[ag1a_in.ap()], outs=[ag1a_out.ap()]))
    cc.inc(g.collective_compute("AllGather", OP.bypass, replica_groups=RG8,
                                ins=[ag1b_in.ap()], outs=[ag1b_out.ap()]))
    barrier()
    fe.close()
    fep.close()

    ro = ExitStack()
    R32 = sb(ro, "R32", [32, 64 * 32], F32)
    sco = sb(ro, "sco", [32, 512], F32)
    msc = sb(ro, "msc", [32, 512], F32)
    gtmp = sb(ro, "gtmp", [32, 128], F32)
    gtmp2 = sb(ro, "gtmp2", [32, 128], F32)
    ge0 = sb(ro, "ge0", [32, 64], F32)
    ge1 = sb(ro, "ge1", [32, 64], F32)
    m1c = sb(ro, "m1c", [32, 64], F32)
    m1r = sb(ro, "m1r", [32, 512], F32)
    f1 = sb(ro, "f1", [32, 512], F32)
    f2 = sb(ro, "f2", [32, 512], F32)
    cum = sb(ro, "cum", [32, 512], F32)
    cum2 = sb(ro, "cum2", [32, 512], F32)
    wsc = sb(ro, "wsc", [32, 64], F32)
    w1c = sb(ro, "w1c", [32, 64], F32)
    w2c = sb(ro, "w2c", [32, 64], F32)
    prodw = sb(ro, "prodw", [32, 512], F32)
    padbuf = sb(ro, "padbuf", [32, 64 * 32], F32)
    combT = sb(ro, "combT", [32, TC], F32)

    def rap(tsr, fsz, off, dims, np_=32):
        return bass.AP(tsr, off, [[fsz, np_]] + dims)

    for j in range(64):
        v.transpose(out=R32[0:32, j * 32:(j + 1) * 32],
                    in_=logits32[0:32, j * 32:(j + 1) * 32])
    wv(s, ve)
    s.activation(out=R32[:], in_=R32[:], func=AFT.Tanh, scale=0.5)
    wv(v, se)
    v.tensor_scalar(out=rap(sco, 512, 0, [[8, 64], [1, 8]]),
                    in0=rap(R32, 64 * 32, 0, [[32, 64], [1, 8]]),
                    scalar1=0.5, scalar2=0.5, op0=OP.mult, op1=OP.add)
    wv(v, ly)
    v.tensor_tensor(out=msc[:], in0=sco[:], in1=p_cbt[:], op=OP.add)
    for idx, (i, j) in enumerate([(0, 1), (0, 2), (0, 3), (1, 2), (1, 3),
                                  (2, 3)]):
        dst = gtmp if idx == 0 else gtmp2
        v.tensor_tensor(out=dst[:], in0=rap(msc, 512, i, [[4, 128]]),
                        in1=rap(msc, 512, j, [[4, 128]]), op=OP.add)
        if idx > 0:
            v.tensor_tensor(out=gtmp[:], in0=gtmp[:], in1=gtmp2[:], op=OP.max)
    v.tensor_tensor(out=ge0[:], in0=rap(gtmp, 128, 0, [[2, 64]]),
                    in1=rap(gtmp, 128, 1, [[2, 64]]), op=OP.is_ge)
    v.tensor_tensor(out=ge1[:], in0=rap(gtmp, 128, 1, [[2, 64]]),
                    in1=rap(gtmp, 128, 0, [[2, 64]]), op=OP.is_gt)
    for e in range(4):
        v.tensor_copy(out=rap(m1r, 512, e, [[8, 64]]), in_=ge0[:])
        v.tensor_copy(out=rap(m1r, 512, 4 + e, [[8, 64]]), in_=ge1[:])
    v.tensor_tensor(out=msc[:], in0=msc[:], in1=m1r[:], op=OP.mult)

    def top_select(fdst):
        v.tensor_reduce(out=m1c[:], in_=rap(msc, 512, 0, [[8, 64], [1, 8]]),
                        axis=AXX, op=OP.max)
        for e in range(8):
            v.tensor_copy(out=rap(m1r, 512, e, [[8, 64]]), in_=m1c[:])
        v.tensor_tensor(out=fdst[:], in0=msc[:], in1=m1r[:], op=OP.is_ge)
        v.tensor_copy(out=cum[:], in_=fdst[:])
        src, dst = cum, cum2
        for shd in (1, 2, 4):
            v.tensor_copy(out=dst[:], in_=src[:])
            v.tensor_tensor(
                out=rap(dst, 512, shd, [[8, 64], [1, 8 - shd]]),
                in0=rap(src, 512, shd, [[8, 64], [1, 8 - shd]]),
                in1=rap(src, 512, 0, [[8, 64], [1, 8 - shd]]), op=OP.add)
            src, dst = dst, src
        v.tensor_scalar(out=dst[:], in0=src[:], scalar1=1.0, scalar2=None,
                        op0=OP.is_le)
        v.tensor_tensor(out=fdst[:], in0=fdst[:], in1=dst[:], op=OP.mult)

    top_select(f1)
    v.tensor_tensor(out=prodw[:], in0=f1[:], in1=sco[:], op=OP.mult)
    v.tensor_reduce(out=w1c[:], in_=rap(prodw, 512, 0, [[8, 64], [1, 8]]),
                    axis=AXX, op=OP.add)
    v.scalar_tensor_tensor(out=msc[:], in0=f1[:], scalar=BIG_NEG, in1=msc[:],
                           op0=OP.mult, op1=OP.add)
    top_select(f2)
    v.tensor_tensor(out=prodw[:], in0=f2[:], in1=sco[:], op=OP.mult)
    v.tensor_reduce(out=w2c[:], in_=rap(prodw, 512, 0, [[8, 64], [1, 8]]),
                    axis=AXX, op=OP.add)
    v.tensor_tensor(out=wsc[:], in0=w1c[:], in1=w2c[:], op=OP.add)
    v.tensor_scalar(out=wsc[:], in0=wsc[:], scalar1=2.0, scalar2=2e-20,
                    op0=OP.mult, op1=OP.add)
    v.reciprocal(out=wsc[:], in_=wsc[:])
    for e in range(8):
        v.tensor_copy(out=rap(m1r, 512, e, [[8, 64]]), in_=w1c[:])
    v.tensor_tensor(out=f1[:], in0=f1[:], in1=m1r[:], op=OP.mult)
    for e in range(8):
        v.tensor_copy(out=rap(m1r, 512, e, [[8, 64]]), in_=w2c[:])
    v.tensor_tensor(out=f2[:], in0=f2[:], in1=m1r[:], op=OP.mult)
    v.tensor_tensor(out=f1[:], in0=f1[:], in1=f2[:], op=OP.add)
    for e in range(8):
        v.tensor_copy(out=rap(m1r, 512, e, [[8, 64]]), in_=wsc[:])
    v.tensor_tensor(out=f1[:], in0=f1[:], in1=m1r[:], op=OP.mult)
    v.memset(padbuf[:], 0.0)
    v.tensor_copy(out=rap(padbuf, 64 * 32, 0, [[32, 64], [1, 8]]),
                  in_=rap(f1, 512, 0, [[8, 64], [1, 8]]))
    for j in range(64):
        v.transpose(out=combT[0:32, j * 32:(j + 1) * 32],
                    in_=padbuf[0:32, j * 32:(j + 1) * 32])
    wv(g, ve)
    gdma(comb_d.ap()[:, :], combT[0:8, :])
    lg_comb = lg.n
    if "comb" in dbg_t:
        wv(g, lg)
        gdma(dbg_t["comb"].ap()[:, :], comb_d.ap()[:, :])
    barrier()
    ro.close()
    fer.close()

    sm = ExitStack()
    pro = ExitStack()
    gstg = sb(pro, "gstg", [128, TC], F32)
    gstgb = sb(pro, "gstgb", [128, TC], BF16)
    bcs = sb(pro, "bcs", [32, 2 * TC], F32)
    wv(g, cc, 2)
    for sh in range(2):
        lg.inc(g.indirect_dma_start(
            out=bcs[0:32, sh * TC:(sh + 1) * TC], out_offset=None,
            in_=bass.AP(ag1a_out, 0, [[TC, 8 * AG1R], [1, TC]]),
            in_offset=bass.IndirectOffsetOnAxis(
                ix_g[0:32, 16 + sh:17 + sh], 0)), 16)
    wv(g, lg)
    for sh in range(2):
        gdma(bc_seq.ap()[:, sh * TC:(sh + 1) * TC],
             bcs[0:32, sh * TC:(sh + 1) * TC])
    for sh in range(2):
        for c in range(4):
            wv(g, lg)
            lg.inc(g.indirect_dma_start(
                out=gstg[:, :], out_offset=None,
                in_=bass.AP(ag1a_out, 0, [[TC, 8 * AG1R], [1, TC]]),
                in_offset=bass.IndirectOffsetOnAxis(
                    ix_g[:, sh * 4 + c: sh * 4 + c + 1], 0)), 16)
            wv(g, lg)
            gdma(dloc.ap()[c * 128:(c + 1) * 128, sh * TC:(sh + 1) * TC],
                 gstg[:, :])
    for sh in range(2):
        for c in range(4):
            wv(g, lg)
            lg.inc(g.indirect_dma_start(
                out=gstgb[:, :], out_offset=None,
                in_=bass.AP(ag1b_out, 0, [[TC, 8 * H], [1, TC]]),
                in_offset=bass.IndirectOffsetOnAxis(
                    ix_g[:, 8 + sh * 4 + c: 8 + sh * 4 + c + 1], 0)), 16)
            wv(g, lg)
            gdma(dxloc.ap()[c * 128:(c + 1) * 128, sh * TC:(sh + 1) * TC],
                 gstgb[:, :])
    if "bcv" in dbg_t:
        wv(g, lg)
        gdma(dbg_t["bcv"].ap()[:, :], bc_seq.ap()[:, :])
    pro.close()
    drep = sb(sm, "drep", [128, 4 * TQ], BF16)
    dxrep = sb(sm, "dxrep", [128, 4 * TQ], BF16)
    brepq = sb(sm, "brepq", [128, 2 * TQ], BF16)
    crepq = sb(sm, "crepq", [128, 2 * TQ], BF16)
    dAt = sb(sm, "dAt", [128, 4 * TQ], F32)
    dBxt = sb(sm, "dBxt", [128, 4 * TQ], F32)
    Hsb = sb(sm, "Hsb", [128, TQ], BF16)
    prodb = sb(sm, "prodb", [128, 2 * TQ], BF16)
    ystg = sb(sm, "ystg", [128, TQ], F32)
    ps_y = [psum(sm, f"ps_y{i}", [128, 512]) for i in range(4)]

    wgs = sb(sm, "wgs", [128, 2 * 8 * II], BF16)
    wus = sb(sm, "wus", [128, 2 * 8 * II], BF16)
    wdmo = sb(sm, "wdmo", [128, 2 * 4 * H], BF16)
    wsdm = sb(sm, "wsdm", [128, 2 * 4 * 128], BF16)
    h_all = sb(sm, "h_all", [128, NE * 4 * TT], BF16)
    h_sh = sb(sm, "h_sh", [128, 4 * TT], BF16)
    combr = sb(sm, "combr", [128, 2 * TT], BF16)
    g_bf = sb(sm, "g_bf", [128, 2 * TT], BF16)
    u_bf = sb(sm, "u_bf", [128, 2 * TT], BF16)
    tnh = sb(sm, "tnh", [128, 2 * TT], BF16)
    h1t = sb(sm, "h1t", [128, 2 * TT], BF16)
    mstage = sb(sm, "mstage", [128, 2 * TT], F32)

    dA_done, dbx_done, scan_done, prod_done = {}, {}, {}, {}
    drep_ld, dxrep_ld, ymm_done, evac_done, yq_dma = {}, {}, {}, {}, {}
    brep_ld = {}
    ypend = []
    ypend_flush = {}

    def emit_ymm(item):
        u, par, kt = item
        cidx2 = u // 16
        pyA2, pyB2, chunk2, sh2, twp2 = ypend_flush[cidx2]
        wv(t, ve, prod_done[u])
        if kt == 0:
            bank_wait(pyA2, pyB2)
        t.matmul(pyA2[:, :], esel[:, kt * 128:(kt + 1) * 128],
                 bass.AP(prodb, par * TQ, [[2 * TQ, 128], [1, 512]]),
                 start=(kt == 0), stop=(kt == 15))
        mm = t.matmul(pyB2[:, :], esel[:, kt * 128:(kt + 1) * 128],
                      bass.AP(prodb, par * TQ + 512,
                              [[2 * TQ, 128], [1, 512]]),
                      start=(kt == 0), stop=(kt == 15))
        te.inc(mm)
        ymm_done[u] = te.n
        if kt == 15:
            del ypend_flush[cidx2]
            wv(s, te, ymm_done[u])
            wv(s, yst)
            s.activation(out=ystg[:, 0:512], in_=pyA2[:, :], func=AFT.Identity)
            s.activation(out=ystg[:, 512:TQ], in_=pyB2[:, :], func=AFT.Identity)
            bank_set(pyA2, se)
            bank_set(pyB2, se)
            evac_done[cidx2] = se.n
            wv(y, se, evac_done[cidx2])
            wv(y, yst)
            ydma(bass.AP(ag2_in, (sh2 * HOH + chunk2 * 128) * TC + twp2,
                         [[TC, 128], [1, TQ]]),
                 ystg[:, 0:TQ], yst)

    def emit_ssm():
        lg_pro = lg.n
        yield
        def unit_loads(u2):
            if u2 >= 256:
                return
            tq2, tile2 = u2 // 64, u2 % 64
            par2 = u2 % 4
            dsl2 = slice(par2 * TQ, (par2 + 1) * TQ)
            wv(y, se, dA_done.get(u2 - 4, 0))
            wv(y, ge, dbx_done.get(u2 - 4, 0))

            ydma(drep[:, dsl2],
                 bass.AP(dloc, (8 * tile2) * L + tq2 * TQ,
                         [[L, 8], [0, 16], [1, TQ]]), dld[par2])
            ydma(dxrep[:, dsl2],
                 bass.AP(dxloc, (8 * tile2) * L + tq2 * TQ,
                         [[L, 8], [0, 16], [1, TQ]]), dld[par2])

        wv(y, lg, lg_pro)
        for uu in range(3):
            unit_loads(uu)
        for tq in range(4):
            sh, twp = tq // 2, (tq % 2) * TQ
            qpar = tq % 2
            qsl = slice(qpar * TQ, (qpar + 1) * TQ)
            wv(y, ve, scan_done.get(("q", tq - 2), 0))
            wv(y, bcl)
            ydma(brepq[:, qsl], bass.AP(bc_seq, tq * TQ,
                                        [[0, 8], [L, 16], [1, TQ]]), bcl)
            ydma(crepq[:, qsl], bass.AP(bc_seq, 16 * L + tq * TQ,
                                        [[0, 8], [L, 16], [1, TQ]]), bcl)
            for chunk in range(4):
                pyA = ps_y[(chunk % 2) * 2]
                pyB = ps_y[(chunk % 2) * 2 + 1]
                ypend_flush[tq * 4 + chunk] = (pyA, pyB, chunk, sh, twp)
                for kt in range(16):
                    tile = chunk * 16 + kt
                    u = tq * 64 + tile
                    lpar = u % 4
                    par = u % 4
                    lsl = slice(lpar * TQ, (lpar + 1) * TQ)
                    dsl = slice(par * TQ, (par + 1) * TQ)
                    unit_loads(u + 3)
                    wv(s, dld[lpar])
                    wv(s, ve, scan_done.get(u - 4, 0))
                    s.activation(out=dAt[:, dsl], in_=drep[:, lsl],
                                 func=AFT.Exp, scale=negA[:, tile:tile + 1])
                    dA_done[u] = se.n
                    wv(g, dld[lpar])
                    wv(g, bcl)
                    wv(g, ve, scan_done.get(u - 4, 0))
                    gtt(dBxt[:, dsl], dxrep[:, lsl], brepq[:, qsl], OP.mult)
                    dbx_done[u] = ge.n
                    wv(v, se, dA_done[u])
                    wv(v, ge, dbx_done[u])
                    v.tensor_tensor_scan(out=Hsb[:, :], data0=dAt[:, dsl],
                                         data1=dBxt[:, dsl],
                                         initial=states[:, tile:tile + 1],
                                         op0=OP.mult, op1=OP.add)
                    v.tensor_copy(out=states[:, tile:tile + 1],
                                  in_=Hsb[:, TQ - 1:TQ])
                    ppar = u % 2
                    psl = slice(ppar * TQ, (ppar + 1) * TQ)
                    wv(v, te, ymm_done.get(u - 2, 0))
                    wv(v, bcl)
                    v.tensor_tensor(out=prodb[:, psl], in0=Hsb[:, :],
                                    in1=crepq[:, qsl], op=OP.mult)
                    scan_done[u] = ve.n
                    if tile == 63:
                        scan_done[("q", tq)] = ve.n
                    prod_done[u] = ve.n
                    ypend.append((u, u % 2, kt))
                    if len(ypend) > YMM_SKEW:
                        emit_ymm(ypend.pop(0))
                yield
            if tq == 1:
                while ypend:
                    emit_ymm(ypend.pop(0))
                wv(g, yst)
                cc.inc(g.collective_compute(
                    "AllGather", OP.bypass, replica_groups=RG8,
                    ins=[ag2_in.ap()[0:HOH, :]],
                    outs=[ag2_out.ap()[0:8 * HOH, :]]))
        while ypend:
            emit_ymm(ypend.pop(0))
        wv(g, yst)
        cc.inc(g.collective_compute(
            "AllGather", OP.bypass, replica_groups=RG8,
            ins=[ag2_in.ap()[HOH:2 * HOH, :]],
            outs=[ag2_out.ap()[8 * HOH:16 * HOH, :]]))
        if "yssm" in dbg_t:
            wv(g, cc)
            wv(g, ly)
            gdma(dbg_t["yssm"].ap()[:, :], ag2_in.ap()[:, :])
        yield

    gu_evac, h_ready, mo_evac, mo_dma, dn_done, hcons = {}, {}, {}, {}, {}, {}
    ld_w, hr_last = {}, {}
    gu_u = [0]
    gupend, dnpend = [], []
    combr_guard = {}

    def moe_w_load(tt, e):
        ex = tt * 9 + (e + 1)
        slot = ex % 2
        ydma(wgs[:, slot * 8 * II:(slot + 1) * 8 * II],
             bass.AP(wg_b if e >= 0 else swg_b, max(e, 0) * H * II,
                     [[II, 128], [128 * II, 8], [1, II]]), mwl[slot])
        ydma(wus[:, slot * 8 * II:(slot + 1) * 8 * II],
             bass.AP(wu_b if e >= 0 else swu_b, max(e, 0) * H * II,
                     [[II, 128], [128 * II, 8], [1, II]]), mwl[slot])
        ld_w[(tt, e)] = (slot, mwl[slot].n)

    def hr_last_for(tt2, e2):
        if e2 >= -1 and (tt2, e2) in hr_last:
            return hr_last[(tt2, e2)]
        if e2 < -1 and (tt2 - 1, e2 + 9) in hr_last:
            return hr_last[(tt2 - 1, e2 + 9)]
        return 0

    def flush_gu(item):
        tt, e, kc4, pg, pu, ug, mmv, crv, cpar = item
        parq = ug % 2
        csl = slice(parq * TT, (parq + 1) * TT)
        wv(s, te, mmv)
        wv(s, ve, h_ready.get(ug - 2, 0))
        s.activation(out=tnh[:, csl], in_=pg[:, :], func=AFT.Tanh, scale=0.5)
        s.activation(out=g_bf[:, csl], in_=pg[:, :], func=AFT.Identity)
        s.activation(out=u_bf[:, csl], in_=pu[:, :], func=AFT.Identity)
        bank_set(pg, se)
        bank_set(pu, se)
        gu_evac[((tt, e), kc4)] = se.n
        wv(v, se, se.n)
        v.scalar_tensor_tensor(out=h1t[:, csl], in0=tnh[:, csl],
                               scalar=1.0, in1=u_bf[:, csl],
                               op0=OP.add, op1=OP.mult)
        if e < 0:
            wv(v, te, dn_done.get((tt - 1, 7), 0))
            v.scalar_tensor_tensor(
                out=h_sh[:, kc4 * TT:(kc4 + 1) * TT],
                in0=h1t[:, csl], scalar=0.5, in1=g_bf[:, csl],
                op0=OP.mult, op1=OP.mult)
        else:
            wv(v, crls[cpar], crv)
            wv(v, te, dn_done.get((tt - 1, 7), 0))
            v.tensor_tensor(out=h1t[:, csl], in0=h1t[:, csl],
                            in1=g_bf[:, csl], op=OP.mult)
            v.tensor_tensor(
                out=h_all[:, (e * 4 + kc4) * TT:(e * 4 + kc4 + 1) * TT],
                in0=h1t[:, csl],
                in1=combr[:, cpar * TT:(cpar + 1) * TT], op=OP.mult)
            combr_guard[cpar] = ve.n
        h_ready[ug] = ve.n
        hr_last[(tt, e)] = ve.n

    def flush_dn(item):
        tt, mo, pd2 = item
        wv(s, te, dn_done[(tt, mo)])
        wv(s, mst)
        s.activation(out=mstage[:, (mo % 2) * TT:(mo % 2 + 1) * TT],
                     in_=pd2[:, :], func=AFT.Identity)
        bank_set(pd2, se)
        mo_evac[(tt, mo)] = se.n
        wv(y, se, se.n)
        wv(y, mst)
        ydma(moe_d.ap()[mo * 128:(mo + 1) * 128, tt * TT:(tt + 1) * TT],
             mstage[:, (mo % 2) * TT:(mo % 2 + 1) * TT], mst)

    def emit_moe():
        wv(y, lg, lg_precast)
        moe_w_load(0, -1)
        moe_w_load(0, 0)
        for tt in range(NTT):
            for e in range(-1, NE):
                slot = (tt * 9 + e + 1) % 2
                wsl_g = wgs[:, slot * 8 * II:(slot + 1) * 8 * II]
                wsl_u = wus[:, slot * 8 * II:(slot + 1) * 8 * II]
                cpar = (e + 1) % 2
                if e >= 0:
                    wv(y, ve, combr_guard.get(cpar, 0))
                    wv(y, lg, lg_comb)
                    ydma(combr[:, cpar * TT:(cpar + 1) * TT],
                         bass.AP(comb_d, e * TC + tt * TT, [[0, 128], [1, TT]]),
                         crls[cpar])
                    crv = crls[cpar].n
                else:
                    crv = 0
                for kc4 in range(4):
                    pg, pu = (ps_a, ps_b) if kc4 % 2 == 0 else (ps_c, ps_dd)
                    wv(t, mwl[ld_w[(tt, e)][0]], ld_w[(tt, e)][1])
                    bank_wait(pg, pu)
                    for kc in range(8):
                        t.matmul(pg[:, :],
                                 wsl_g[:, kc * II + kc4 * 128:
                                       kc * II + (kc4 + 1) * 128],
                                 xn_bf[:, kc * TC + tt * TT:
                                       kc * TC + (tt + 1) * TT],
                                 start=(kc == 0), stop=(kc == 7))
                    for kc in range(8):
                        mm = t.matmul(pu[:, :],
                                      wsl_u[:, kc * II + kc4 * 128:
                                            kc * II + (kc4 + 1) * 128],
                                      xn_bf[:, kc * TC + tt * TT:
                                            kc * TC + (tt + 1) * TT],
                                      start=(kc == 0), stop=(kc == 7))
                    te.inc(mm)
                    mmv = te.n
                    gu_u[0] += 1
                    ug = gu_u[0]
                    gupend.append((tt, e, kc4, pg, pu, ug, mmv, crv, cpar))
                    if len(gupend) > GU_SKEW:
                        flush_gu(gupend.pop(0))
                    mm_last = te.n
                    yield
                nxt_e, nxt_tt = e + 2, tt
                if nxt_e >= NE:
                    nxt_e -= (NE + 1)
                    nxt_tt += 1
                if nxt_tt < NTT:
                    wv(y, te, mm_last)
                    moe_w_load(nxt_tt, nxt_e)

            while gupend:
                flush_gu(gupend.pop(0))
            for mo in range(8):
                pd2 = ps_a if mo % 2 == 0 else ps_b
                slot = mo % 2
                wv(y, te, dn_done.get((tt, mo - 2), 0))
                ydma(wdmo[:, slot * 4 * H:(slot + 1) * 4 * H],
                     bass.AP(wdn_b, mo * 128,
                             [[H, 128], [II * H, 8], [128 * H, 4], [1, 128]]),
                     wdl[slot])
                ydma(wsdm[:, slot * 4 * 128:(slot + 1) * 4 * 128],
                     bass.AP(swd_b, mo * 128,
                             [[H, 128], [128 * H, 4], [1, 128]]), wdl[slot])
                wv(t, ve, hr_last[(tt, NE - 1)])
                wv(t, ve, hr_last[(tt, -1)])
                wv(t, wdl[slot])
                bank_wait(pd2)
                for e in range(NE):
                    for kc in range(4):
                        t.matmul(pd2[:, :],
                                 wdmo[:, slot * 4 * H + (e * 4 + kc) * 128:
                                       slot * 4 * H + (e * 4 + kc + 1) * 128],
                                 h_all[:, (e * 4 + kc) * TT:
                                       (e * 4 + kc + 1) * TT],
                                 start=(e == 0 and kc == 0), stop=False)
                for kc in range(4):
                    mm = t.matmul(pd2[:, :],
                                  wsdm[:, slot * 4 * 128 + kc * 128:
                                        slot * 4 * 128 + (kc + 1) * 128],
                                  h_sh[:, kc * TT:(kc + 1) * TT],
                                  start=False, stop=(kc == 3))
                te.inc(mm)
                dn_done[(tt, mo)] = te.n
                dnpend.append((tt, mo, pd2))
                if len(dnpend) > DN_SKEW:
                    flush_dn(dnpend.pop(0))
                yield
            while dnpend:
                flush_dn(dnpend.pop(0))
        if "moe" in dbg_t:
            wv(g, mst)
            wv(g, lg)
            gdma(dbg_t["moe"].ap()[:, :], moe_d.ap()[:, :])
        yield

    gen_s = emit_ssm()
    gen_m = emit_moe()
    alive_s, alive_m = True, True
    for _ in range(46):
        try:
            next(gen_m)
        except StopIteration:
            alive_m = False
            break
    next(gen_s)
    while alive_s or alive_m:
        if alive_s:
            try:
                next(gen_s)
            except StopIteration:
                alive_s = False
        if alive_m:
            for _ in range(8):
                try:
                    next(gen_m)
                except StopIteration:
                    alive_m = False
                    break
    barrier()
    sm.close()

    p4 = ExitStack()
    p4p = ExitStack()
    ps_s2 = psum(p4p, "ps_s2", [1, 512])
    ps_q2 = psum(p4p, "ps_q2", [1, 512])
    accf = sb(p4, "accf", [128, 8 * TC], F32)

    p4a = ExitStack()
    mu_b = sb(p4a, "mu_b2", [128, TT], F32)
    r_b = sb(p4a, "r_b2", [128, TT], F32)
    strow = sb(p4a, "strow2", [1, 3 * TT], F32)
    ystage = sb(p4a, "ystage", [128, 2 * TT], F32)
    astage = sb(p4a, "astage", [128, 2 * TT], F32)
    ac_bf = sb(p4a, "ac_bf", [128, 8 * TT], BF16)
    aq_bf = sb(p4a, "aq_bf", [128, 8 * TT], BF16)

    yg_cons, hff_done, ots_done = {}, {}, {}
    w1ld, w2ld = {}, {}

    wv(g, cc, 4)
    ag2av = bass.AP(ag2_out, 0, [[TT, 8 * 2 * HOH * 4], [1, TT]])
    for tt in range(NTT):
        for hc in range(8):
            uy = tt * 8 + hc
            ypar = hc % 2
            wv(g, ve, yg_cons.get(uy - 2, 0))
            ygls[ypar].inc(g.indirect_dma_start(
                out=ystage[:, ypar * TT:(ypar + 1) * TT], out_offset=None,
                in_=ag2av,
                in_offset=bass.IndirectOffsetOnAxis(ix_y[:, hc:hc + 1], 0),
                element_offset=tt * TT), 16)
            wv(y, ve, yg_cons.get(uy - 2, 0))
            ydma(astage[:, ypar * TT:(ypar + 1) * TT],
                 accin_d.ap()[hc * 128:(hc + 1) * 128, tt * TT:(tt + 1) * TT],
                 a2ls[ypar])
            wv(v, ygls[ypar])
            wv(v, a2ls[ypar])
            v.tensor_tensor(
                out=accf[:, hc * TC + tt * TT: hc * TC + (tt + 1) * TT],
                in0=astage[:, ypar * TT:(ypar + 1) * TT],
                in1=ystage[:, ypar * TT:(ypar + 1) * TT], op=OP.add)
            yg_cons[uy] = ve.n
    if "accf" in dbg_t:
        wv(y, ve)
        for hc in range(8):
            ydma(dbg_t["accf"].ap()[hc * 128:(hc + 1) * 128, :],
                 accf[:, hc * TC:(hc + 1) * TC])
    for tt in range(NTT):
        wv(s, ve)
        for hc in range(8):
            s.activation(out=aq_bf[:, hc * TT:(hc + 1) * TT],
                         in_=accf[:, hc * TC + tt * TT: hc * TC + (tt + 1) * TT],
                         func=AFT.Square)
            s.activation(out=ac_bf[:, hc * TT:(hc + 1) * TT],
                         in_=accf[:, hc * TC + tt * TT: hc * TC + (tt + 1) * TT],
                         func=AFT.Identity)
        ln_stats(tt, ac_bf, aq_bf, 4, ps_s2, ps_q2, strow, mu_b, r_b,
                 store=True)
    barrier()
    p4a.close()
    nstage = sb(p4, "nstage", [128, 8 * TT], BF16)
    hff = sb(p4, "hff", [128, 32 * TT], BF16)
    w1s = sb(p4, "w1s", [128, 4 * 8 * 128], BF16)
    w2s = sb(p4, "w2s", [128, 2 * 32 * 128], BF16)
    xts = sb(p4, "xts", [128, 2 * TT], F32)
    mts = sb(p4, "mts", [128, 2 * TT], F32)
    ots = sb(p4, "ots", [128, 2 * TT], F32)
    aplybuf = sb(p4, "aplybuf", [128, 4 * TT], F32)
    mu_b4 = sb(p4, "mu_b4", [128, 4 * TT], F32)
    r_b4 = sb(p4, "r_b4", [128, 4 * TT], F32)
    wv(y, stl)
    for tt in range(NTT):
        ydma(mu_b4[:, tt * TT:(tt + 1) * TT],
             bass.AP(stats_d, 4 * TC + tt * TT, [[0, 128], [1, TT]]), stl)
        ydma(r_b4[:, tt * TT:(tt + 1) * TT],
             bass.AP(stats_d, 5 * TC + tt * TT, [[0, 128], [1, TT]]), stl)
    stats_rdy = stl.n

    def w1_load(mf):
        slot = mf % 4
        sdma(w1s[:, slot * 8 * 128:(slot + 1) * 8 * 128],
             bass.AP(w1_b, mf * 128, [[FF, 128], [128 * FF, 8], [1, 128]]),
             w1l[slot])
        w1ld[mf] = (slot, w1l[slot].n)

    def w2_load(mo):
        slot = mo % 2
        ydma(w2s[:, slot * 32 * 128:(slot + 1) * 32 * 128],
             bass.AP(w2_b, mo * 128, [[H, 128], [128 * H, 32], [1, 128]]),
             w2l[slot])
        w2ld[mo] = (slot, w2l[slot].n)

    for mf in range(4):
        w1_load(mf)
    w2_load(0)
    w2_load(1)
    for tt in range(NTT):
        wv(s, te)
        wv(v, stl, stats_rdy)
        for gh in range(2):
            wv(v, se)
            v.tensor_tensor(out=aplybuf[:],
                            in0=bass.AP(accf, gh * 4 * TC + tt * TT,
                                        [[8 * TC, 128], [TC, 4], [1, TT]]),
                            in1=bass.AP(mu_b4, tt * TT,
                                        [[4 * TT, 128], [0, 4], [1, TT]]),
                            op=OP.subtract)
            v.tensor_tensor(out=aplybuf[:], in0=aplybuf[:],
                            in1=bass.AP(r_b4, tt * TT,
                                        [[4 * TT, 128], [0, 4], [1, TT]]),
                            op=OP.mult)
            wv(s, ve)
            for hc4 in range(4):
                hc = gh * 4 + hc4
                s.activation(out=nstage[:, hc * TT:(hc + 1) * TT],
                             in_=aplybuf[:, hc4 * TT:(hc4 + 1) * TT],
                             func=AFT.Identity, scale=p_ln2g[:, hc:hc + 1],
                             bias=p_ln2b[:, hc:hc + 1])
        nsv = se.n
        for mf in range(32):
            pd3 = (ps_a, ps_b, ps_c, ps_dd)[mf % 4]
            slot = mf % 4
            wsl = w1s[:, slot * 8 * 128:(slot + 1) * 8 * 128]
            wv(t, w1l[w1ld[mf][0]], w1ld[mf][1])
            wv(t, se, nsv)
            bank_wait(pd3)
            for kc in range(8):
                mm = t.matmul(pd3[:, :], wsl[:, kc * 128:(kc + 1) * 128],
                              nstage[:, kc * TT:(kc + 1) * TT],
                              start=(kc == 0), stop=(kc == 7))
            te.inc(mm)
            wv(s, te, te.n)
            s.activation(out=hff[:, mf * TT:(mf + 1) * TT], in_=pd3[:, :],
                         func=AFT.Gelu, bias=p_b1[:, mf:mf + 1])
            bank_set(pd3, se)
            hff_done[mf] = se.n
            wv(s0, te, te.n)
            nxt = mf + 4
            if nxt < 32:
                w1_load(nxt)
            elif tt + 1 < NTT:
                w1_load(nxt - 32)
        for mo in range(8):
            pd4 = (ps_a, ps_b, ps_c, ps_dd)[mo % 4]
            slot = mo % 2
            wsl2 = w2s[:, slot * 32 * 128:(slot + 1) * 32 * 128]
            wv(t, w2l[w2ld[mo][0]], w2ld[mo][1])
            wv(t, se, hff_done[31])
            bank_wait(pd4)
            for kc in range(32):
                mm = t.matmul(pd4[:, :], wsl2[:, kc * 128:(kc + 1) * 128],
                              hff[:, kc * TT:(kc + 1) * TT],
                              start=(kc == 0), stop=(kc == 31))
            te.inc(mm)
            w2v = te.n
            opar = mo % 2
            osl = slice(opar * TT, (opar + 1) * TT)
            wv(y, ve, ots_done.get(mo - 2, 0))
            ydma(xts[:, osl], bass.AP(x_fm, mo * 128 * TC + tt * TT,
                                      [[TC, 128], [1, TT]]), xmls[opar])
            ydma(mts[:, osl],
                 moe_d.ap()[mo * 128:(mo + 1) * 128, tt * TT:(tt + 1) * TT],
                 xmls[opar])
            wv(v, te, w2v)
            wv(v, xmls[opar])
            wv(v, so, so.n)
            v.scalar_tensor_tensor(
                out=ots[:, osl], in0=pd4[:, :], scalar=p_b2[:, mo:mo + 1],
                in1=accf[:, mo * TC + tt * TT: mo * TC + (tt + 1) * TT],
                op0=OP.add, op1=OP.add)
            bank_set(pd4, ve)
            if "attno" in dbg_t:
                wv(y, ve)
                ydma(dbg_t["attno"].ap()[mo * 128:(mo + 1) * 128,
                                         tt * TT:(tt + 1) * TT], ots[:, osl])
                wv(v, ly)
            v.tensor_tensor(out=ots[:, osl], in0=ots[:, osl], in1=xts[:, osl],
                            op=OP.add)
            v.tensor_tensor(out=ots[:, osl], in0=ots[:, osl], in1=mts[:, osl],
                            op=OP.add)
            ots_done[mo] = ve.n
            wv(y, ve, ve.n)
            so.inc(y.dma_start(out=out_t.ap()[mo * 128:(mo + 1) * 128,
                                              tt * TT:(tt + 1) * TT],
                               in_=ots[:, osl]), 16)
            wv(y, te, w2v)
            if mo + 2 < 8:
                w2_load(mo + 2)
            elif tt + 1 < NTT:
                w2_load(mo + 2 - 8)
        ots_done.clear()
        hff_done.clear()
    y.wait_ge(so.h, so.n)
    barrier()
    p4.close()
    p4p.close()
    gst.close()
    return nc



def _prep_core(inputs, k):
    q = k % 2
    b = k // 2
    perm = np.arange(H) if q == 0 else np.r_[np.arange(512, 1024), np.arange(512)]
    asc = np.ascontiguousarray
    f32 = np.float32

    x = np.asarray(inputs["L0_attn_x"])
    xk = x[b, q * TC:(q + 1) * TC, :]
    d = {"x_fm": asc(xk.T[perm])}

    Wd = np.asarray(inputs["L0_attn_Wd"])
    d["wd_t"] = asc(Wd[np.ix_(perm, perm)].T)
    d["wbc_t"] = asc(np.asarray(inputs["L0_attn_Wbc"])[:, perm].T)
    d["w1_t"] = asc(np.asarray(inputs["L0_attn_W1"])[:, perm].T)
    d["w2_t"] = asc(np.asarray(inputs["L0_attn_W2"])[perm].T)
    d["rw_t"] = asc(np.asarray(inputs["L0_ffn_router_weight"])[:, perm].T)
    d["wg_e"] = asc(np.asarray(inputs["L0_ffn_w_gate"])[:, perm, :])
    d["wu_e"] = asc(np.asarray(inputs["L0_ffn_w_up"])[:, perm, :])
    d["wdn_e"] = asc(np.asarray(inputs["L0_ffn_w_down"])[:, :, perm])
    d["swg"] = asc(np.asarray(inputs["L0_ffn_sw_gate"])[perm])
    d["swu"] = asc(np.asarray(inputs["L0_ffn_sw_up"])[perm])
    d["swd"] = asc(np.asarray(inputs["L0_ffn_sw_down"])[:, perm])

    def col8(vv):
        return asc(np.asarray(vv)[perm].reshape(8, 128).T.astype(f32))

    d["ln1_g"] = col8(inputs["L0_attn_ln1_g"])
    d["ln1_b"] = col8(inputs["L0_attn_ln1_b"])
    d["ln2_g"] = col8(inputs["L0_attn_ln2_g"])
    d["ln2_b"] = col8(inputs["L0_attn_ln2_b"])
    d["bd_c"] = col8(inputs["L0_attn_bd"])
    d["b2_c"] = col8(inputs["L0_attn_b2"])
    d["d_c"] = col8(inputs["L0_attn_D"])
    d["b1_c"] = asc(np.asarray(inputs["L0_attn_b1"]).reshape(32, 128).T
                    .astype(f32))
    d["bbc_c"] = asc(np.asarray(inputs["L0_attn_bbc"]).reshape(32, 1)
                     .astype(f32))
    alog = np.asarray(inputs["L0_attn_A_log"])[perm[:HOH]]
    a2 = np.empty((128, 64), f32)
    for tile in range(64):
        for hl in range(8):
            for n in range(NS):
                a2[hl * NS + n, tile] = alog[tile * 8 + hl, n]
    d["alog2"] = asc(a2)
    es = np.zeros((128, 16 * 128), f32)
    for ktile in range(16):
        for p in range(128):
            es[p, ktile * 128 + 8 * ktile + p // NS] = 1.0
    d["esel_i"] = asc(es)
    cb = np.asarray(inputs["L0_ffn_correction_bias"]).astype(f32)
    d["cbt_i"] = asc(np.tile(np.tile(cb, 64), (32, 1)))

    gi = np.zeros((128, 20), np.uint32)
    for sh in range(2):
        src = 2 * b + sh
        for c in range(4):
            p = np.arange(128)
            i = c * 128 + p
            gl = q * 512 + i
            pos = gl if sh == 0 else (gl + 512) % H
            gi[:, sh * 4 + c] = src * AG1R + pos
            gi[:, 8 + sh * 4 + c] = src * H + pos
        gi[0:32, 16 + sh] = src * AG1R + H + np.arange(32)
    d["gidx"] = gi
    yi = np.zeros((128, 8), np.uint32)
    for hc in range(8):
        p = np.arange(128)
        i = hc * 128 + p
        row = np.where(i < 512,
                       q * 8 * HOH + k * HOH + i,
                       q * 8 * HOH + (k ^ 1) * HOH + (i - 512))
        yi[:, hc] = row * 4
    d["yidx"] = yi
    d = {kk: (vv if vv.dtype == np.uint32 else vv.astype(np.float32))
         for kk, vv in d.items()}
    return d, perm


LAST_RESULTS = None


def kernel(**inputs):
    global LAST_RESULTS
    in_maps = []
    perms = []
    for k in range(8):
        m, perm = _prep_core(inputs, k)
        in_maps.append(m)
        perms.append(perm)
    nc = build_program(dbg=KDBG)
    res = run_bass_kernel_spmd(nc, in_maps, core_ids=list(range(8)))
    LAST_RESULTS = res.results
    out = np.empty((4, L, H), np.float32)
    for k in range(8):
        q, b = k % 2, k // 2
        o_fm = res.results[k]["out"]
        o_nat = np.empty_like(o_fm)
        o_nat[perms[k]] = o_fm
        out[b, q * TC:(q + 1) * TC, :] = o_nat.T
    return out


if __name__ == "__main__":
    nc = build_program()
    print("built OK")


# version 13
# speedup vs baseline: 1.0926x; 1.0035x over previous
import os
from contextlib import ExitStack

import numpy as np
import ml_dtypes
import concourse.bass as bass
import concourse.mybir as mybir
from concourse.bass_utils import run_bass_kernel_spmd
from concourse.alu_op_type import AluOpType as OP

F32 = mybir.dt.float32
BF16 = mybir.dt.bfloat16
U32 = mybir.dt.uint32
AFT = mybir.ActivationFunctionType
AXX = mybir.AxisListType.X

H = 1024
NS = 16
FF = 4096
II = 512
NE = 8
TC = 2048
TT = 512
NTT = TC // TT
L = 4096
EPS = 1e-5
HOH = 512
AG1R = H + 32
TQ = 1024
BIG_NEG = -1e30
RG8 = [[0, 1, 2, 3, 4, 5, 6, 7]]

KDBG = ()
RACECHECK = bool(os.environ.get("KERNEL_RACECHECK"))
GU_SKEW = 1
DN_SKEW = 1
YMM_SKEW = 1


class Sem:
    def __init__(self, st, nc, name):
        self.h = st.enter_context(nc.semaphore(name))
        self.n = 0

    def inc(self, inst, k=1):
        inst.then_inc(self.h, k)
        self.n += k
        return inst


def _ap_tname(x):
    try:
        t = x.tensor
        return getattr(t, "name", None) or str(id(t))
    except Exception:
        return None


class EW:
    OPS = ("tensor_tensor", "tensor_scalar", "scalar_tensor_tensor",
           "tensor_copy", "tensor_reduce", "reciprocal", "memset",
           "transpose", "tensor_tensor_scan", "activation", "iota")
    RKEYS = ("in_", "in0", "in1", "data0", "data1", "initial", "scalar",
             "scalar1", "scalar2", "bias", "scale", "mask")

    def __init__(self, eng, sem, waited):
        self._e = eng
        self._s = sem
        self._w = waited
        self._tw = {}
        self._tr = {}

    def __getattr__(self, nm):
        f = getattr(self._e, nm)
        if nm in EW.OPS:
            def wrapped(*a, **k):
                if RACECHECK:
                    key = (id(self._e), id(self._s))
                    if self._w.get(key, -1) < self._s.n:
                        self._e.wait_ge(self._s.h, self._s.n)
                        self._w[key] = self._s.n
                else:
                    wt = k.get("out")
                    if wt is None and nm == "memset" and a:
                        wt = a[0]
                    rd = [k[r] for r in EW.RKEYS if r in k]
                    need = 0
                    wtn = _ap_tname(wt) if wt is not None else None
                    for r in rd:
                        t2 = _ap_tname(r)
                        if t2 is not None:
                            need = max(need, self._tw.get(t2, 0))
                    if wtn is not None:
                        need = max(need, self._tw.get(wtn, 0),
                                   self._tr.get(wtn, 0))
                    key = (id(self._e), id(self._s))
                    if need > self._w.get(key, -1):
                        self._e.wait_ge(self._s.h, need)
                        self._w[key] = need
                r2 = self._s.inc(f(*a, **k))
                if not RACECHECK:
                    if wtn is not None:
                        self._tw[wtn] = self._s.n
                    for r in rd:
                        t2 = _ap_tname(r)
                        if t2 is not None:
                            self._tr[t2] = self._s.n
                return r2
            return wrapped
        return f


def build_program(dbg=()):
    nc = bass.Bass("TRN2", target_bir_lowering=False,
                   detect_race_conditions=RACECHECK)
    t, v0, s0, y, g = nc.tensor, nc.vector, nc.scalar, nc.sync, nc.gpsimd

    def din(name, shape, dt=F32):
        return nc.dram_tensor(name, shape, dt, kind="ExternalInput")

    x_fm = din("x_fm", [H, TC])
    wd_t = din("wd_t", [H, H])
    wbc_t = din("wbc_t", [H, 32])
    w1_t = din("w1_t", [H, FF])
    w2_t = din("w2_t", [FF, H])
    rw_t = din("rw_t", [H, NE])
    wg_e = din("wg_e", [NE, H, II])
    wu_e = din("wu_e", [NE, H, II])
    wdn_e = din("wdn_e", [NE, II, H])
    swg = din("swg", [H, II])
    swu = din("swu", [H, II])
    swd = din("swd", [II, H])
    ln1_g = din("ln1_g", [128, 8])
    ln1_b = din("ln1_b", [128, 8])
    ln2_g = din("ln2_g", [128, 8])
    ln2_b = din("ln2_b", [128, 8])
    bd_c = din("bd_c", [128, 8])
    b2_c = din("b2_c", [128, 8])
    d_c = din("d_c", [128, 8])
    b1_c = din("b1_c", [128, 32])
    bbc_c = din("bbc_c", [32, 1])
    alog2 = din("alog2", [128, 64])
    esel_i = din("esel_i", [128, 16 * 128])
    cbt_i = din("cbt_i", [32, 512])
    gidx = din("gidx", [128, 20], U32)
    yidx = din("yidx", [128, 8], U32)

    out_t = nc.dram_tensor("out", [H, TC], F32, kind="ExternalOutput")

    ag1a_in = nc.dram_tensor("ag1a_in", [AG1R, TC], F32, kind="Internal")
    ag1a_out = nc.dram_tensor("ag1a_out", [8 * AG1R, TC], F32, kind="Internal")
    ag1b_in = nc.dram_tensor("ag1b_in", [H, TC], BF16, kind="Internal")
    ag1b_out = nc.dram_tensor("ag1b_out", [8 * H, TC], BF16, kind="Internal")
    ag2_in = nc.dram_tensor("ag2_in", [2 * HOH, TC], F32, kind="Internal")
    ag2_out = nc.dram_tensor("ag2_out", [8 * 2 * HOH, TC], F32, kind="Internal")
    dloc = nc.dram_tensor("dloc", [HOH, L], BF16, kind="Internal")
    dxloc = nc.dram_tensor("dxloc", [HOH, L], BF16, kind="Internal")
    accin_d = nc.dram_tensor("accin_d", [H, TC], F32, kind="Internal")
    moe_d = nc.dram_tensor("moe_d", [H, TC], F32, kind="Internal")
    bc_seq = nc.dram_tensor("bc_seq", [32, L], BF16, kind="Internal")
    stats_d = nc.dram_tensor("stats_d", [8, TC], F32, kind="Internal")
    comb_d = nc.dram_tensor("comb_d", [NE, TC], BF16, kind="Internal")
    w1_b = nc.dram_tensor("w1_b", [H, FF], BF16, kind="Internal")
    w2_b = nc.dram_tensor("w2_b", [FF, H], BF16, kind="Internal")
    wg_b = nc.dram_tensor("wg_b", [NE, H, II], BF16, kind="Internal")
    wu_b = nc.dram_tensor("wu_b", [NE, H, II], BF16, kind="Internal")
    wdn_b = nc.dram_tensor("wdn_b", [NE, II, H], BF16, kind="Internal")
    swg_b = nc.dram_tensor("swg_b", [H, II], BF16, kind="Internal")
    swu_b = nc.dram_tensor("swu_b", [H, II], BF16, kind="Internal")
    swd_b = nc.dram_tensor("swd_b", [II, H], BF16, kind="Internal")

    dbg_t = {}
    for nm in dbg:
        shp, dt = {
            "xn_bf": ([H, TC], BF16), "delta": ([H, TC], F32),
            "bcv": ([32, L], BF16),
            "yssm": ([2 * HOH, TC], F32), "accin": ([H, TC], F32),
            "logits": ([8, TC], F32), "comb": ([NE, TC], BF16),
            "moe": ([H, TC], F32), "accf": ([H, TC], F32),
            "normed": ([H, TC], BF16), "attno": ([H, TC], F32),
        }[nm]
        dbg_t[nm] = nc.dram_tensor("dbg_" + nm, shp, dt, kind="ExternalOutput")

    gst = ExitStack()

    def sb(stk, name, shape, dt):
        return stk.enter_context(nc.sbuf_tensor(name, shape, dt))

    def psum(stk, name, shape):
        return stk.enter_context(nc.psum_tensor(name, shape, F32))

    sems = {n: Sem(gst, nc, "sem_" + n)
            for n in ("ly", "ls", "lg", "te", "ve", "se", "ge", "cc", "so",
                      "xw", "stl", "agst", "bcl", "yst", "crl", "mst",
                      "a2l", "xml",
                      "dld0", "dld1", "dld2", "dld3", "mwl0", "mwl1", "wdl0", "wdl1",
                      "w1l0", "w1l1", "w1l2", "w1l3", "w2l0", "w2l1",
                      "crl0", "crl1", "a2l0", "a2l1", "ygl0", "ygl1",
                      "xml0", "xml1", "dsg0", "dsg1", "acg0", "acg1", "bcg")}
    ly, ls, lg, te, ve, se, ge, cc, so = (
        sems[n] for n in ("ly", "ls", "lg", "te", "ve", "se", "ge", "cc", "so"))
    xw, stl, agst, bcl, yst, crl, mst, a2l, xml = (
        sems[n] for n in ("xw", "stl", "agst", "bcl", "yst", "crl", "mst",
                          "a2l", "xml"))
    dld = [sems["dld0"], sems["dld1"], sems["dld2"], sems["dld3"]]
    crls = [sems["crl0"], sems["crl1"]]
    a2ls = [sems["a2l0"], sems["a2l1"]]
    ygls = [sems["ygl0"], sems["ygl1"]]
    xmls = [sems["xml0"], sems["xml1"]]
    dsg = [sems["dsg0"], sems["dsg1"]]
    acg = [sems["acg0"], sems["acg1"]]
    bcg = sems["bcg"]
    mwl = [sems["mwl0"], sems["mwl1"]]
    wdl = [sems["wdl0"], sems["wdl1"]]
    w1l = [sems["w1l0"], sems["w1l1"], sems["w1l2"], sems["w1l3"]]
    w2l = [sems["w2l0"], sems["w2l1"]]
    waited = {}
    v = EW(v0, ve, waited)
    s = EW(s0, se, waited)

    def wv(eng, sem, val=None):
        val = sem.n if val is None else val
        e0 = eng._e if isinstance(eng, EW) else eng
        key = (id(e0), id(sem))
        if waited.get(key, -1) >= val:
            return
        e0.wait_ge(sem.h, val)
        waited[key] = val

    def ydma(dst, src, sem=None):
        (sem or ly).inc(y.dma_start(out=dst, in_=src), 16)

    def sdma(dst, src, sem=None):
        (sem or ls).inc(s0.dma_start(out=dst, in_=src), 16)

    def gdma(dst, src, sem=None):
        (sem or lg).inc(g.dma_start(out=dst, in_=src), 16)

    g_tw, g_tr = {}, {}

    def gtt(out, in0, in1, op):
        key = (id(g), id(ge))
        if RACECHECK:
            if waited.get(key, -1) < ge.n:
                g.wait_ge(ge.h, ge.n)
                waited[key] = ge.n
        else:
            need = 0
            wtn = _ap_tname(out)
            for r in (in0, in1):
                t2 = _ap_tname(r)
                if t2 is not None:
                    need = max(need, g_tw.get(t2, 0))
            need = max(need, g_tw.get(wtn, 0), g_tr.get(wtn, 0))
            if need > waited.get(key, -1):
                g.wait_ge(ge.h, need)
                waited[key] = need
        r2 = ge.inc(g.tensor_tensor(out=out, in0=in0, in1=in1, op=op))
        if not RACECHECK:
            g_tw[wtn] = ge.n
            for r in (in0, in1):
                t2 = _ap_tname(r)
                if t2 is not None:
                    g_tr[t2] = ge.n
        return r2

    bank_guard = {}

    def bank_wait(*banks):
        for bk in banks:
            for sem, val in bank_guard.get(id(bk), ()):
                wv(t, sem, val)

    def bank_set(bank, sem):
        bank_guard[id(bank)] = ((sem, sem.n),)

    def barrier():
        for eng in (t, v, s, y, g):
            for sem in sems.values():
                if sem.n and sem is not so and sem is not cc:
                    wv(eng, sem)

    ones_bf = sb(gst, "ones_bf", [128, 1], BF16)
    ones_f = sb(gst, "ones_f", [1, 128], F32)
    xn_bf = sb(gst, "xn_bf", [128, 8 * TC], BF16)
    negA = sb(gst, "negA", [128, 64], F32)
    esel = sb(gst, "esel", [128, 16 * 128], BF16)
    states = sb(gst, "states", [128, 64], F32)
    p_ln1g = sb(gst, "p_ln1g", [128, 8], F32)
    p_ln1b = sb(gst, "p_ln1b", [128, 8], F32)
    p_ln2g = sb(gst, "p_ln2g", [128, 8], F32)
    p_ln2b = sb(gst, "p_ln2b", [128, 8], F32)
    p_bd = sb(gst, "p_bd", [128, 8], F32)
    p_b2 = sb(gst, "p_b2", [128, 8], F32)
    p_d = sb(gst, "p_d", [128, 8], F32)
    p_b1 = sb(gst, "p_b1", [128, 32], F32)
    p_bbc = sb(gst, "p_bbc", [32, 1], F32)
    p_cbt = sb(gst, "p_cbt", [32, 512], F32)
    ix_g = sb(gst, "ix_g", [128, 20], U32)
    ix_y = sb(gst, "ix_y", [128, 8], U32)
    alog_sb = sb(gst, "alog_sb", [128, 64], F32)

    ps_a = psum(gst, "ps_a", [128, 512])
    ps_b = psum(gst, "ps_b", [128, 512])
    ps_c = psum(gst, "ps_c", [128, 512])
    ps_dd = psum(gst, "ps_d", [128, 512])

    for dst, src in ((p_ln1g, ln1_g), (p_ln1b, ln1_b), (p_ln2g, ln2_g),
                     (p_ln2b, ln2_b), (p_bd, bd_c), (p_b2, b2_c), (p_d, d_c),
                     (p_b1, b1_c), (p_bbc, bbc_c), (p_cbt, cbt_i),
                     (ix_g, gidx), (ix_y, yidx), (alog_sb, alog2)):
        ydma(dst[:], src.ap())
    gdma(esel[:], esel_i.ap())
    v.memset(ones_bf[:], 1.0)
    v.memset(ones_f[0:1, :], 1.0)
    v.memset(states[:], 0.0)
    wv(s, ly)
    s.activation(out=negA[:], in_=alog_sb[:], func=AFT.Exp)
    wv(v, se)
    v.tensor_scalar(out=negA[:], in0=negA[:], scalar1=-1.0, scalar2=None,
                    op0=OP.mult)

    fer = ExitStack()
    logits32 = sb(fer, "logits32", [32, TC], F32)
    fe = ExitStack()
    v.memset(logits32[:], 0.0)
    fep = ExitStack()
    ps_s = psum(fep, "ps_s", [1, 512])
    ps_q = psum(fep, "ps_q", [1, 512])
    ps_bc = psum(fep, "ps_bc", [32, 512])
    ps_rt = psum(fep, "ps_rt", [8, 512])

    mu_b = sb(fe, "mu_b", [128, TT], F32)
    r_b = sb(fe, "r_b", [128, TT], F32)
    strow = sb(fe, "strow", [1, 3 * TT], F32)
    wdT = sb(fe, "wdT", [128, 8 * H], BF16)
    wbcT = sb(fe, "wbcT", [128, 8 * 32], BF16)
    rwT = sb(fe, "rwT", [128, 8 * NE], BF16)
    for kc in range(8):
        gdma(wdT[:, kc * H:(kc + 1) * H], wd_t.ap()[kc * 128:(kc + 1) * 128, :])
        gdma(wbcT[:, kc * 32:(kc + 1) * 32], wbc_t.ap()[kc * 128:(kc + 1) * 128, :])
        gdma(rwT[:, kc * NE:(kc + 1) * NE], rw_t.ap()[kc * 128:(kc + 1) * 128, :])
    for db, df in ((wg_b, wg_e), (wu_b, wu_e), (wdn_b, wdn_e), (swg_b, swg),
                   (swu_b, swu), (swd_b, swd), (w1_b, w1_t), (w2_b, w2_t)):
        gdma(db.ap(), df.ap())
    lg_precast = lg.n

    xt = sb(fe, "xt", [128, 8 * TT], F32)
    xb_bf = sb(fe, "xb_bf", [128, 8 * TT], BF16)
    xq_bf = sb(fe, "xq_bf", [128, 8 * TT], BF16)
    xn_f = sb(fe, "xn_f", [128, 8 * TT], F32)
    xn2_f = sb(fe, "xn2_f", [128, 8 * TT], F32)
    xn2b = sb(fe, "xn2b", [128, 8 * TT], BF16)
    dstage = sb(fe, "dstage", [128, 2 * TT], F32)
    dxstage = sb(fe, "dxstage", [128, 2 * TT], BF16)
    bcst = sb(fe, "bcst", [32, TT], F32)
    acst = sb(fe, "acst", [128, 2 * TT], F32)

    def ln_stats(tt, src_bf, sq_bf, row, pss, psq, strow, mu_b, r_b,
                 store=False):
        wv(t, ve)
        wv(t, se)
        wv(v, stl)
        bank_wait(pss, psq)
        for hc in range(8):
            t.matmul(pss[:, :], ones_bf[:], src_bf[:, hc * TT:(hc + 1) * TT],
                     start=(hc == 0), stop=(hc == 7))
        for hc in range(8):
            mm = t.matmul(psq[:, :], ones_bf[:], sq_bf[:, hc * TT:(hc + 1) * TT],
                          start=(hc == 0), stop=(hc == 7))
        te.inc(mm)
        wv(v, te)
        v.tensor_scalar(out=strow[0:1, 0:TT], in0=pss[0:1, :],
                        scalar1=1.0 / H, scalar2=None, op0=OP.mult)
        v.tensor_scalar(out=strow[0:1, TT:2 * TT], in0=psq[0:1, :],
                        scalar1=1.0 / H, scalar2=None, op0=OP.mult)
        bank_set(pss, ve)
        bank_set(psq, ve)
        v.tensor_tensor(out=strow[0:1, 2 * TT:3 * TT], in0=strow[0:1, 0:TT],
                        in1=strow[0:1, 0:TT], op=OP.mult)
        v.tensor_tensor(out=strow[0:1, 2 * TT:3 * TT],
                        in0=strow[0:1, 2 * TT:3 * TT],
                        in1=strow[0:1, TT:2 * TT], op=OP.subtract)
        v.tensor_scalar(out=strow[0:1, 2 * TT:3 * TT],
                        in0=strow[0:1, 2 * TT:3 * TT], scalar1=-1.0,
                        scalar2=EPS, op0=OP.mult, op1=OP.add)
        wv(s, ve)
        s.activation(out=strow[0:1, TT:2 * TT], in_=strow[0:1, 2 * TT:3 * TT],
                     func=AFT.Ln)
        s.activation(out=strow[0:1, TT:2 * TT], in_=strow[0:1, TT:2 * TT],
                     func=AFT.Exp, scale=-0.5)
        if store:
            wv(y, se)
            wv(y, ve)
            ydma(stats_d.ap()[row:row + 1, tt * TT:(tt + 1) * TT],
                 strow[0:1, 0:TT], stl)
            ydma(stats_d.ap()[row + 1:row + 2, tt * TT:(tt + 1) * TT],
                 strow[0:1, TT:2 * TT], stl)
            return
        bank_wait(ps_c, ps_dd)
        wv(t, se)
        t.matmul(ps_c[:, :], ones_f[0:1, :], strow[0:1, 0:TT],
                 start=True, stop=True)
        mmb = t.matmul(ps_dd[:, :], ones_f[0:1, :], strow[0:1, TT:2 * TT],
                       start=True, stop=True)
        te.inc(mmb)
        wv(s, te)
        s.activation(out=mu_b[:], in_=ps_c[:, :], func=AFT.Identity)
        s.activation(out=r_b[:], in_=ps_dd[:, :], func=AFT.Identity)
        bank_set(ps_c, se)
        bank_set(ps_dd, se)

    for tt in range(NTT):
        wv(y, ve)
        wv(y, se)
        ydma(xt[:], bass.AP(x_fm, tt * TT, [[TC, 128], [128 * TC, 8], [1, TT]]),
             xw)
        wv(s, xw)
        s.activation(out=xq_bf[:], in_=xt[:], func=AFT.Square)
        wv(v, xw)
        v.tensor_copy(out=xb_bf[:], in_=xt[:])
        ln_stats(tt, xb_bf, xq_bf, 0, ps_s, ps_q, strow, mu_b, r_b)
        wv(v, se)
        v.tensor_tensor(out=xn_f[:], in0=xt[:],
                        in1=bass.AP(mu_b, 0, [[512, 128], [0, 8], [1, 512]]),
                        op=OP.subtract)
        v.tensor_tensor(out=xn_f[:], in0=xn_f[:],
                        in1=bass.AP(r_b, 0, [[512, 128], [0, 8], [1, 512]]),
                        op=OP.mult)
        v.tensor_copy(
            out=bass.AP(xn_bf, tt * TT, [[8 * TC, 128], [TC, 8], [1, TT]]),
            in_=xn_f[:])
        wv(s, ve)
        s.activation(out=xq_bf[:], in_=xn_f[:], func=AFT.Square)
        s.activation(out=xb_bf[:], in_=xn_f[:], func=AFT.Identity)
        ln_stats(tt, xb_bf, xq_bf, 2, ps_s, ps_q, strow, mu_b, r_b)
        wv(v, se)
        v.tensor_tensor(out=xn2_f[:], in0=xn_f[:],
                        in1=bass.AP(mu_b, 0, [[512, 128], [0, 8], [1, 512]]),
                        op=OP.subtract)
        v.tensor_tensor(out=xn2_f[:], in0=xn2_f[:],
                        in1=bass.AP(r_b, 0, [[512, 128], [0, 8], [1, 512]]),
                        op=OP.mult)
        wv(s, ve)
        wv(s, te)
        for hc in range(8):
            cs = slice(hc * TT, (hc + 1) * TT)
            s.activation(out=xn2b[:, cs], in_=xn2_f[:, cs], func=AFT.Identity,
                         scale=p_ln1g[:, hc:hc + 1], bias=p_ln1b[:, hc:hc + 1])
        wv(v, se)
        v.tensor_copy(out=xn2_f[:], in_=xn2b[:])
        for hc in range(8):
            cs = slice(hc * TT, (hc + 1) * TT)
            stg = acst[:, (hc % 2) * TT:(hc % 2 + 1) * TT]
            wv(v, acg[hc % 2])
            v.scalar_tensor_tensor(out=stg, in0=xn2_f[:, cs],
                                   scalar=p_d[:, hc:hc + 1], in1=xn_f[:, cs],
                                   op0=OP.mult, op1=OP.add)
            wv(y, ve)
            ydma(accin_d.ap()[hc * 128:(hc + 1) * 128, tt * TT:(tt + 1) * TT],
                 stg, acg[hc % 2])
        wv(t, lg)
        wv(t, se)
        for mo in range(8):
            pd = ps_a if (mo % 2 == 0) else ps_b
            dstg = dstage[:, (mo % 2) * TT:(mo % 2 + 1) * TT]
            dxstg = dxstage[:, (mo % 2) * TT:(mo % 2 + 1) * TT]
            bank_wait(pd)
            for kc in range(8):
                mm = t.matmul(pd[:, :],
                              wdT[:, kc * H + mo * 128: kc * H + (mo + 1) * 128],
                              xn2b[:, kc * TT:(kc + 1) * TT],
                              start=(kc == 0), stop=(kc == 7))
            te.inc(mm)
            wv(s, te)
            wv(s, dsg[mo % 2])
            s.activation(out=dstg, in_=pd[:, :], func=AFT.Abs,
                         bias=p_bd[:, mo:mo + 1])
            s.activation(out=dstg, in_=dstg, func=AFT.Exp, scale=-1.0)
            s.activation(out=dstg, in_=dstg, func=AFT.Ln, bias=1.0)
            s.activation(out=dxstg, in_=pd[:, :], func=AFT.Relu,
                         bias=p_bd[:, mo:mo + 1])
            bank_set(pd, se)
            wv(v, se)
            wv(v, dsg[mo % 2])
            v.tensor_tensor(out=dstg, in0=dstg, in1=dxstg, op=OP.add)
            v.tensor_tensor(out=dxstg, in0=dstg,
                            in1=xn2_f[:, mo * TT:(mo + 1) * TT], op=OP.mult)
            wv(y, ve)
            ydma(ag1a_in.ap()[mo * 128:(mo + 1) * 128, tt * TT:(tt + 1) * TT],
                 dstg, dsg[mo % 2])
            ydma(ag1b_in.ap()[mo * 128:(mo + 1) * 128, tt * TT:(tt + 1) * TT],
                 dxstg, dsg[mo % 2])
        bank_wait(ps_bc)
        for kc in range(8):
            mm = t.matmul(ps_bc[:, :], wbcT[:, kc * 32:(kc + 1) * 32],
                          xn2b[:, kc * TT:(kc + 1) * TT],
                          start=(kc == 0), stop=(kc == 7))
        te.inc(mm)
        wv(v, te)
        wv(v, bcg)
        v.tensor_scalar(out=bcst[:], in0=ps_bc[:, :], scalar1=p_bbc[:, 0:1],
                        scalar2=None, op0=OP.add)
        bank_set(ps_bc, ve)
        wv(y, ve)
        ydma(ag1a_in.ap()[H:H + 32, tt * TT:(tt + 1) * TT], bcst[:], bcg)
        bank_wait(ps_rt)
        for kc in range(8):
            mm = t.matmul(ps_rt[:, :], rwT[:, kc * NE:(kc + 1) * NE],
                          xn_bf[:, kc * TC + tt * TT: kc * TC + (tt + 1) * TT],
                          start=(kc == 0), stop=(kc == 7))
        te.inc(mm)
        wv(v, te)
        v.tensor_copy(out=logits32[0:8, tt * TT:(tt + 1) * TT], in_=ps_rt[:, :])
        bank_set(ps_rt, ve)

    if "xn_bf" in dbg_t:
        wv(y, ve)
        for hc in range(8):
            ydma(dbg_t["xn_bf"].ap()[hc * 128:(hc + 1) * 128, :],
                 xn_bf[:, hc * TC:(hc + 1) * TC])
    if "logits" in dbg_t:
        wv(y, ve)
        ydma(dbg_t["logits"].ap()[:, :], logits32[0:8, :])
    if "accin" in dbg_t:
        wv(g, ly)
        wv(g, lg)
        gdma(dbg_t["accin"].ap()[:, :], accin_d.ap()[:, :])
    if "delta" in dbg_t:
        wv(g, ly)
        wv(g, lg)
        gdma(dbg_t["delta"].ap()[:, :], ag1a_in.ap()[0:H, :])

    for sem_ in (dsg[0], dsg[1], acg[0], acg[1], bcg):
        wv(g, sem_)
    cc.inc(g.collective_compute("AllGather", OP.bypass, replica_groups=RG8,
                                ins=[ag1a_in.ap()], outs=[ag1a_out.ap()]))
    cc.inc(g.collective_compute("AllGather", OP.bypass, replica_groups=RG8,
                                ins=[ag1b_in.ap()], outs=[ag1b_out.ap()]))
    barrier()
    fe.close()
    fep.close()

    ro = ExitStack()
    R32 = sb(ro, "R32", [32, 64 * 32], F32)
    sco = sb(ro, "sco", [32, 512], F32)
    msc = sb(ro, "msc", [32, 512], F32)
    gtmp = sb(ro, "gtmp", [32, 128], F32)
    gtmp2 = sb(ro, "gtmp2", [32, 128], F32)
    ge0 = sb(ro, "ge0", [32, 64], F32)
    ge1 = sb(ro, "ge1", [32, 64], F32)
    m1c = sb(ro, "m1c", [32, 64], F32)
    m1r = sb(ro, "m1r", [32, 512], F32)
    f1 = sb(ro, "f1", [32, 512], F32)
    f2 = sb(ro, "f2", [32, 512], F32)
    cum = sb(ro, "cum", [32, 512], F32)
    cum2 = sb(ro, "cum2", [32, 512], F32)
    wsc = sb(ro, "wsc", [32, 64], F32)
    w1c = sb(ro, "w1c", [32, 64], F32)
    w2c = sb(ro, "w2c", [32, 64], F32)
    prodw = sb(ro, "prodw", [32, 512], F32)
    padbuf = sb(ro, "padbuf", [32, 64 * 32], F32)
    combT = sb(ro, "combT", [32, TC], F32)

    def rap(tsr, fsz, off, dims, np_=32):
        return bass.AP(tsr, off, [[fsz, np_]] + dims)

    for j in range(64):
        v.transpose(out=R32[0:32, j * 32:(j + 1) * 32],
                    in_=logits32[0:32, j * 32:(j + 1) * 32])
    wv(s, ve)
    s.activation(out=R32[:], in_=R32[:], func=AFT.Tanh, scale=0.5)
    wv(v, se)
    v.tensor_scalar(out=rap(sco, 512, 0, [[8, 64], [1, 8]]),
                    in0=rap(R32, 64 * 32, 0, [[32, 64], [1, 8]]),
                    scalar1=0.5, scalar2=0.5, op0=OP.mult, op1=OP.add)
    wv(v, ly)
    v.tensor_tensor(out=msc[:], in0=sco[:], in1=p_cbt[:], op=OP.add)
    for idx, (i, j) in enumerate([(0, 1), (0, 2), (0, 3), (1, 2), (1, 3),
                                  (2, 3)]):
        dst = gtmp if idx == 0 else gtmp2
        v.tensor_tensor(out=dst[:], in0=rap(msc, 512, i, [[4, 128]]),
                        in1=rap(msc, 512, j, [[4, 128]]), op=OP.add)
        if idx > 0:
            v.tensor_tensor(out=gtmp[:], in0=gtmp[:], in1=gtmp2[:], op=OP.max)
    v.tensor_tensor(out=ge0[:], in0=rap(gtmp, 128, 0, [[2, 64]]),
                    in1=rap(gtmp, 128, 1, [[2, 64]]), op=OP.is_ge)
    v.tensor_tensor(out=ge1[:], in0=rap(gtmp, 128, 1, [[2, 64]]),
                    in1=rap(gtmp, 128, 0, [[2, 64]]), op=OP.is_gt)
    for e in range(4):
        v.tensor_copy(out=rap(m1r, 512, e, [[8, 64]]), in_=ge0[:])
        v.tensor_copy(out=rap(m1r, 512, 4 + e, [[8, 64]]), in_=ge1[:])
    v.tensor_tensor(out=msc[:], in0=msc[:], in1=m1r[:], op=OP.mult)

    def top_select(fdst):
        v.tensor_reduce(out=m1c[:], in_=rap(msc, 512, 0, [[8, 64], [1, 8]]),
                        axis=AXX, op=OP.max)
        for e in range(8):
            v.tensor_copy(out=rap(m1r, 512, e, [[8, 64]]), in_=m1c[:])
        v.tensor_tensor(out=fdst[:], in0=msc[:], in1=m1r[:], op=OP.is_ge)
        v.tensor_copy(out=cum[:], in_=fdst[:])
        src, dst = cum, cum2
        for shd in (1, 2, 4):
            v.tensor_copy(out=dst[:], in_=src[:])
            v.tensor_tensor(
                out=rap(dst, 512, shd, [[8, 64], [1, 8 - shd]]),
                in0=rap(src, 512, shd, [[8, 64], [1, 8 - shd]]),
                in1=rap(src, 512, 0, [[8, 64], [1, 8 - shd]]), op=OP.add)
            src, dst = dst, src
        v.tensor_scalar(out=dst[:], in0=src[:], scalar1=1.0, scalar2=None,
                        op0=OP.is_le)
        v.tensor_tensor(out=fdst[:], in0=fdst[:], in1=dst[:], op=OP.mult)

    top_select(f1)
    v.tensor_tensor(out=prodw[:], in0=f1[:], in1=sco[:], op=OP.mult)
    v.tensor_reduce(out=w1c[:], in_=rap(prodw, 512, 0, [[8, 64], [1, 8]]),
                    axis=AXX, op=OP.add)
    v.scalar_tensor_tensor(out=msc[:], in0=f1[:], scalar=BIG_NEG, in1=msc[:],
                           op0=OP.mult, op1=OP.add)
    top_select(f2)
    v.tensor_tensor(out=prodw[:], in0=f2[:], in1=sco[:], op=OP.mult)
    v.tensor_reduce(out=w2c[:], in_=rap(prodw, 512, 0, [[8, 64], [1, 8]]),
                    axis=AXX, op=OP.add)
    v.tensor_tensor(out=wsc[:], in0=w1c[:], in1=w2c[:], op=OP.add)
    v.tensor_scalar(out=wsc[:], in0=wsc[:], scalar1=2.0, scalar2=2e-20,
                    op0=OP.mult, op1=OP.add)
    v.reciprocal(out=wsc[:], in_=wsc[:])
    for e in range(8):
        v.tensor_copy(out=rap(m1r, 512, e, [[8, 64]]), in_=w1c[:])
    v.tensor_tensor(out=f1[:], in0=f1[:], in1=m1r[:], op=OP.mult)
    for e in range(8):
        v.tensor_copy(out=rap(m1r, 512, e, [[8, 64]]), in_=w2c[:])
    v.tensor_tensor(out=f2[:], in0=f2[:], in1=m1r[:], op=OP.mult)
    v.tensor_tensor(out=f1[:], in0=f1[:], in1=f2[:], op=OP.add)
    for e in range(8):
        v.tensor_copy(out=rap(m1r, 512, e, [[8, 64]]), in_=wsc[:])
    v.tensor_tensor(out=f1[:], in0=f1[:], in1=m1r[:], op=OP.mult)
    v.memset(padbuf[:], 0.0)
    v.tensor_copy(out=rap(padbuf, 64 * 32, 0, [[32, 64], [1, 8]]),
                  in_=rap(f1, 512, 0, [[8, 64], [1, 8]]))
    for j in range(64):
        v.transpose(out=combT[0:32, j * 32:(j + 1) * 32],
                    in_=padbuf[0:32, j * 32:(j + 1) * 32])
    wv(g, ve)
    gdma(comb_d.ap()[:, :], combT[0:8, :])
    lg_comb = lg.n
    if "comb" in dbg_t:
        wv(g, lg)
        gdma(dbg_t["comb"].ap()[:, :], comb_d.ap()[:, :])
    barrier()
    ro.close()
    fer.close()

    sm = ExitStack()
    pro = ExitStack()
    gstg = sb(pro, "gstg", [128, TC], F32)
    gstgb = sb(pro, "gstgb", [128, TC], BF16)
    bcs = sb(pro, "bcs", [32, 2 * TC], F32)
    wv(g, cc, 2)
    for sh in range(2):
        lg.inc(g.indirect_dma_start(
            out=bcs[0:32, sh * TC:(sh + 1) * TC], out_offset=None,
            in_=bass.AP(ag1a_out, 0, [[TC, 8 * AG1R], [1, TC]]),
            in_offset=bass.IndirectOffsetOnAxis(
                ix_g[0:32, 16 + sh:17 + sh], 0)), 16)
    wv(g, lg)
    for sh in range(2):
        gdma(bc_seq.ap()[:, sh * TC:(sh + 1) * TC],
             bcs[0:32, sh * TC:(sh + 1) * TC])
    for sh in range(2):
        for c in range(4):
            wv(g, lg)
            lg.inc(g.indirect_dma_start(
                out=gstg[:, :], out_offset=None,
                in_=bass.AP(ag1a_out, 0, [[TC, 8 * AG1R], [1, TC]]),
                in_offset=bass.IndirectOffsetOnAxis(
                    ix_g[:, sh * 4 + c: sh * 4 + c + 1], 0)), 16)
            wv(g, lg)
            gdma(dloc.ap()[c * 128:(c + 1) * 128, sh * TC:(sh + 1) * TC],
                 gstg[:, :])
    for sh in range(2):
        for c in range(4):
            wv(g, lg)
            lg.inc(g.indirect_dma_start(
                out=gstgb[:, :], out_offset=None,
                in_=bass.AP(ag1b_out, 0, [[TC, 8 * H], [1, TC]]),
                in_offset=bass.IndirectOffsetOnAxis(
                    ix_g[:, 8 + sh * 4 + c: 8 + sh * 4 + c + 1], 0)), 16)
            wv(g, lg)
            gdma(dxloc.ap()[c * 128:(c + 1) * 128, sh * TC:(sh + 1) * TC],
                 gstgb[:, :])
    if "bcv" in dbg_t:
        wv(g, lg)
        gdma(dbg_t["bcv"].ap()[:, :], bc_seq.ap()[:, :])
    pro.close()
    drep = sb(sm, "drep", [128, 4 * TQ], BF16)
    dxrep = sb(sm, "dxrep", [128, 4 * TQ], BF16)
    brepq = sb(sm, "brepq", [128, 2 * TQ], BF16)
    crepq = sb(sm, "crepq", [128, 2 * TQ], BF16)
    dAt = sb(sm, "dAt", [128, 4 * TQ], F32)
    dBxt = sb(sm, "dBxt", [128, 4 * TQ], F32)
    Hsb = sb(sm, "Hsb", [128, TQ], BF16)
    prodb = sb(sm, "prodb", [128, 2 * TQ], BF16)
    ystg = sb(sm, "ystg", [128, TQ], F32)
    ps_y = [psum(sm, f"ps_y{i}", [128, 512]) for i in range(4)]

    wgs = sb(sm, "wgs", [128, 2 * 8 * II], BF16)
    wus = sb(sm, "wus", [128, 2 * 8 * II], BF16)
    wdmo = sb(sm, "wdmo", [128, 2 * 4 * H], BF16)
    wsdm = sb(sm, "wsdm", [128, 2 * 4 * 128], BF16)
    h_all = sb(sm, "h_all", [128, NE * 4 * TT], BF16)
    h_sh = sb(sm, "h_sh", [128, 4 * TT], BF16)
    combr = sb(sm, "combr", [128, 2 * TT], BF16)
    g_bf = sb(sm, "g_bf", [128, 2 * TT], BF16)
    u_bf = sb(sm, "u_bf", [128, 2 * TT], BF16)
    tnh = sb(sm, "tnh", [128, 2 * TT], BF16)
    h1t = sb(sm, "h1t", [128, 2 * TT], BF16)
    mstage = sb(sm, "mstage", [128, 2 * TT], F32)

    dA_done, dbx_done, scan_done, prod_done = {}, {}, {}, {}
    drep_ld, dxrep_ld, ymm_done, evac_done, yq_dma = {}, {}, {}, {}, {}
    brep_ld = {}
    ypend = []
    ypend_flush = {}

    def emit_ymm(item):
        u, par, kt = item
        cidx2 = u // 16
        pyA2, pyB2, chunk2, sh2, twp2 = ypend_flush[cidx2]
        wv(t, ve, prod_done[u])
        if kt == 0:
            bank_wait(pyA2, pyB2)
        t.matmul(pyA2[:, :], esel[:, kt * 128:(kt + 1) * 128],
                 bass.AP(prodb, par * TQ, [[2 * TQ, 128], [1, 512]]),
                 start=(kt == 0), stop=(kt == 15))
        mm = t.matmul(pyB2[:, :], esel[:, kt * 128:(kt + 1) * 128],
                      bass.AP(prodb, par * TQ + 512,
                              [[2 * TQ, 128], [1, 512]]),
                      start=(kt == 0), stop=(kt == 15))
        te.inc(mm)
        ymm_done[u] = te.n
        if kt == 15:
            del ypend_flush[cidx2]
            wv(s, te, ymm_done[u])
            wv(s, yst)
            s.activation(out=ystg[:, 0:512], in_=pyA2[:, :], func=AFT.Identity)
            s.activation(out=ystg[:, 512:TQ], in_=pyB2[:, :], func=AFT.Identity)
            bank_set(pyA2, se)
            bank_set(pyB2, se)
            evac_done[cidx2] = se.n
            wv(y, se, evac_done[cidx2])
            wv(y, yst)
            ydma(bass.AP(ag2_in, (sh2 * HOH + chunk2 * 128) * TC + twp2,
                         [[TC, 128], [1, TQ]]),
                 ystg[:, 0:TQ], yst)

    def emit_ssm():
        lg_pro = lg.n
        yield
        def unit_loads(u2):
            if u2 >= 256:
                return
            tq2, tile2 = u2 // 64, u2 % 64
            par2 = u2 % 4
            dsl2 = slice(par2 * TQ, (par2 + 1) * TQ)
            wv(y, se, dA_done.get(u2 - 4, 0))
            wv(y, ge, dbx_done.get(u2 - 4, 0))

            ydma(drep[:, dsl2],
                 bass.AP(dloc, (8 * tile2) * L + tq2 * TQ,
                         [[L, 8], [0, 16], [1, TQ]]), dld[par2])
            ydma(dxrep[:, dsl2],
                 bass.AP(dxloc, (8 * tile2) * L + tq2 * TQ,
                         [[L, 8], [0, 16], [1, TQ]]), dld[par2])

        wv(y, lg, lg_pro)
        for uu in range(3):
            unit_loads(uu)
        for tq in range(4):
            sh, twp = tq // 2, (tq % 2) * TQ
            qpar = tq % 2
            qsl = slice(qpar * TQ, (qpar + 1) * TQ)
            wv(y, ve, scan_done.get(("q", tq - 2), 0))
            wv(y, bcl)
            ydma(brepq[:, qsl], bass.AP(bc_seq, tq * TQ,
                                        [[0, 8], [L, 16], [1, TQ]]), bcl)
            ydma(crepq[:, qsl], bass.AP(bc_seq, 16 * L + tq * TQ,
                                        [[0, 8], [L, 16], [1, TQ]]), bcl)
            for chunk in range(4):
                pyA = ps_y[(chunk % 2) * 2]
                pyB = ps_y[(chunk % 2) * 2 + 1]
                ypend_flush[tq * 4 + chunk] = (pyA, pyB, chunk, sh, twp)
                for kt in range(16):
                    tile = chunk * 16 + kt
                    u = tq * 64 + tile
                    lpar = u % 4
                    par = u % 4
                    lsl = slice(lpar * TQ, (lpar + 1) * TQ)
                    dsl = slice(par * TQ, (par + 1) * TQ)
                    unit_loads(u + 3)
                    wv(s, dld[lpar])
                    wv(s, ve, scan_done.get(u - 4, 0))
                    s.activation(out=dAt[:, dsl], in_=drep[:, lsl],
                                 func=AFT.Exp, scale=negA[:, tile:tile + 1])
                    dA_done[u] = se.n
                    wv(g, dld[lpar])
                    wv(g, bcl)
                    wv(g, ve, scan_done.get(u - 4, 0))
                    gtt(dBxt[:, dsl], dxrep[:, lsl], brepq[:, qsl], OP.mult)
                    dbx_done[u] = ge.n
                    wv(v, se, dA_done[u])
                    wv(v, ge, dbx_done[u])
                    v.tensor_tensor_scan(out=Hsb[:, :], data0=dAt[:, dsl],
                                         data1=dBxt[:, dsl],
                                         initial=states[:, tile:tile + 1],
                                         op0=OP.mult, op1=OP.add)
                    v.tensor_copy(out=states[:, tile:tile + 1],
                                  in_=Hsb[:, TQ - 1:TQ])
                    ppar = u % 2
                    psl = slice(ppar * TQ, (ppar + 1) * TQ)
                    wv(v, te, ymm_done.get(u - 2, 0))
                    wv(v, bcl)
                    v.tensor_tensor(out=prodb[:, psl], in0=Hsb[:, :],
                                    in1=crepq[:, qsl], op=OP.mult)
                    scan_done[u] = ve.n
                    if tile == 63:
                        scan_done[("q", tq)] = ve.n
                    prod_done[u] = ve.n
                    ypend.append((u, u % 2, kt))
                    if len(ypend) > YMM_SKEW:
                        emit_ymm(ypend.pop(0))
                yield
            if tq == 1:
                while ypend:
                    emit_ymm(ypend.pop(0))
                wv(g, yst)
                cc.inc(g.collective_compute(
                    "AllGather", OP.bypass, replica_groups=RG8,
                    ins=[ag2_in.ap()[0:HOH, :]],
                    outs=[ag2_out.ap()[0:8 * HOH, :]]))
        while ypend:
            emit_ymm(ypend.pop(0))
        wv(g, yst)
        cc.inc(g.collective_compute(
            "AllGather", OP.bypass, replica_groups=RG8,
            ins=[ag2_in.ap()[HOH:2 * HOH, :]],
            outs=[ag2_out.ap()[8 * HOH:16 * HOH, :]]))
        if "yssm" in dbg_t:
            wv(g, cc)
            wv(g, ly)
            gdma(dbg_t["yssm"].ap()[:, :], ag2_in.ap()[:, :])
        yield

    gu_evac, h_ready, mo_evac, mo_dma, dn_done, hcons = {}, {}, {}, {}, {}, {}
    ld_w, hr_last = {}, {}
    gu_u = [0]
    gupend, dnpend = [], []
    combr_guard = {}

    def moe_w_load(tt, e):
        ex = tt * 9 + (e + 1)
        slot = ex % 2
        ydma(wgs[:, slot * 8 * II:(slot + 1) * 8 * II],
             bass.AP(wg_b if e >= 0 else swg_b, max(e, 0) * H * II,
                     [[II, 128], [128 * II, 8], [1, II]]), mwl[slot])
        ydma(wus[:, slot * 8 * II:(slot + 1) * 8 * II],
             bass.AP(wu_b if e >= 0 else swu_b, max(e, 0) * H * II,
                     [[II, 128], [128 * II, 8], [1, II]]), mwl[slot])
        ld_w[(tt, e)] = (slot, mwl[slot].n)

    def hr_last_for(tt2, e2):
        if e2 >= -1 and (tt2, e2) in hr_last:
            return hr_last[(tt2, e2)]
        if e2 < -1 and (tt2 - 1, e2 + 9) in hr_last:
            return hr_last[(tt2 - 1, e2 + 9)]
        return 0

    def flush_gu(item):
        tt, e, kc4, pg, pu, ug, mmv, crv, cpar = item
        parq = ug % 2
        csl = slice(parq * TT, (parq + 1) * TT)
        wv(s, te, mmv)
        wv(s, ve, h_ready.get(ug - 2, 0))
        s.activation(out=tnh[:, csl], in_=pg[:, :], func=AFT.Tanh, scale=0.5)
        s.activation(out=g_bf[:, csl], in_=pg[:, :], func=AFT.Identity)
        s.activation(out=u_bf[:, csl], in_=pu[:, :], func=AFT.Identity)
        bank_set(pg, se)
        bank_set(pu, se)
        gu_evac[((tt, e), kc4)] = se.n
        wv(v, se, se.n)
        v.scalar_tensor_tensor(out=h1t[:, csl], in0=tnh[:, csl],
                               scalar=1.0, in1=u_bf[:, csl],
                               op0=OP.add, op1=OP.mult)
        if e < 0:
            wv(v, te, dn_done.get((tt - 1, 7), 0))
            v.scalar_tensor_tensor(
                out=h_sh[:, kc4 * TT:(kc4 + 1) * TT],
                in0=h1t[:, csl], scalar=0.5, in1=g_bf[:, csl],
                op0=OP.mult, op1=OP.mult)
        else:
            wv(v, crls[cpar], crv)
            wv(v, te, dn_done.get((tt - 1, 7), 0))
            v.tensor_tensor(out=h1t[:, csl], in0=h1t[:, csl],
                            in1=g_bf[:, csl], op=OP.mult)
            v.tensor_tensor(
                out=h_all[:, (e * 4 + kc4) * TT:(e * 4 + kc4 + 1) * TT],
                in0=h1t[:, csl],
                in1=combr[:, cpar * TT:(cpar + 1) * TT], op=OP.mult)
            combr_guard[cpar] = ve.n
        h_ready[ug] = ve.n
        hr_last[(tt, e)] = ve.n

    def flush_dn(item):
        tt, mo, pd2 = item
        wv(s, te, dn_done[(tt, mo)])
        wv(s, mst)
        s.activation(out=mstage[:, (mo % 2) * TT:(mo % 2 + 1) * TT],
                     in_=pd2[:, :], func=AFT.Identity)
        bank_set(pd2, se)
        mo_evac[(tt, mo)] = se.n
        wv(y, se, se.n)
        wv(y, mst)
        ydma(moe_d.ap()[mo * 128:(mo + 1) * 128, tt * TT:(tt + 1) * TT],
             mstage[:, (mo % 2) * TT:(mo % 2 + 1) * TT], mst)

    def emit_moe():
        wv(y, lg, lg_precast)
        moe_w_load(0, -1)
        moe_w_load(0, 0)
        for tt in range(NTT):
            for e in range(-1, NE):
                slot = (tt * 9 + e + 1) % 2
                wsl_g = wgs[:, slot * 8 * II:(slot + 1) * 8 * II]
                wsl_u = wus[:, slot * 8 * II:(slot + 1) * 8 * II]
                cpar = (e + 1) % 2
                if e >= 0:
                    wv(y, ve, combr_guard.get(cpar, 0))
                    wv(y, lg, lg_comb)
                    ydma(combr[:, cpar * TT:(cpar + 1) * TT],
                         bass.AP(comb_d, e * TC + tt * TT, [[0, 128], [1, TT]]),
                         crls[cpar])
                    crv = crls[cpar].n
                else:
                    crv = 0
                for kc4 in range(4):
                    pg, pu = (ps_a, ps_b) if kc4 % 2 == 0 else (ps_c, ps_dd)
                    wv(t, mwl[ld_w[(tt, e)][0]], ld_w[(tt, e)][1])
                    bank_wait(pg, pu)
                    for kc in range(8):
                        t.matmul(pg[:, :],
                                 wsl_g[:, kc * II + kc4 * 128:
                                       kc * II + (kc4 + 1) * 128],
                                 xn_bf[:, kc * TC + tt * TT:
                                       kc * TC + (tt + 1) * TT],
                                 start=(kc == 0), stop=(kc == 7))
                    for kc in range(8):
                        mm = t.matmul(pu[:, :],
                                      wsl_u[:, kc * II + kc4 * 128:
                                            kc * II + (kc4 + 1) * 128],
                                      xn_bf[:, kc * TC + tt * TT:
                                            kc * TC + (tt + 1) * TT],
                                      start=(kc == 0), stop=(kc == 7))
                    te.inc(mm)
                    mmv = te.n
                    gu_u[0] += 1
                    ug = gu_u[0]
                    gupend.append((tt, e, kc4, pg, pu, ug, mmv, crv, cpar))
                    if len(gupend) > GU_SKEW:
                        flush_gu(gupend.pop(0))
                    mm_last = te.n
                    yield
                nxt_e, nxt_tt = e + 2, tt
                if nxt_e >= NE:
                    nxt_e -= (NE + 1)
                    nxt_tt += 1
                if nxt_tt < NTT:
                    wv(y, te, mm_last)
                    moe_w_load(nxt_tt, nxt_e)

            while gupend:
                flush_gu(gupend.pop(0))
            for mo in range(8):
                pd2 = ps_a if mo % 2 == 0 else ps_b
                slot = mo % 2
                wv(y, te, dn_done.get((tt, mo - 2), 0))
                ydma(wdmo[:, slot * 4 * H:(slot + 1) * 4 * H],
                     bass.AP(wdn_b, mo * 128,
                             [[H, 128], [II * H, 8], [128 * H, 4], [1, 128]]),
                     wdl[slot])
                ydma(wsdm[:, slot * 4 * 128:(slot + 1) * 4 * 128],
                     bass.AP(swd_b, mo * 128,
                             [[H, 128], [128 * H, 4], [1, 128]]), wdl[slot])
                wv(t, ve, hr_last[(tt, NE - 1)])
                wv(t, ve, hr_last[(tt, -1)])
                wv(t, wdl[slot])
                bank_wait(pd2)
                for e in range(NE):
                    for kc in range(4):
                        t.matmul(pd2[:, :],
                                 wdmo[:, slot * 4 * H + (e * 4 + kc) * 128:
                                       slot * 4 * H + (e * 4 + kc + 1) * 128],
                                 h_all[:, (e * 4 + kc) * TT:
                                       (e * 4 + kc + 1) * TT],
                                 start=(e == 0 and kc == 0), stop=False)
                for kc in range(4):
                    mm = t.matmul(pd2[:, :],
                                  wsdm[:, slot * 4 * 128 + kc * 128:
                                        slot * 4 * 128 + (kc + 1) * 128],
                                  h_sh[:, kc * TT:(kc + 1) * TT],
                                  start=False, stop=(kc == 3))
                te.inc(mm)
                dn_done[(tt, mo)] = te.n
                dnpend.append((tt, mo, pd2))
                if len(dnpend) > DN_SKEW:
                    flush_dn(dnpend.pop(0))
                yield
            while dnpend:
                flush_dn(dnpend.pop(0))
        if "moe" in dbg_t:
            wv(g, mst)
            wv(g, lg)
            gdma(dbg_t["moe"].ap()[:, :], moe_d.ap()[:, :])
        yield

    gen_s = emit_ssm()
    gen_m = emit_moe()
    alive_s, alive_m = True, True
    for _ in range(70):
        try:
            next(gen_m)
        except StopIteration:
            alive_m = False
            break
    next(gen_s)
    while alive_s or alive_m:
        if alive_s:
            try:
                next(gen_s)
            except StopIteration:
                alive_s = False
        if alive_m:
            for _ in range(8):
                try:
                    next(gen_m)
                except StopIteration:
                    alive_m = False
                    break
    barrier()
    sm.close()

    p4 = ExitStack()
    p4p = ExitStack()
    ps_s2 = psum(p4p, "ps_s2", [1, 512])
    ps_q2 = psum(p4p, "ps_q2", [1, 512])
    accf = sb(p4, "accf", [128, 8 * TC], F32)

    p4a = ExitStack()
    mu_b = sb(p4a, "mu_b2", [128, TT], F32)
    r_b = sb(p4a, "r_b2", [128, TT], F32)
    strow = sb(p4a, "strow2", [1, 3 * TT], F32)
    ystage = sb(p4a, "ystage", [128, 2 * TT], F32)
    astage = sb(p4a, "astage", [128, 2 * TT], F32)
    ac_bf = sb(p4a, "ac_bf", [128, 8 * TT], BF16)
    aq_bf = sb(p4a, "aq_bf", [128, 8 * TT], BF16)

    yg_cons, hff_done, ots_done = {}, {}, {}
    w1ld, w2ld = {}, {}

    wv(g, cc, 4)
    ag2av = bass.AP(ag2_out, 0, [[TT, 8 * 2 * HOH * 4], [1, TT]])
    for tt in range(NTT):
        for hc in range(8):
            uy = tt * 8 + hc
            ypar = hc % 2
            wv(g, ve, yg_cons.get(uy - 2, 0))
            ygls[ypar].inc(g.indirect_dma_start(
                out=ystage[:, ypar * TT:(ypar + 1) * TT], out_offset=None,
                in_=ag2av,
                in_offset=bass.IndirectOffsetOnAxis(ix_y[:, hc:hc + 1], 0),
                element_offset=tt * TT), 16)
            wv(y, ve, yg_cons.get(uy - 2, 0))
            ydma(astage[:, ypar * TT:(ypar + 1) * TT],
                 accin_d.ap()[hc * 128:(hc + 1) * 128, tt * TT:(tt + 1) * TT],
                 a2ls[ypar])
            wv(v, ygls[ypar])
            wv(v, a2ls[ypar])
            v.tensor_tensor(
                out=accf[:, hc * TC + tt * TT: hc * TC + (tt + 1) * TT],
                in0=astage[:, ypar * TT:(ypar + 1) * TT],
                in1=ystage[:, ypar * TT:(ypar + 1) * TT], op=OP.add)
            yg_cons[uy] = ve.n
    if "accf" in dbg_t:
        wv(y, ve)
        for hc in range(8):
            ydma(dbg_t["accf"].ap()[hc * 128:(hc + 1) * 128, :],
                 accf[:, hc * TC:(hc + 1) * TC])
    for tt in range(NTT):
        wv(s, ve)
        for hc in range(8):
            s.activation(out=aq_bf[:, hc * TT:(hc + 1) * TT],
                         in_=accf[:, hc * TC + tt * TT: hc * TC + (tt + 1) * TT],
                         func=AFT.Square)
            s.activation(out=ac_bf[:, hc * TT:(hc + 1) * TT],
                         in_=accf[:, hc * TC + tt * TT: hc * TC + (tt + 1) * TT],
                         func=AFT.Identity)
        ln_stats(tt, ac_bf, aq_bf, 4, ps_s2, ps_q2, strow, mu_b, r_b,
                 store=True)
    barrier()
    p4a.close()
    nstage = sb(p4, "nstage", [128, 8 * TT], BF16)
    hff = sb(p4, "hff", [128, 32 * TT], BF16)
    w1s = sb(p4, "w1s", [128, 4 * 8 * 128], BF16)
    w2s = sb(p4, "w2s", [128, 2 * 32 * 128], BF16)
    xts = sb(p4, "xts", [128, 2 * TT], F32)
    mts = sb(p4, "mts", [128, 2 * TT], F32)
    ots = sb(p4, "ots", [128, 2 * TT], F32)
    aplybuf = sb(p4, "aplybuf", [128, 4 * TT], F32)
    mu_b4 = sb(p4, "mu_b4", [128, 4 * TT], F32)
    r_b4 = sb(p4, "r_b4", [128, 4 * TT], F32)
    wv(y, stl)
    for tt in range(NTT):
        ydma(mu_b4[:, tt * TT:(tt + 1) * TT],
             bass.AP(stats_d, 4 * TC + tt * TT, [[0, 128], [1, TT]]), stl)
        ydma(r_b4[:, tt * TT:(tt + 1) * TT],
             bass.AP(stats_d, 5 * TC + tt * TT, [[0, 128], [1, TT]]), stl)
    stats_rdy = stl.n

    def w1_load(mf):
        slot = mf % 4
        sdma(w1s[:, slot * 8 * 128:(slot + 1) * 8 * 128],
             bass.AP(w1_b, mf * 128, [[FF, 128], [128 * FF, 8], [1, 128]]),
             w1l[slot])
        w1ld[mf] = (slot, w1l[slot].n)

    def w2_load(mo):
        slot = mo % 2
        ydma(w2s[:, slot * 32 * 128:(slot + 1) * 32 * 128],
             bass.AP(w2_b, mo * 128, [[H, 128], [128 * H, 32], [1, 128]]),
             w2l[slot])
        w2ld[mo] = (slot, w2l[slot].n)

    for mf in range(4):
        w1_load(mf)
    w2_load(0)
    w2_load(1)
    for tt in range(NTT):
        wv(s, te)
        wv(v, stl, stats_rdy)
        for gh in range(2):
            wv(v, se)
            v.tensor_tensor(out=aplybuf[:],
                            in0=bass.AP(accf, gh * 4 * TC + tt * TT,
                                        [[8 * TC, 128], [TC, 4], [1, TT]]),
                            in1=bass.AP(mu_b4, tt * TT,
                                        [[4 * TT, 128], [0, 4], [1, TT]]),
                            op=OP.subtract)
            v.tensor_tensor(out=aplybuf[:], in0=aplybuf[:],
                            in1=bass.AP(r_b4, tt * TT,
                                        [[4 * TT, 128], [0, 4], [1, TT]]),
                            op=OP.mult)
            wv(s, ve)
            for hc4 in range(4):
                hc = gh * 4 + hc4
                s.activation(out=nstage[:, hc * TT:(hc + 1) * TT],
                             in_=aplybuf[:, hc4 * TT:(hc4 + 1) * TT],
                             func=AFT.Identity, scale=p_ln2g[:, hc:hc + 1],
                             bias=p_ln2b[:, hc:hc + 1])
        nsv = se.n
        for mf in range(32):
            pd3 = (ps_a, ps_b, ps_c, ps_dd)[mf % 4]
            slot = mf % 4
            wsl = w1s[:, slot * 8 * 128:(slot + 1) * 8 * 128]
            wv(t, w1l[w1ld[mf][0]], w1ld[mf][1])
            wv(t, se, nsv)
            bank_wait(pd3)
            for kc in range(8):
                mm = t.matmul(pd3[:, :], wsl[:, kc * 128:(kc + 1) * 128],
                              nstage[:, kc * TT:(kc + 1) * TT],
                              start=(kc == 0), stop=(kc == 7))
            te.inc(mm)
            wv(s, te, te.n)
            s.activation(out=hff[:, mf * TT:(mf + 1) * TT], in_=pd3[:, :],
                         func=AFT.Gelu, bias=p_b1[:, mf:mf + 1])
            bank_set(pd3, se)
            hff_done[mf] = se.n
            wv(s0, te, te.n)
            nxt = mf + 4
            if nxt < 32:
                w1_load(nxt)
            elif tt + 1 < NTT:
                w1_load(nxt - 32)
        for mo in range(8):
            pd4 = (ps_a, ps_b, ps_c, ps_dd)[mo % 4]
            slot = mo % 2
            wsl2 = w2s[:, slot * 32 * 128:(slot + 1) * 32 * 128]
            wv(t, w2l[w2ld[mo][0]], w2ld[mo][1])
            wv(t, se, hff_done[31])
            bank_wait(pd4)
            for kc in range(32):
                mm = t.matmul(pd4[:, :], wsl2[:, kc * 128:(kc + 1) * 128],
                              hff[:, kc * TT:(kc + 1) * TT],
                              start=(kc == 0), stop=(kc == 31))
            te.inc(mm)
            w2v = te.n
            opar = mo % 2
            osl = slice(opar * TT, (opar + 1) * TT)
            wv(y, ve, ots_done.get(mo - 2, 0))
            ydma(xts[:, osl], bass.AP(x_fm, mo * 128 * TC + tt * TT,
                                      [[TC, 128], [1, TT]]), xmls[opar])
            ydma(mts[:, osl],
                 moe_d.ap()[mo * 128:(mo + 1) * 128, tt * TT:(tt + 1) * TT],
                 xmls[opar])
            wv(v, te, w2v)
            wv(v, xmls[opar])
            wv(v, so, so.n)
            v.scalar_tensor_tensor(
                out=ots[:, osl], in0=pd4[:, :], scalar=p_b2[:, mo:mo + 1],
                in1=accf[:, mo * TC + tt * TT: mo * TC + (tt + 1) * TT],
                op0=OP.add, op1=OP.add)
            bank_set(pd4, ve)
            if "attno" in dbg_t:
                wv(y, ve)
                ydma(dbg_t["attno"].ap()[mo * 128:(mo + 1) * 128,
                                         tt * TT:(tt + 1) * TT], ots[:, osl])
                wv(v, ly)
            v.tensor_tensor(out=ots[:, osl], in0=ots[:, osl], in1=xts[:, osl],
                            op=OP.add)
            v.tensor_tensor(out=ots[:, osl], in0=ots[:, osl], in1=mts[:, osl],
                            op=OP.add)
            ots_done[mo] = ve.n
            wv(y, ve, ve.n)
            so.inc(y.dma_start(out=out_t.ap()[mo * 128:(mo + 1) * 128,
                                              tt * TT:(tt + 1) * TT],
                               in_=ots[:, osl]), 16)
            wv(y, te, w2v)
            if mo + 2 < 8:
                w2_load(mo + 2)
            elif tt + 1 < NTT:
                w2_load(mo + 2 - 8)
        ots_done.clear()
        hff_done.clear()
    y.wait_ge(so.h, so.n)
    barrier()
    p4.close()
    p4p.close()
    gst.close()
    return nc



def _prep_core(inputs, k):
    q = k % 2
    b = k // 2
    perm = np.arange(H) if q == 0 else np.r_[np.arange(512, 1024), np.arange(512)]
    asc = np.ascontiguousarray
    f32 = np.float32

    x = np.asarray(inputs["L0_attn_x"])
    xk = x[b, q * TC:(q + 1) * TC, :]
    d = {"x_fm": asc(xk.T[perm])}

    Wd = np.asarray(inputs["L0_attn_Wd"])
    d["wd_t"] = asc(Wd[np.ix_(perm, perm)].T)
    d["wbc_t"] = asc(np.asarray(inputs["L0_attn_Wbc"])[:, perm].T)
    d["w1_t"] = asc(np.asarray(inputs["L0_attn_W1"])[:, perm].T)
    d["w2_t"] = asc(np.asarray(inputs["L0_attn_W2"])[perm].T)
    d["rw_t"] = asc(np.asarray(inputs["L0_ffn_router_weight"])[:, perm].T)
    d["wg_e"] = asc(np.asarray(inputs["L0_ffn_w_gate"])[:, perm, :])
    d["wu_e"] = asc(np.asarray(inputs["L0_ffn_w_up"])[:, perm, :])
    d["wdn_e"] = asc(np.asarray(inputs["L0_ffn_w_down"])[:, :, perm])
    d["swg"] = asc(np.asarray(inputs["L0_ffn_sw_gate"])[perm])
    d["swu"] = asc(np.asarray(inputs["L0_ffn_sw_up"])[perm])
    d["swd"] = asc(np.asarray(inputs["L0_ffn_sw_down"])[:, perm])

    def col8(vv):
        return asc(np.asarray(vv)[perm].reshape(8, 128).T.astype(f32))

    d["ln1_g"] = col8(inputs["L0_attn_ln1_g"])
    d["ln1_b"] = col8(inputs["L0_attn_ln1_b"])
    d["ln2_g"] = col8(inputs["L0_attn_ln2_g"])
    d["ln2_b"] = col8(inputs["L0_attn_ln2_b"])
    d["bd_c"] = col8(inputs["L0_attn_bd"])
    d["b2_c"] = col8(inputs["L0_attn_b2"])
    d["d_c"] = col8(inputs["L0_attn_D"])
    d["b1_c"] = asc(np.asarray(inputs["L0_attn_b1"]).reshape(32, 128).T
                    .astype(f32))
    d["bbc_c"] = asc(np.asarray(inputs["L0_attn_bbc"]).reshape(32, 1)
                     .astype(f32))
    alog = np.asarray(inputs["L0_attn_A_log"])[perm[:HOH]]
    a2 = np.empty((128, 64), f32)
    for tile in range(64):
        for hl in range(8):
            for n in range(NS):
                a2[hl * NS + n, tile] = alog[tile * 8 + hl, n]
    d["alog2"] = asc(a2)
    es = np.zeros((128, 16 * 128), f32)
    for ktile in range(16):
        for p in range(128):
            es[p, ktile * 128 + 8 * ktile + p // NS] = 1.0
    d["esel_i"] = asc(es)
    cb = np.asarray(inputs["L0_ffn_correction_bias"]).astype(f32)
    d["cbt_i"] = asc(np.tile(np.tile(cb, 64), (32, 1)))

    gi = np.zeros((128, 20), np.uint32)
    for sh in range(2):
        src = 2 * b + sh
        for c in range(4):
            p = np.arange(128)
            i = c * 128 + p
            gl = q * 512 + i
            pos = gl if sh == 0 else (gl + 512) % H
            gi[:, sh * 4 + c] = src * AG1R + pos
            gi[:, 8 + sh * 4 + c] = src * H + pos
        gi[0:32, 16 + sh] = src * AG1R + H + np.arange(32)
    d["gidx"] = gi
    yi = np.zeros((128, 8), np.uint32)
    for hc in range(8):
        p = np.arange(128)
        i = hc * 128 + p
        row = np.where(i < 512,
                       q * 8 * HOH + k * HOH + i,
                       q * 8 * HOH + (k ^ 1) * HOH + (i - 512))
        yi[:, hc] = row * 4
    d["yidx"] = yi
    d = {kk: (vv if vv.dtype == np.uint32 else vv.astype(np.float32))
         for kk, vv in d.items()}
    return d, perm


LAST_RESULTS = None


def kernel(**inputs):
    global LAST_RESULTS
    in_maps = []
    perms = []
    for k in range(8):
        m, perm = _prep_core(inputs, k)
        in_maps.append(m)
        perms.append(perm)
    nc = build_program(dbg=KDBG)
    res = run_bass_kernel_spmd(nc, in_maps, core_ids=list(range(8)))
    LAST_RESULTS = res.results
    out = np.empty((4, L, H), np.float32)
    for k in range(8):
        q, b = k % 2, k // 2
        o_fm = res.results[k]["out"]
        o_nat = np.empty_like(o_fm)
        o_nat[perms[k]] = o_fm
        out[b, q * TC:(q + 1) * TC, :] = o_nat.T
    return out


if __name__ == "__main__":
    nc = build_program()
    print("built OK")


# version 14
# speedup vs baseline: 1.1138x; 1.0195x over previous
import os
from contextlib import ExitStack

import numpy as np
import ml_dtypes
import concourse.bass as bass
import concourse.mybir as mybir
from concourse.bass_utils import run_bass_kernel_spmd
from concourse.alu_op_type import AluOpType as OP

F32 = mybir.dt.float32
BF16 = mybir.dt.bfloat16
U32 = mybir.dt.uint32
AFT = mybir.ActivationFunctionType
AXX = mybir.AxisListType.X

H = 1024
NS = 16
FF = 4096
II = 512
NE = 8
TC = 2048
TT = 512
NTT = TC // TT
L = 4096
EPS = 1e-5
HOH = 512
AG1R = H + 32
TQ = 1024
BIG_NEG = -1e30
RG8 = [[0, 1, 2, 3, 4, 5, 6, 7]]

KDBG = ()
RACECHECK = bool(os.environ.get("KERNEL_RACECHECK"))
GU_SKEW = 1
DN_SKEW = 1
YMM_SKEW = 1


class Sem:
    def __init__(self, st, nc, name):
        self.h = st.enter_context(nc.semaphore(name))
        self.n = 0

    def inc(self, inst, k=1):
        inst.then_inc(self.h, k)
        self.n += k
        return inst


def _ap_tname(x):
    try:
        t = x.tensor
        return getattr(t, "name", None) or str(id(t))
    except Exception:
        return None


class EW:
    OPS = ("tensor_tensor", "tensor_scalar", "scalar_tensor_tensor",
           "tensor_copy", "tensor_reduce", "reciprocal", "memset",
           "transpose", "tensor_tensor_scan", "activation", "iota")
    RKEYS = ("in_", "in0", "in1", "data0", "data1", "initial", "scalar",
             "scalar1", "scalar2", "bias", "scale", "mask")

    def __init__(self, eng, sem, waited):
        self._e = eng
        self._s = sem
        self._w = waited
        self._tw = {}
        self._tr = {}

    def __getattr__(self, nm):
        f = getattr(self._e, nm)
        if nm in EW.OPS:
            def wrapped(*a, **k):
                if RACECHECK:
                    key = (id(self._e), id(self._s))
                    if self._w.get(key, -1) < self._s.n:
                        self._e.wait_ge(self._s.h, self._s.n)
                        self._w[key] = self._s.n
                else:
                    wt = k.get("out")
                    if wt is None and nm == "memset" and a:
                        wt = a[0]
                    rd = [k[r] for r in EW.RKEYS if r in k]
                    need = 0
                    wtn = _ap_tname(wt) if wt is not None else None
                    for r in rd:
                        t2 = _ap_tname(r)
                        if t2 is not None:
                            need = max(need, self._tw.get(t2, 0))
                    if wtn is not None:
                        need = max(need, self._tw.get(wtn, 0),
                                   self._tr.get(wtn, 0))
                    key = (id(self._e), id(self._s))
                    if need > self._w.get(key, -1):
                        self._e.wait_ge(self._s.h, need)
                        self._w[key] = need
                r2 = self._s.inc(f(*a, **k))
                if not RACECHECK:
                    if wtn is not None:
                        self._tw[wtn] = self._s.n
                    for r in rd:
                        t2 = _ap_tname(r)
                        if t2 is not None:
                            self._tr[t2] = self._s.n
                return r2
            return wrapped
        return f


def build_program(dbg=()):
    nc = bass.Bass("TRN2", target_bir_lowering=False,
                   detect_race_conditions=RACECHECK)
    t, v0, s0, y, g = nc.tensor, nc.vector, nc.scalar, nc.sync, nc.gpsimd

    def din(name, shape, dt=F32):
        return nc.dram_tensor(name, shape, dt, kind="ExternalInput")

    x_fm = din("x_fm", [H, TC])
    wd_t = din("wd_t", [H, H])
    wbc_t = din("wbc_t", [H, 32])
    w1_t = din("w1_t", [H, FF])
    w2_t = din("w2_t", [FF, H])
    rw_t = din("rw_t", [H, NE])
    wg_e = din("wg_e", [NE, H, II])
    wu_e = din("wu_e", [NE, H, II])
    wdn_e = din("wdn_e", [NE, II, H])
    swg = din("swg", [H, II])
    swu = din("swu", [H, II])
    swd = din("swd", [II, H])
    ln1_g = din("ln1_g", [128, 8])
    ln1_b = din("ln1_b", [128, 8])
    ln2_g = din("ln2_g", [128, 8])
    ln2_b = din("ln2_b", [128, 8])
    bd_c = din("bd_c", [128, 8])
    b2_c = din("b2_c", [128, 8])
    d_c = din("d_c", [128, 8])
    b1_c = din("b1_c", [128, 32])
    bbc_c = din("bbc_c", [32, 1])
    alog2 = din("alog2", [128, 64])
    esel_i = din("esel_i", [128, 16 * 128])
    cbt_i = din("cbt_i", [32, 512])
    gidx = din("gidx", [128, 20], U32)
    yidx = din("yidx", [128, 8], U32)

    out_t = nc.dram_tensor("out", [H, TC], F32, kind="ExternalOutput")

    ag1a_in = nc.dram_tensor("ag1a_in", [AG1R, TC], F32, kind="Internal")
    ag1a_out = nc.dram_tensor("ag1a_out", [8 * AG1R, TC], F32, kind="Internal")
    ag1b_in = nc.dram_tensor("ag1b_in", [H, TC], BF16, kind="Internal")
    ag1b_out = nc.dram_tensor("ag1b_out", [8 * H, TC], BF16, kind="Internal")
    ag2_in = nc.dram_tensor("ag2_in", [2 * HOH, TC], F32, kind="Internal")
    ag2_out = nc.dram_tensor("ag2_out", [8 * 2 * HOH, TC], F32, kind="Internal")
    dloc = nc.dram_tensor("dloc", [HOH, L], BF16, kind="Internal")
    dxloc = nc.dram_tensor("dxloc", [HOH, L], BF16, kind="Internal")
    accin_d = nc.dram_tensor("accin_d", [H, TC], F32, kind="Internal")
    moe_d = nc.dram_tensor("moe_d", [H, TC], F32, kind="Internal")
    bc_seq = nc.dram_tensor("bc_seq", [32, L], BF16, kind="Internal")
    stats_d = nc.dram_tensor("stats_d", [8, TC], F32, kind="Internal")
    comb_d = nc.dram_tensor("comb_d", [NE, TC], BF16, kind="Internal")
    w1_b = nc.dram_tensor("w1_b", [H, FF], BF16, kind="Internal")
    w2_b = nc.dram_tensor("w2_b", [FF, H], BF16, kind="Internal")
    wg_b = nc.dram_tensor("wg_b", [NE, H, II], BF16, kind="Internal")
    wu_b = nc.dram_tensor("wu_b", [NE, H, II], BF16, kind="Internal")
    wdn_b = nc.dram_tensor("wdn_b", [NE, II, H], BF16, kind="Internal")
    swg_b = nc.dram_tensor("swg_b", [H, II], BF16, kind="Internal")
    swu_b = nc.dram_tensor("swu_b", [H, II], BF16, kind="Internal")
    swd_b = nc.dram_tensor("swd_b", [II, H], BF16, kind="Internal")

    dbg_t = {}
    for nm in dbg:
        shp, dt = {
            "xn_bf": ([H, TC], BF16), "delta": ([H, TC], F32),
            "bcv": ([32, L], BF16),
            "yssm": ([2 * HOH, TC], F32), "accin": ([H, TC], F32),
            "logits": ([8, TC], F32), "comb": ([NE, TC], BF16),
            "moe": ([H, TC], F32), "accf": ([H, TC], F32),
            "normed": ([H, TC], BF16), "attno": ([H, TC], F32),
        }[nm]
        dbg_t[nm] = nc.dram_tensor("dbg_" + nm, shp, dt, kind="ExternalOutput")

    gst = ExitStack()

    def sb(stk, name, shape, dt):
        return stk.enter_context(nc.sbuf_tensor(name, shape, dt))

    def psum(stk, name, shape):
        return stk.enter_context(nc.psum_tensor(name, shape, F32))

    sems = {n: Sem(gst, nc, "sem_" + n)
            for n in ("ly", "ls", "lg", "te", "ve", "se", "ge", "cc", "so",
                      "xw", "stl", "agst", "bcl", "yst", "crl", "mst",
                      "a2l", "xml",
                      "dld0", "dld1", "dld2", "dld3", "mwl0", "mwl1", "wdl0", "wdl1",
                      "w1l0", "w1l1", "w1l2", "w1l3", "w2l0", "w2l1",
                      "crl0", "crl1", "a2l0", "a2l1", "ygl0", "ygl1",
                      "xml0", "xml1", "dsg0", "dsg1", "acg0", "acg1", "bcg")}
    ly, ls, lg, te, ve, se, ge, cc, so = (
        sems[n] for n in ("ly", "ls", "lg", "te", "ve", "se", "ge", "cc", "so"))
    xw, stl, agst, bcl, yst, crl, mst, a2l, xml = (
        sems[n] for n in ("xw", "stl", "agst", "bcl", "yst", "crl", "mst",
                          "a2l", "xml"))
    dld = [sems["dld0"], sems["dld1"], sems["dld2"], sems["dld3"]]
    crls = [sems["crl0"], sems["crl1"]]
    a2ls = [sems["a2l0"], sems["a2l1"]]
    ygls = [sems["ygl0"], sems["ygl1"]]
    xmls = [sems["xml0"], sems["xml1"]]
    dsg = [sems["dsg0"], sems["dsg1"]]
    acg = [sems["acg0"], sems["acg1"]]
    bcg = sems["bcg"]
    mwl = [sems["mwl0"], sems["mwl1"]]
    wdl = [sems["wdl0"], sems["wdl1"]]
    w1l = [sems["w1l0"], sems["w1l1"], sems["w1l2"], sems["w1l3"]]
    w2l = [sems["w2l0"], sems["w2l1"]]
    waited = {}
    v = EW(v0, ve, waited)
    s = EW(s0, se, waited)

    def wv(eng, sem, val=None):
        val = sem.n if val is None else val
        e0 = eng._e if isinstance(eng, EW) else eng
        key = (id(e0), id(sem))
        if waited.get(key, -1) >= val:
            return
        e0.wait_ge(sem.h, val)
        waited[key] = val

    def ydma(dst, src, sem=None):
        (sem or ly).inc(y.dma_start(out=dst, in_=src), 16)

    def sdma(dst, src, sem=None):
        (sem or ls).inc(s0.dma_start(out=dst, in_=src), 16)

    def gdma(dst, src, sem=None):
        (sem or lg).inc(g.dma_start(out=dst, in_=src), 16)

    g_tw, g_tr = {}, {}

    def gtt(out, in0, in1, op):
        key = (id(g), id(ge))
        if RACECHECK:
            if waited.get(key, -1) < ge.n:
                g.wait_ge(ge.h, ge.n)
                waited[key] = ge.n
        else:
            need = 0
            wtn = _ap_tname(out)
            for r in (in0, in1):
                t2 = _ap_tname(r)
                if t2 is not None:
                    need = max(need, g_tw.get(t2, 0))
            need = max(need, g_tw.get(wtn, 0), g_tr.get(wtn, 0))
            if need > waited.get(key, -1):
                g.wait_ge(ge.h, need)
                waited[key] = need
        r2 = ge.inc(g.tensor_tensor(out=out, in0=in0, in1=in1, op=op))
        if not RACECHECK:
            g_tw[wtn] = ge.n
            for r in (in0, in1):
                t2 = _ap_tname(r)
                if t2 is not None:
                    g_tr[t2] = ge.n
        return r2

    bank_guard = {}

    def bank_wait(*banks):
        for bk in banks:
            for sem, val in bank_guard.get(id(bk), ()):
                wv(t, sem, val)

    def bank_set(bank, sem):
        bank_guard[id(bank)] = ((sem, sem.n),)

    def barrier():
        for eng in (t, v, s, y, g):
            for sem in sems.values():
                if sem.n and sem is not so and sem is not cc:
                    wv(eng, sem)

    ones_bf = sb(gst, "ones_bf", [128, 1], BF16)
    ones_f = sb(gst, "ones_f", [1, 128], F32)
    xn_bf = sb(gst, "xn_bf", [128, 8 * TC], BF16)
    negA = sb(gst, "negA", [128, 64], F32)
    esel = sb(gst, "esel", [128, 16 * 128], BF16)
    states = sb(gst, "states", [128, 64], F32)
    p_ln1g = sb(gst, "p_ln1g", [128, 8], F32)
    p_ln1b = sb(gst, "p_ln1b", [128, 8], F32)
    p_ln2g = sb(gst, "p_ln2g", [128, 8], F32)
    p_ln2b = sb(gst, "p_ln2b", [128, 8], F32)
    p_bd = sb(gst, "p_bd", [128, 8], F32)
    p_b2 = sb(gst, "p_b2", [128, 8], F32)
    p_d = sb(gst, "p_d", [128, 8], F32)
    p_b1 = sb(gst, "p_b1", [128, 32], F32)
    p_bbc = sb(gst, "p_bbc", [32, 1], F32)
    p_cbt = sb(gst, "p_cbt", [32, 512], F32)
    ix_g = sb(gst, "ix_g", [128, 20], U32)
    ix_y = sb(gst, "ix_y", [128, 8], U32)
    alog_sb = sb(gst, "alog_sb", [128, 64], F32)

    ps_a = psum(gst, "ps_a", [128, 512])
    ps_b = psum(gst, "ps_b", [128, 512])
    ps_c = psum(gst, "ps_c", [128, 512])
    ps_dd = psum(gst, "ps_d", [128, 512])

    for dst, src in ((p_ln1g, ln1_g), (p_ln1b, ln1_b), (p_ln2g, ln2_g),
                     (p_ln2b, ln2_b), (p_bd, bd_c), (p_b2, b2_c), (p_d, d_c),
                     (p_b1, b1_c), (p_bbc, bbc_c), (p_cbt, cbt_i),
                     (ix_g, gidx), (ix_y, yidx), (alog_sb, alog2)):
        ydma(dst[:], src.ap())
    gdma(esel[:], esel_i.ap())
    v.memset(ones_bf[:], 1.0)
    v.memset(ones_f[0:1, :], 1.0)
    v.memset(states[:], 0.0)
    wv(s, ly)
    s.activation(out=negA[:], in_=alog_sb[:], func=AFT.Exp)
    wv(v, se)
    v.tensor_scalar(out=negA[:], in0=negA[:], scalar1=-1.0, scalar2=None,
                    op0=OP.mult)

    fer = ExitStack()
    logits32 = sb(fer, "logits32", [32, TC], F32)
    fe = ExitStack()
    v.memset(logits32[:], 0.0)
    fep = ExitStack()
    ps_s = psum(fep, "ps_s", [1, 512])
    ps_q = psum(fep, "ps_q", [1, 512])
    ps_bc = psum(fep, "ps_bc", [32, 512])
    ps_rt = psum(fep, "ps_rt", [8, 512])

    mu_b = sb(fe, "mu_b", [128, TT], F32)
    r_b = sb(fe, "r_b", [128, TT], F32)
    strow = sb(fe, "strow", [1, 3 * TT], F32)
    wdT = sb(fe, "wdT", [128, 8 * H], BF16)
    wbcT = sb(fe, "wbcT", [128, 8 * 32], BF16)
    rwT = sb(fe, "rwT", [128, 8 * NE], BF16)
    for kc in range(8):
        gdma(wdT[:, kc * H:(kc + 1) * H], wd_t.ap()[kc * 128:(kc + 1) * 128, :])
        gdma(wbcT[:, kc * 32:(kc + 1) * 32], wbc_t.ap()[kc * 128:(kc + 1) * 128, :])
        gdma(rwT[:, kc * NE:(kc + 1) * NE], rw_t.ap()[kc * 128:(kc + 1) * 128, :])
    for db, df in ((wg_b, wg_e), (wu_b, wu_e), (wdn_b, wdn_e), (swg_b, swg),
                   (swu_b, swu), (swd_b, swd), (w1_b, w1_t), (w2_b, w2_t)):
        gdma(db.ap(), df.ap())
    lg_precast = lg.n

    xt = sb(fe, "xt", [128, 8 * TT], F32)
    xb_bf = sb(fe, "xb_bf", [128, 8 * TT], BF16)
    xq_bf = sb(fe, "xq_bf", [128, 8 * TT], BF16)
    xn_f = sb(fe, "xn_f", [128, 8 * TT], F32)
    xn2_f = sb(fe, "xn2_f", [128, 8 * TT], F32)
    xn2b = sb(fe, "xn2b", [128, 8 * TT], BF16)
    dstage = sb(fe, "dstage", [128, 2 * TT], F32)
    dxstage = sb(fe, "dxstage", [128, 2 * TT], BF16)
    bcst = sb(fe, "bcst", [32, TT], F32)
    acst = sb(fe, "acst", [128, 2 * TT], F32)

    def ln_stats(tt, src_bf, sq_bf, row, pss, psq, strow, mu_b, r_b,
                 store=False):
        wv(t, ve)
        wv(t, se)
        wv(v, stl)
        bank_wait(pss, psq)
        for hc in range(8):
            t.matmul(pss[:, :], ones_bf[:], src_bf[:, hc * TT:(hc + 1) * TT],
                     start=(hc == 0), stop=(hc == 7))
        for hc in range(8):
            mm = t.matmul(psq[:, :], ones_bf[:], sq_bf[:, hc * TT:(hc + 1) * TT],
                          start=(hc == 0), stop=(hc == 7))
        te.inc(mm)
        wv(v, te)
        v.tensor_scalar(out=strow[0:1, 0:TT], in0=pss[0:1, :],
                        scalar1=1.0 / H, scalar2=None, op0=OP.mult)
        v.tensor_scalar(out=strow[0:1, TT:2 * TT], in0=psq[0:1, :],
                        scalar1=1.0 / H, scalar2=None, op0=OP.mult)
        bank_set(pss, ve)
        bank_set(psq, ve)
        v.tensor_tensor(out=strow[0:1, 2 * TT:3 * TT], in0=strow[0:1, 0:TT],
                        in1=strow[0:1, 0:TT], op=OP.mult)
        v.tensor_tensor(out=strow[0:1, 2 * TT:3 * TT],
                        in0=strow[0:1, 2 * TT:3 * TT],
                        in1=strow[0:1, TT:2 * TT], op=OP.subtract)
        v.tensor_scalar(out=strow[0:1, 2 * TT:3 * TT],
                        in0=strow[0:1, 2 * TT:3 * TT], scalar1=-1.0,
                        scalar2=EPS, op0=OP.mult, op1=OP.add)
        wv(s, ve)
        s.activation(out=strow[0:1, TT:2 * TT], in_=strow[0:1, 2 * TT:3 * TT],
                     func=AFT.Ln)
        s.activation(out=strow[0:1, TT:2 * TT], in_=strow[0:1, TT:2 * TT],
                     func=AFT.Exp, scale=-0.5)
        if store:
            wv(y, se)
            wv(y, ve)
            ydma(stats_d.ap()[row:row + 1, tt * TT:(tt + 1) * TT],
                 strow[0:1, 0:TT], stl)
            ydma(stats_d.ap()[row + 1:row + 2, tt * TT:(tt + 1) * TT],
                 strow[0:1, TT:2 * TT], stl)
            return
        bank_wait(ps_c, ps_dd)
        wv(t, se)
        t.matmul(ps_c[:, :], ones_f[0:1, :], strow[0:1, 0:TT],
                 start=True, stop=True)
        mmb = t.matmul(ps_dd[:, :], ones_f[0:1, :], strow[0:1, TT:2 * TT],
                       start=True, stop=True)
        te.inc(mmb)
        wv(s, te)
        s.activation(out=mu_b[:], in_=ps_c[:, :], func=AFT.Identity)
        s.activation(out=r_b[:], in_=ps_dd[:, :], func=AFT.Identity)
        bank_set(ps_c, se)
        bank_set(ps_dd, se)

    for tt in range(NTT):
        wv(y, ve)
        wv(y, se)
        ydma(xt[:], bass.AP(x_fm, tt * TT, [[TC, 128], [128 * TC, 8], [1, TT]]),
             xw)
        wv(s, xw)
        s.activation(out=xq_bf[:], in_=xt[:], func=AFT.Square)
        wv(v, xw)
        v.tensor_copy(out=xb_bf[:], in_=xt[:])
        ln_stats(tt, xb_bf, xq_bf, 0, ps_s, ps_q, strow, mu_b, r_b)
        wv(v, se)
        v.tensor_tensor(out=xn_f[:], in0=xt[:],
                        in1=bass.AP(mu_b, 0, [[512, 128], [0, 8], [1, 512]]),
                        op=OP.subtract)
        v.tensor_tensor(out=xn_f[:], in0=xn_f[:],
                        in1=bass.AP(r_b, 0, [[512, 128], [0, 8], [1, 512]]),
                        op=OP.mult)
        v.tensor_copy(
            out=bass.AP(xn_bf, tt * TT, [[8 * TC, 128], [TC, 8], [1, TT]]),
            in_=xn_f[:])
        wv(s, ve)
        s.activation(out=xq_bf[:], in_=xn_f[:], func=AFT.Square)
        s.activation(out=xb_bf[:], in_=xn_f[:], func=AFT.Identity)
        ln_stats(tt, xb_bf, xq_bf, 2, ps_s, ps_q, strow, mu_b, r_b)
        wv(v, se)
        v.tensor_tensor(out=xn2_f[:], in0=xn_f[:],
                        in1=bass.AP(mu_b, 0, [[512, 128], [0, 8], [1, 512]]),
                        op=OP.subtract)
        v.tensor_tensor(out=xn2_f[:], in0=xn2_f[:],
                        in1=bass.AP(r_b, 0, [[512, 128], [0, 8], [1, 512]]),
                        op=OP.mult)
        wv(s, ve)
        wv(s, te)
        for hc in range(8):
            cs = slice(hc * TT, (hc + 1) * TT)
            s.activation(out=xn2b[:, cs], in_=xn2_f[:, cs], func=AFT.Identity,
                         scale=p_ln1g[:, hc:hc + 1], bias=p_ln1b[:, hc:hc + 1])
        wv(v, se)
        v.tensor_copy(out=xn2_f[:], in_=xn2b[:])
        for hc in range(8):
            cs = slice(hc * TT, (hc + 1) * TT)
            stg = acst[:, (hc % 2) * TT:(hc % 2 + 1) * TT]
            wv(v, acg[hc % 2])
            v.scalar_tensor_tensor(out=stg, in0=xn2_f[:, cs],
                                   scalar=p_d[:, hc:hc + 1], in1=xn_f[:, cs],
                                   op0=OP.mult, op1=OP.add)
            wv(y, ve)
            ydma(accin_d.ap()[hc * 128:(hc + 1) * 128, tt * TT:(tt + 1) * TT],
                 stg, acg[hc % 2])
        wv(t, lg)
        wv(t, se)
        for mo in range(8):
            pd = ps_a if (mo % 2 == 0) else ps_b
            dstg = dstage[:, (mo % 2) * TT:(mo % 2 + 1) * TT]
            dxstg = dxstage[:, (mo % 2) * TT:(mo % 2 + 1) * TT]
            bank_wait(pd)
            for kc in range(8):
                mm = t.matmul(pd[:, :],
                              wdT[:, kc * H + mo * 128: kc * H + (mo + 1) * 128],
                              xn2b[:, kc * TT:(kc + 1) * TT],
                              start=(kc == 0), stop=(kc == 7))
            te.inc(mm)
            wv(s, te)
            wv(s, dsg[mo % 2])
            s.activation(out=dstg, in_=pd[:, :], func=AFT.Abs,
                         bias=p_bd[:, mo:mo + 1])
            s.activation(out=dstg, in_=dstg, func=AFT.Exp, scale=-1.0)
            s.activation(out=dstg, in_=dstg, func=AFT.Ln, bias=1.0)
            s.activation(out=dxstg, in_=pd[:, :], func=AFT.Relu,
                         bias=p_bd[:, mo:mo + 1])
            bank_set(pd, se)
            wv(v, se)
            wv(v, dsg[mo % 2])
            v.tensor_tensor(out=dstg, in0=dstg, in1=dxstg, op=OP.add)
            v.tensor_tensor(out=dxstg, in0=dstg,
                            in1=xn2_f[:, mo * TT:(mo + 1) * TT], op=OP.mult)
            wv(y, ve)
            ydma(ag1a_in.ap()[mo * 128:(mo + 1) * 128, tt * TT:(tt + 1) * TT],
                 dstg, dsg[mo % 2])
            ydma(ag1b_in.ap()[mo * 128:(mo + 1) * 128, tt * TT:(tt + 1) * TT],
                 dxstg, dsg[mo % 2])
        bank_wait(ps_bc)
        for kc in range(8):
            mm = t.matmul(ps_bc[:, :], wbcT[:, kc * 32:(kc + 1) * 32],
                          xn2b[:, kc * TT:(kc + 1) * TT],
                          start=(kc == 0), stop=(kc == 7))
        te.inc(mm)
        wv(v, te)
        wv(v, bcg)
        v.tensor_scalar(out=bcst[:], in0=ps_bc[:, :], scalar1=p_bbc[:, 0:1],
                        scalar2=None, op0=OP.add)
        bank_set(ps_bc, ve)
        wv(y, ve)
        ydma(ag1a_in.ap()[H:H + 32, tt * TT:(tt + 1) * TT], bcst[:], bcg)
        bank_wait(ps_rt)
        for kc in range(8):
            mm = t.matmul(ps_rt[:, :], rwT[:, kc * NE:(kc + 1) * NE],
                          xn_bf[:, kc * TC + tt * TT: kc * TC + (tt + 1) * TT],
                          start=(kc == 0), stop=(kc == 7))
        te.inc(mm)
        wv(v, te)
        v.tensor_copy(out=logits32[0:8, tt * TT:(tt + 1) * TT], in_=ps_rt[:, :])
        bank_set(ps_rt, ve)

    if "xn_bf" in dbg_t:
        wv(y, ve)
        for hc in range(8):
            ydma(dbg_t["xn_bf"].ap()[hc * 128:(hc + 1) * 128, :],
                 xn_bf[:, hc * TC:(hc + 1) * TC])
    if "logits" in dbg_t:
        wv(y, ve)
        ydma(dbg_t["logits"].ap()[:, :], logits32[0:8, :])
    if "accin" in dbg_t:
        wv(g, ly)
        wv(g, lg)
        gdma(dbg_t["accin"].ap()[:, :], accin_d.ap()[:, :])
    if "delta" in dbg_t:
        wv(g, ly)
        wv(g, lg)
        gdma(dbg_t["delta"].ap()[:, :], ag1a_in.ap()[0:H, :])

    for sem_ in (dsg[0], dsg[1], acg[0], acg[1], bcg):
        wv(g, sem_)
    cc.inc(g.collective_compute("AllGather", OP.bypass, replica_groups=RG8,
                                ins=[ag1a_in.ap()], outs=[ag1a_out.ap()]))
    cc.inc(g.collective_compute("AllGather", OP.bypass, replica_groups=RG8,
                                ins=[ag1b_in.ap()], outs=[ag1b_out.ap()]))
    barrier()
    fe.close()
    fep.close()

    ro = ExitStack()
    R32 = sb(ro, "R32", [32, 64 * 32], F32)
    sco = sb(ro, "sco", [32, 512], F32)
    msc = sb(ro, "msc", [32, 512], F32)
    gtmp = sb(ro, "gtmp", [32, 128], F32)
    gtmp2 = sb(ro, "gtmp2", [32, 128], F32)
    ge0 = sb(ro, "ge0", [32, 64], F32)
    ge1 = sb(ro, "ge1", [32, 64], F32)
    m1c = sb(ro, "m1c", [32, 64], F32)
    m1r = sb(ro, "m1r", [32, 512], F32)
    f1 = sb(ro, "f1", [32, 512], F32)
    f2 = sb(ro, "f2", [32, 512], F32)
    cum = sb(ro, "cum", [32, 512], F32)
    cum2 = sb(ro, "cum2", [32, 512], F32)
    wsc = sb(ro, "wsc", [32, 64], F32)
    w1c = sb(ro, "w1c", [32, 64], F32)
    w2c = sb(ro, "w2c", [32, 64], F32)
    prodw = sb(ro, "prodw", [32, 512], F32)
    padbuf = sb(ro, "padbuf", [32, 64 * 32], F32)
    combT = sb(ro, "combT", [32, TC], F32)

    def rap(tsr, fsz, off, dims, np_=32):
        return bass.AP(tsr, off, [[fsz, np_]] + dims)

    for j in range(64):
        v.transpose(out=R32[0:32, j * 32:(j + 1) * 32],
                    in_=logits32[0:32, j * 32:(j + 1) * 32])
    wv(s, ve)
    s.activation(out=R32[:], in_=R32[:], func=AFT.Tanh, scale=0.5)
    wv(v, se)
    v.tensor_scalar(out=rap(sco, 512, 0, [[8, 64], [1, 8]]),
                    in0=rap(R32, 64 * 32, 0, [[32, 64], [1, 8]]),
                    scalar1=0.5, scalar2=0.5, op0=OP.mult, op1=OP.add)
    wv(v, ly)
    v.tensor_tensor(out=msc[:], in0=sco[:], in1=p_cbt[:], op=OP.add)
    for idx, (i, j) in enumerate([(0, 1), (0, 2), (0, 3), (1, 2), (1, 3),
                                  (2, 3)]):
        dst = gtmp if idx == 0 else gtmp2
        v.tensor_tensor(out=dst[:], in0=rap(msc, 512, i, [[4, 128]]),
                        in1=rap(msc, 512, j, [[4, 128]]), op=OP.add)
        if idx > 0:
            v.tensor_tensor(out=gtmp[:], in0=gtmp[:], in1=gtmp2[:], op=OP.max)
    v.tensor_tensor(out=ge0[:], in0=rap(gtmp, 128, 0, [[2, 64]]),
                    in1=rap(gtmp, 128, 1, [[2, 64]]), op=OP.is_ge)
    v.tensor_tensor(out=ge1[:], in0=rap(gtmp, 128, 1, [[2, 64]]),
                    in1=rap(gtmp, 128, 0, [[2, 64]]), op=OP.is_gt)
    for e in range(4):
        v.tensor_copy(out=rap(m1r, 512, e, [[8, 64]]), in_=ge0[:])
        v.tensor_copy(out=rap(m1r, 512, 4 + e, [[8, 64]]), in_=ge1[:])
    v.tensor_tensor(out=msc[:], in0=msc[:], in1=m1r[:], op=OP.mult)

    def top_select(fdst):
        v.tensor_reduce(out=m1c[:], in_=rap(msc, 512, 0, [[8, 64], [1, 8]]),
                        axis=AXX, op=OP.max)
        for e in range(8):
            v.tensor_copy(out=rap(m1r, 512, e, [[8, 64]]), in_=m1c[:])
        v.tensor_tensor(out=fdst[:], in0=msc[:], in1=m1r[:], op=OP.is_ge)
        v.tensor_copy(out=cum[:], in_=fdst[:])
        src, dst = cum, cum2
        for shd in (1, 2, 4):
            v.tensor_copy(out=dst[:], in_=src[:])
            v.tensor_tensor(
                out=rap(dst, 512, shd, [[8, 64], [1, 8 - shd]]),
                in0=rap(src, 512, shd, [[8, 64], [1, 8 - shd]]),
                in1=rap(src, 512, 0, [[8, 64], [1, 8 - shd]]), op=OP.add)
            src, dst = dst, src
        v.tensor_scalar(out=dst[:], in0=src[:], scalar1=1.0, scalar2=None,
                        op0=OP.is_le)
        v.tensor_tensor(out=fdst[:], in0=fdst[:], in1=dst[:], op=OP.mult)

    top_select(f1)
    v.tensor_tensor(out=prodw[:], in0=f1[:], in1=sco[:], op=OP.mult)
    v.tensor_reduce(out=w1c[:], in_=rap(prodw, 512, 0, [[8, 64], [1, 8]]),
                    axis=AXX, op=OP.add)
    v.scalar_tensor_tensor(out=msc[:], in0=f1[:], scalar=BIG_NEG, in1=msc[:],
                           op0=OP.mult, op1=OP.add)
    top_select(f2)
    v.tensor_tensor(out=prodw[:], in0=f2[:], in1=sco[:], op=OP.mult)
    v.tensor_reduce(out=w2c[:], in_=rap(prodw, 512, 0, [[8, 64], [1, 8]]),
                    axis=AXX, op=OP.add)
    v.tensor_tensor(out=wsc[:], in0=w1c[:], in1=w2c[:], op=OP.add)
    v.tensor_scalar(out=wsc[:], in0=wsc[:], scalar1=2.0, scalar2=2e-20,
                    op0=OP.mult, op1=OP.add)
    v.reciprocal(out=wsc[:], in_=wsc[:])
    for e in range(8):
        v.tensor_copy(out=rap(m1r, 512, e, [[8, 64]]), in_=w1c[:])
    v.tensor_tensor(out=f1[:], in0=f1[:], in1=m1r[:], op=OP.mult)
    for e in range(8):
        v.tensor_copy(out=rap(m1r, 512, e, [[8, 64]]), in_=w2c[:])
    v.tensor_tensor(out=f2[:], in0=f2[:], in1=m1r[:], op=OP.mult)
    v.tensor_tensor(out=f1[:], in0=f1[:], in1=f2[:], op=OP.add)
    for e in range(8):
        v.tensor_copy(out=rap(m1r, 512, e, [[8, 64]]), in_=wsc[:])
    v.tensor_tensor(out=f1[:], in0=f1[:], in1=m1r[:], op=OP.mult)
    v.memset(padbuf[:], 0.0)
    v.tensor_copy(out=rap(padbuf, 64 * 32, 0, [[32, 64], [1, 8]]),
                  in_=rap(f1, 512, 0, [[8, 64], [1, 8]]))
    for j in range(64):
        v.transpose(out=combT[0:32, j * 32:(j + 1) * 32],
                    in_=padbuf[0:32, j * 32:(j + 1) * 32])
    wv(g, ve)
    gdma(comb_d.ap()[:, :], combT[0:8, :])
    lg_comb = lg.n
    if "comb" in dbg_t:
        wv(g, lg)
        gdma(dbg_t["comb"].ap()[:, :], comb_d.ap()[:, :])
    barrier()
    ro.close()
    fer.close()

    sm = ExitStack()
    pro = ExitStack()
    gstg = sb(pro, "gstg", [128, TC], F32)
    gstgb = sb(pro, "gstgb", [128, TC], BF16)
    bcs = sb(pro, "bcs", [32, 2 * TC], F32)
    wv(g, cc, 2)
    for sh in range(2):
        lg.inc(g.indirect_dma_start(
            out=bcs[0:32, sh * TC:(sh + 1) * TC], out_offset=None,
            in_=bass.AP(ag1a_out, 0, [[TC, 8 * AG1R], [1, TC]]),
            in_offset=bass.IndirectOffsetOnAxis(
                ix_g[0:32, 16 + sh:17 + sh], 0)), 16)
    wv(g, lg)
    for sh in range(2):
        gdma(bc_seq.ap()[:, sh * TC:(sh + 1) * TC],
             bcs[0:32, sh * TC:(sh + 1) * TC])
    for sh in range(2):
        for c in range(4):
            wv(g, lg)
            lg.inc(g.indirect_dma_start(
                out=gstg[:, :], out_offset=None,
                in_=bass.AP(ag1a_out, 0, [[TC, 8 * AG1R], [1, TC]]),
                in_offset=bass.IndirectOffsetOnAxis(
                    ix_g[:, sh * 4 + c: sh * 4 + c + 1], 0)), 16)
            wv(g, lg)
            gdma(dloc.ap()[c * 128:(c + 1) * 128, sh * TC:(sh + 1) * TC],
                 gstg[:, :])
    for sh in range(2):
        for c in range(4):
            wv(g, lg)
            lg.inc(g.indirect_dma_start(
                out=gstgb[:, :], out_offset=None,
                in_=bass.AP(ag1b_out, 0, [[TC, 8 * H], [1, TC]]),
                in_offset=bass.IndirectOffsetOnAxis(
                    ix_g[:, 8 + sh * 4 + c: 8 + sh * 4 + c + 1], 0)), 16)
            wv(g, lg)
            gdma(dxloc.ap()[c * 128:(c + 1) * 128, sh * TC:(sh + 1) * TC],
                 gstgb[:, :])
    if "bcv" in dbg_t:
        wv(g, lg)
        gdma(dbg_t["bcv"].ap()[:, :], bc_seq.ap()[:, :])
    pro.close()
    drep = sb(sm, "drep", [128, 4 * TQ], BF16)
    dxrep = sb(sm, "dxrep", [128, 4 * TQ], BF16)
    brepq = sb(sm, "brepq", [128, 2 * TQ], BF16)
    crepq = sb(sm, "crepq", [128, 2 * TQ], BF16)
    dAt = sb(sm, "dAt", [128, 4 * TQ], F32)
    dBxt = sb(sm, "dBxt", [128, 4 * TQ], F32)
    Hsb = sb(sm, "Hsb", [128, TQ], BF16)
    prodb = sb(sm, "prodb", [128, 2 * TQ], BF16)
    ystg = sb(sm, "ystg", [128, TQ], F32)
    ps_y = [psum(sm, f"ps_y{i}", [128, 512]) for i in range(4)]

    wgs = sb(sm, "wgs", [128, 2 * 8 * II], BF16)
    wus = sb(sm, "wus", [128, 2 * 8 * II], BF16)
    wdmo = sb(sm, "wdmo", [128, 2 * 4 * H], BF16)
    wsdm = sb(sm, "wsdm", [128, 2 * 4 * 128], BF16)
    h_all = sb(sm, "h_all", [128, NE * 4 * TT], BF16)
    h_sh = sb(sm, "h_sh", [128, 4 * TT], BF16)
    combr = sb(sm, "combr", [128, 2 * TT], BF16)
    g_bf = sb(sm, "g_bf", [128, 2 * TT], BF16)
    u_bf = sb(sm, "u_bf", [128, 2 * TT], BF16)
    tnh = sb(sm, "tnh", [128, 2 * TT], BF16)
    h1t = sb(sm, "h1t", [128, 2 * TT], BF16)
    mstage = sb(sm, "mstage", [128, 2 * TT], F32)

    dA_done, dbx_done, scan_done, prod_done = {}, {}, {}, {}
    drep_ld, dxrep_ld, ymm_done, evac_done, yq_dma = {}, {}, {}, {}, {}
    brep_ld = {}
    ypend = []
    ypend_flush = {}

    def emit_ymm(item):
        u, par, kt = item
        cidx2 = u // 16
        pyA2, pyB2, chunk2, sh2, twp2 = ypend_flush[cidx2]
        wv(t, ve, prod_done[u])
        if kt == 0:
            bank_wait(pyA2, pyB2)
        t.matmul(pyA2[:, :], esel[:, kt * 128:(kt + 1) * 128],
                 bass.AP(prodb, par * TQ, [[2 * TQ, 128], [1, 512]]),
                 start=(kt == 0), stop=(kt == 15))
        mm = t.matmul(pyB2[:, :], esel[:, kt * 128:(kt + 1) * 128],
                      bass.AP(prodb, par * TQ + 512,
                              [[2 * TQ, 128], [1, 512]]),
                      start=(kt == 0), stop=(kt == 15))
        te.inc(mm)
        ymm_done[u] = te.n
        if kt == 15:
            del ypend_flush[cidx2]
            wv(s, te, ymm_done[u])
            wv(s, yst)
            s.activation(out=ystg[:, 0:512], in_=pyA2[:, :], func=AFT.Identity)
            s.activation(out=ystg[:, 512:TQ], in_=pyB2[:, :], func=AFT.Identity)
            bank_set(pyA2, se)
            bank_set(pyB2, se)
            evac_done[cidx2] = se.n
            wv(y, se, evac_done[cidx2])
            wv(y, yst)
            ydma(bass.AP(ag2_in, (sh2 * HOH + chunk2 * 128) * TC + twp2,
                         [[TC, 128], [1, TQ]]),
                 ystg[:, 0:TQ], yst)

    def emit_ssm():
        lg_pro = lg.n
        yield
        def unit_loads(u2):
            if u2 >= 256:
                return
            tq2, tile2 = u2 // 64, u2 % 64
            par2 = u2 % 4
            dsl2 = slice(par2 * TQ, (par2 + 1) * TQ)
            wv(y, se, dA_done.get(u2 - 4, 0))
            wv(y, ge, dbx_done.get(u2 - 4, 0))

            ydma(drep[:, dsl2],
                 bass.AP(dloc, (8 * tile2) * L + tq2 * TQ,
                         [[L, 8], [0, 16], [1, TQ]]), dld[par2])
            ydma(dxrep[:, dsl2],
                 bass.AP(dxloc, (8 * tile2) * L + tq2 * TQ,
                         [[L, 8], [0, 16], [1, TQ]]), dld[par2])

        wv(y, lg, lg_pro)
        for uu in range(3):
            unit_loads(uu)
        for tq in range(4):
            sh, twp = tq // 2, (tq % 2) * TQ
            qpar = tq % 2
            qsl = slice(qpar * TQ, (qpar + 1) * TQ)
            wv(y, ve, scan_done.get(("q", tq - 2), 0))
            wv(y, bcl)
            ydma(brepq[:, qsl], bass.AP(bc_seq, tq * TQ,
                                        [[0, 8], [L, 16], [1, TQ]]), bcl)
            ydma(crepq[:, qsl], bass.AP(bc_seq, 16 * L + tq * TQ,
                                        [[0, 8], [L, 16], [1, TQ]]), bcl)
            for chunk in range(4):
                pyA = ps_y[(chunk % 2) * 2]
                pyB = ps_y[(chunk % 2) * 2 + 1]
                ypend_flush[tq * 4 + chunk] = (pyA, pyB, chunk, sh, twp)
                for kt in range(16):
                    tile = chunk * 16 + kt
                    u = tq * 64 + tile
                    lpar = u % 4
                    par = u % 4
                    lsl = slice(lpar * TQ, (lpar + 1) * TQ)
                    dsl = slice(par * TQ, (par + 1) * TQ)
                    unit_loads(u + 3)
                    wv(s, dld[lpar])
                    wv(s, ve, scan_done.get(u - 4, 0))
                    s.activation(out=dAt[:, dsl], in_=drep[:, lsl],
                                 func=AFT.Exp, scale=negA[:, tile:tile + 1])
                    dA_done[u] = se.n
                    wv(g, dld[lpar])
                    wv(g, bcl)
                    wv(g, ve, scan_done.get(u - 4, 0))
                    gtt(dBxt[:, dsl], dxrep[:, lsl], brepq[:, qsl], OP.mult)
                    dbx_done[u] = ge.n
                    wv(v, se, dA_done[u])
                    wv(v, ge, dbx_done[u])
                    v.tensor_tensor_scan(out=Hsb[:, :], data0=dAt[:, dsl],
                                         data1=dBxt[:, dsl],
                                         initial=states[:, tile:tile + 1],
                                         op0=OP.mult, op1=OP.add)
                    v.tensor_copy(out=states[:, tile:tile + 1],
                                  in_=Hsb[:, TQ - 1:TQ])
                    ppar = u % 2
                    psl = slice(ppar * TQ, (ppar + 1) * TQ)
                    wv(v, te, ymm_done.get(u - 2, 0))
                    wv(v, bcl)
                    v.tensor_tensor(out=prodb[:, psl], in0=Hsb[:, :],
                                    in1=crepq[:, qsl], op=OP.mult)
                    scan_done[u] = ve.n
                    if tile == 63:
                        scan_done[("q", tq)] = ve.n
                    prod_done[u] = ve.n
                    ypend.append((u, u % 2, kt))
                    if len(ypend) > YMM_SKEW:
                        emit_ymm(ypend.pop(0))
                yield
            if tq == 1:
                while ypend:
                    emit_ymm(ypend.pop(0))
                wv(g, yst)
                cc.inc(g.collective_compute(
                    "AllGather", OP.bypass, replica_groups=RG8,
                    ins=[ag2_in.ap()[0:HOH, :]],
                    outs=[ag2_out.ap()[0:8 * HOH, :]]))
        while ypend:
            emit_ymm(ypend.pop(0))
        wv(g, yst)
        cc.inc(g.collective_compute(
            "AllGather", OP.bypass, replica_groups=RG8,
            ins=[ag2_in.ap()[HOH:2 * HOH, :]],
            outs=[ag2_out.ap()[8 * HOH:16 * HOH, :]]))
        if "yssm" in dbg_t:
            wv(g, cc)
            wv(g, ly)
            gdma(dbg_t["yssm"].ap()[:, :], ag2_in.ap()[:, :])
        yield

    gu_evac, h_ready, mo_evac, mo_dma, dn_done, hcons = {}, {}, {}, {}, {}, {}
    ld_w, hr_last = {}, {}
    gu_u = [0]
    gupend, dnpend = [], []
    combr_guard = {}

    def moe_w_load(tt, e):
        ex = tt * 9 + (e + 1)
        slot = ex % 2
        ydma(wgs[:, slot * 8 * II:(slot + 1) * 8 * II],
             bass.AP(wg_b if e >= 0 else swg_b, max(e, 0) * H * II,
                     [[II, 128], [128 * II, 8], [1, II]]), mwl[slot])
        ydma(wus[:, slot * 8 * II:(slot + 1) * 8 * II],
             bass.AP(wu_b if e >= 0 else swu_b, max(e, 0) * H * II,
                     [[II, 128], [128 * II, 8], [1, II]]), mwl[slot])
        ld_w[(tt, e)] = (slot, mwl[slot].n)

    def hr_last_for(tt2, e2):
        if e2 >= -1 and (tt2, e2) in hr_last:
            return hr_last[(tt2, e2)]
        if e2 < -1 and (tt2 - 1, e2 + 9) in hr_last:
            return hr_last[(tt2 - 1, e2 + 9)]
        return 0

    def flush_gu(item):
        tt, e, kc4, pg, pu, ug, mmv, crv, cpar = item
        parq = ug % 2
        csl = slice(parq * TT, (parq + 1) * TT)
        wv(s, te, mmv)
        wv(s, ve, h_ready.get(ug - 2, 0))
        s.activation(out=tnh[:, csl], in_=pg[:, :], func=AFT.Tanh, scale=0.5)
        s.activation(out=g_bf[:, csl], in_=pg[:, :], func=AFT.Identity)
        s.activation(out=u_bf[:, csl], in_=pu[:, :], func=AFT.Identity)
        bank_set(pg, se)
        bank_set(pu, se)
        gu_evac[((tt, e), kc4)] = se.n
        wv(v, se, se.n)
        v.scalar_tensor_tensor(out=h1t[:, csl], in0=tnh[:, csl],
                               scalar=1.0, in1=u_bf[:, csl],
                               op0=OP.add, op1=OP.mult)
        if e < 0:
            wv(v, te, dn_done.get((tt - 1, 7), 0))
            v.scalar_tensor_tensor(
                out=h_sh[:, kc4 * TT:(kc4 + 1) * TT],
                in0=h1t[:, csl], scalar=0.5, in1=g_bf[:, csl],
                op0=OP.mult, op1=OP.mult)
        else:
            wv(v, crls[cpar], crv)
            wv(v, te, dn_done.get((tt - 1, 7), 0))
            v.tensor_tensor(out=h1t[:, csl], in0=h1t[:, csl],
                            in1=g_bf[:, csl], op=OP.mult)
            v.tensor_tensor(
                out=h_all[:, (e * 4 + kc4) * TT:(e * 4 + kc4 + 1) * TT],
                in0=h1t[:, csl],
                in1=combr[:, cpar * TT:(cpar + 1) * TT], op=OP.mult)
            combr_guard[cpar] = ve.n
        h_ready[ug] = ve.n
        hr_last[(tt, e)] = ve.n

    def flush_dn(item):
        tt, mo, pd2 = item
        wv(s, te, dn_done[(tt, mo)])
        wv(s, mst)
        s.activation(out=mstage[:, (mo % 2) * TT:(mo % 2 + 1) * TT],
                     in_=pd2[:, :], func=AFT.Identity)
        bank_set(pd2, se)
        mo_evac[(tt, mo)] = se.n
        wv(y, se, se.n)
        wv(y, mst)
        ydma(moe_d.ap()[mo * 128:(mo + 1) * 128, tt * TT:(tt + 1) * TT],
             mstage[:, (mo % 2) * TT:(mo % 2 + 1) * TT], mst)

    def emit_moe():
        wv(y, lg, lg_precast)
        moe_w_load(0, -1)
        moe_w_load(0, 0)
        for tt in range(NTT):
            for e in range(-1, NE):
                slot = (tt * 9 + e + 1) % 2
                wsl_g = wgs[:, slot * 8 * II:(slot + 1) * 8 * II]
                wsl_u = wus[:, slot * 8 * II:(slot + 1) * 8 * II]
                cpar = (e + 1) % 2
                if e >= 0:
                    wv(y, ve, combr_guard.get(cpar, 0))
                    wv(y, lg, lg_comb)
                    ydma(combr[:, cpar * TT:(cpar + 1) * TT],
                         bass.AP(comb_d, e * TC + tt * TT, [[0, 128], [1, TT]]),
                         crls[cpar])
                    crv = crls[cpar].n
                else:
                    crv = 0
                for kc4 in range(4):
                    pg, pu = (ps_a, ps_b) if kc4 % 2 == 0 else (ps_c, ps_dd)
                    wv(t, mwl[ld_w[(tt, e)][0]], ld_w[(tt, e)][1])
                    bank_wait(pg, pu)
                    for kc in range(8):
                        t.matmul(pg[:, :],
                                 wsl_g[:, kc * II + kc4 * 128:
                                       kc * II + (kc4 + 1) * 128],
                                 xn_bf[:, kc * TC + tt * TT:
                                       kc * TC + (tt + 1) * TT],
                                 start=(kc == 0), stop=(kc == 7))
                    for kc in range(8):
                        mm = t.matmul(pu[:, :],
                                      wsl_u[:, kc * II + kc4 * 128:
                                            kc * II + (kc4 + 1) * 128],
                                      xn_bf[:, kc * TC + tt * TT:
                                            kc * TC + (tt + 1) * TT],
                                      start=(kc == 0), stop=(kc == 7))
                    te.inc(mm)
                    mmv = te.n
                    gu_u[0] += 1
                    ug = gu_u[0]
                    gupend.append((tt, e, kc4, pg, pu, ug, mmv, crv, cpar))
                    if len(gupend) > GU_SKEW:
                        flush_gu(gupend.pop(0))
                    mm_last = te.n
                    yield
                nxt_e, nxt_tt = e + 2, tt
                if nxt_e >= NE:
                    nxt_e -= (NE + 1)
                    nxt_tt += 1
                if nxt_tt < NTT:
                    wv(y, te, mm_last)
                    moe_w_load(nxt_tt, nxt_e)

            while gupend:
                flush_gu(gupend.pop(0))
            for mo in range(8):
                pd2 = ps_a if mo % 2 == 0 else ps_b
                slot = mo % 2
                wv(y, te, dn_done.get((tt, mo - 2), 0))
                ydma(wdmo[:, slot * 4 * H:(slot + 1) * 4 * H],
                     bass.AP(wdn_b, mo * 128,
                             [[H, 128], [II * H, 8], [128 * H, 4], [1, 128]]),
                     wdl[slot])
                ydma(wsdm[:, slot * 4 * 128:(slot + 1) * 4 * 128],
                     bass.AP(swd_b, mo * 128,
                             [[H, 128], [128 * H, 4], [1, 128]]), wdl[slot])
                wv(t, ve, hr_last[(tt, NE - 1)])
                wv(t, ve, hr_last[(tt, -1)])
                wv(t, wdl[slot])
                bank_wait(pd2)
                for e in range(NE):
                    for kc in range(4):
                        t.matmul(pd2[:, :],
                                 wdmo[:, slot * 4 * H + (e * 4 + kc) * 128:
                                       slot * 4 * H + (e * 4 + kc + 1) * 128],
                                 h_all[:, (e * 4 + kc) * TT:
                                       (e * 4 + kc + 1) * TT],
                                 start=(e == 0 and kc == 0), stop=False)
                for kc in range(4):
                    mm = t.matmul(pd2[:, :],
                                  wsdm[:, slot * 4 * 128 + kc * 128:
                                        slot * 4 * 128 + (kc + 1) * 128],
                                  h_sh[:, kc * TT:(kc + 1) * TT],
                                  start=False, stop=(kc == 3))
                te.inc(mm)
                dn_done[(tt, mo)] = te.n
                dnpend.append((tt, mo, pd2))
                if len(dnpend) > DN_SKEW:
                    flush_dn(dnpend.pop(0))
                yield
            while dnpend:
                flush_dn(dnpend.pop(0))
        if "moe" in dbg_t:
            wv(g, mst)
            wv(g, lg)
            gdma(dbg_t["moe"].ap()[:, :], moe_d.ap()[:, :])
        yield

    gen_s = emit_ssm()
    gen_m = emit_moe()
    alive_s, alive_m = True, True
    for _ in range(90):
        try:
            next(gen_m)
        except StopIteration:
            alive_m = False
            break
    next(gen_s)
    while alive_s or alive_m:
        if alive_s:
            try:
                next(gen_s)
            except StopIteration:
                alive_s = False
        if alive_m:
            for _ in range(8):
                try:
                    next(gen_m)
                except StopIteration:
                    alive_m = False
                    break
    barrier()
    sm.close()

    p4 = ExitStack()
    p4p = ExitStack()
    ps_s2 = psum(p4p, "ps_s2", [1, 512])
    ps_q2 = psum(p4p, "ps_q2", [1, 512])
    accf = sb(p4, "accf", [128, 8 * TC], F32)

    p4a = ExitStack()
    mu_b = sb(p4a, "mu_b2", [128, TT], F32)
    r_b = sb(p4a, "r_b2", [128, TT], F32)
    strow = sb(p4a, "strow2", [1, 3 * TT], F32)
    ystage = sb(p4a, "ystage", [128, 2 * TT], F32)
    astage = sb(p4a, "astage", [128, 2 * TT], F32)
    ac_bf = sb(p4a, "ac_bf", [128, 8 * TT], BF16)
    aq_bf = sb(p4a, "aq_bf", [128, 8 * TT], BF16)

    yg_cons, hff_done, ots_done = {}, {}, {}
    w1ld, w2ld = {}, {}

    wv(g, cc, 4)
    ag2av = bass.AP(ag2_out, 0, [[TT, 8 * 2 * HOH * 4], [1, TT]])
    for tt in range(NTT):
        for hc in range(8):
            uy = tt * 8 + hc
            ypar = hc % 2
            wv(g, ve, yg_cons.get(uy - 2, 0))
            ygls[ypar].inc(g.indirect_dma_start(
                out=ystage[:, ypar * TT:(ypar + 1) * TT], out_offset=None,
                in_=ag2av,
                in_offset=bass.IndirectOffsetOnAxis(ix_y[:, hc:hc + 1], 0),
                element_offset=tt * TT), 16)
            wv(y, ve, yg_cons.get(uy - 2, 0))
            ydma(astage[:, ypar * TT:(ypar + 1) * TT],
                 accin_d.ap()[hc * 128:(hc + 1) * 128, tt * TT:(tt + 1) * TT],
                 a2ls[ypar])
            wv(v, ygls[ypar])
            wv(v, a2ls[ypar])
            v.tensor_tensor(
                out=accf[:, hc * TC + tt * TT: hc * TC + (tt + 1) * TT],
                in0=astage[:, ypar * TT:(ypar + 1) * TT],
                in1=ystage[:, ypar * TT:(ypar + 1) * TT], op=OP.add)
            yg_cons[uy] = ve.n
    if "accf" in dbg_t:
        wv(y, ve)
        for hc in range(8):
            ydma(dbg_t["accf"].ap()[hc * 128:(hc + 1) * 128, :],
                 accf[:, hc * TC:(hc + 1) * TC])
    for tt in range(NTT):
        wv(s, ve)
        for hc in range(8):
            s.activation(out=aq_bf[:, hc * TT:(hc + 1) * TT],
                         in_=accf[:, hc * TC + tt * TT: hc * TC + (tt + 1) * TT],
                         func=AFT.Square)
            s.activation(out=ac_bf[:, hc * TT:(hc + 1) * TT],
                         in_=accf[:, hc * TC + tt * TT: hc * TC + (tt + 1) * TT],
                         func=AFT.Identity)
        ln_stats(tt, ac_bf, aq_bf, 4, ps_s2, ps_q2, strow, mu_b, r_b,
                 store=True)
    barrier()
    p4a.close()
    nstage = sb(p4, "nstage", [128, 8 * TT], BF16)
    hff = sb(p4, "hff", [128, 32 * TT], BF16)
    w1s = sb(p4, "w1s", [128, 4 * 8 * 128], BF16)
    w2s = sb(p4, "w2s", [128, 2 * 32 * 128], BF16)
    xts = sb(p4, "xts", [128, 2 * TT], F32)
    mts = sb(p4, "mts", [128, 2 * TT], F32)
    ots = sb(p4, "ots", [128, 2 * TT], F32)
    aplybuf = sb(p4, "aplybuf", [128, 4 * TT], F32)
    mu_b4 = sb(p4, "mu_b4", [128, 4 * TT], F32)
    r_b4 = sb(p4, "r_b4", [128, 4 * TT], F32)
    wv(y, stl)
    for tt in range(NTT):
        ydma(mu_b4[:, tt * TT:(tt + 1) * TT],
             bass.AP(stats_d, 4 * TC + tt * TT, [[0, 128], [1, TT]]), stl)
        ydma(r_b4[:, tt * TT:(tt + 1) * TT],
             bass.AP(stats_d, 5 * TC + tt * TT, [[0, 128], [1, TT]]), stl)
    stats_rdy = stl.n

    def w1_load(mf):
        slot = mf % 4
        sdma(w1s[:, slot * 8 * 128:(slot + 1) * 8 * 128],
             bass.AP(w1_b, mf * 128, [[FF, 128], [128 * FF, 8], [1, 128]]),
             w1l[slot])
        w1ld[mf] = (slot, w1l[slot].n)

    def w2_load(mo):
        slot = mo % 2
        ydma(w2s[:, slot * 32 * 128:(slot + 1) * 32 * 128],
             bass.AP(w2_b, mo * 128, [[H, 128], [128 * H, 32], [1, 128]]),
             w2l[slot])
        w2ld[mo] = (slot, w2l[slot].n)

    for mf in range(4):
        w1_load(mf)
    w2_load(0)
    w2_load(1)
    for tt in range(NTT):
        wv(s, te)
        wv(v, stl, stats_rdy)
        for gh in range(2):
            wv(v, se)
            v.tensor_tensor(out=aplybuf[:],
                            in0=bass.AP(accf, gh * 4 * TC + tt * TT,
                                        [[8 * TC, 128], [TC, 4], [1, TT]]),
                            in1=bass.AP(mu_b4, tt * TT,
                                        [[4 * TT, 128], [0, 4], [1, TT]]),
                            op=OP.subtract)
            v.tensor_tensor(out=aplybuf[:], in0=aplybuf[:],
                            in1=bass.AP(r_b4, tt * TT,
                                        [[4 * TT, 128], [0, 4], [1, TT]]),
                            op=OP.mult)
            wv(s, ve)
            for hc4 in range(4):
                hc = gh * 4 + hc4
                s.activation(out=nstage[:, hc * TT:(hc + 1) * TT],
                             in_=aplybuf[:, hc4 * TT:(hc4 + 1) * TT],
                             func=AFT.Identity, scale=p_ln2g[:, hc:hc + 1],
                             bias=p_ln2b[:, hc:hc + 1])
        nsv = se.n
        for mf in range(32):
            pd3 = (ps_a, ps_b, ps_c, ps_dd)[mf % 4]
            slot = mf % 4
            wsl = w1s[:, slot * 8 * 128:(slot + 1) * 8 * 128]
            wv(t, w1l[w1ld[mf][0]], w1ld[mf][1])
            wv(t, se, nsv)
            bank_wait(pd3)
            for kc in range(8):
                mm = t.matmul(pd3[:, :], wsl[:, kc * 128:(kc + 1) * 128],
                              nstage[:, kc * TT:(kc + 1) * TT],
                              start=(kc == 0), stop=(kc == 7))
            te.inc(mm)
            wv(s, te, te.n)
            s.activation(out=hff[:, mf * TT:(mf + 1) * TT], in_=pd3[:, :],
                         func=AFT.Gelu, bias=p_b1[:, mf:mf + 1])
            bank_set(pd3, se)
            hff_done[mf] = se.n
            wv(s0, te, te.n)
            nxt = mf + 4
            if nxt < 32:
                w1_load(nxt)
            elif tt + 1 < NTT:
                w1_load(nxt - 32)
        for mo in range(8):
            pd4 = (ps_a, ps_b, ps_c, ps_dd)[mo % 4]
            slot = mo % 2
            wsl2 = w2s[:, slot * 32 * 128:(slot + 1) * 32 * 128]
            wv(t, w2l[w2ld[mo][0]], w2ld[mo][1])
            wv(t, se, hff_done[31])
            bank_wait(pd4)
            for kc in range(32):
                mm = t.matmul(pd4[:, :], wsl2[:, kc * 128:(kc + 1) * 128],
                              hff[:, kc * TT:(kc + 1) * TT],
                              start=(kc == 0), stop=(kc == 31))
            te.inc(mm)
            w2v = te.n
            opar = mo % 2
            osl = slice(opar * TT, (opar + 1) * TT)
            wv(y, ve, ots_done.get(mo - 2, 0))
            ydma(xts[:, osl], bass.AP(x_fm, mo * 128 * TC + tt * TT,
                                      [[TC, 128], [1, TT]]), xmls[opar])
            ydma(mts[:, osl],
                 moe_d.ap()[mo * 128:(mo + 1) * 128, tt * TT:(tt + 1) * TT],
                 xmls[opar])
            wv(v, te, w2v)
            wv(v, xmls[opar])
            wv(v, so, so.n)
            v.scalar_tensor_tensor(
                out=ots[:, osl], in0=pd4[:, :], scalar=p_b2[:, mo:mo + 1],
                in1=accf[:, mo * TC + tt * TT: mo * TC + (tt + 1) * TT],
                op0=OP.add, op1=OP.add)
            bank_set(pd4, ve)
            if "attno" in dbg_t:
                wv(y, ve)
                ydma(dbg_t["attno"].ap()[mo * 128:(mo + 1) * 128,
                                         tt * TT:(tt + 1) * TT], ots[:, osl])
                wv(v, ly)
            v.tensor_tensor(out=ots[:, osl], in0=ots[:, osl], in1=xts[:, osl],
                            op=OP.add)
            v.tensor_tensor(out=ots[:, osl], in0=ots[:, osl], in1=mts[:, osl],
                            op=OP.add)
            ots_done[mo] = ve.n
            wv(y, ve, ve.n)
            so.inc(y.dma_start(out=out_t.ap()[mo * 128:(mo + 1) * 128,
                                              tt * TT:(tt + 1) * TT],
                               in_=ots[:, osl]), 16)
            wv(y, te, w2v)
            if mo + 2 < 8:
                w2_load(mo + 2)
            elif tt + 1 < NTT:
                w2_load(mo + 2 - 8)
        ots_done.clear()
        hff_done.clear()
    y.wait_ge(so.h, so.n)
    barrier()
    p4.close()
    p4p.close()
    gst.close()
    return nc



def _prep_core(inputs, k):
    q = k % 2
    b = k // 2
    perm = np.arange(H) if q == 0 else np.r_[np.arange(512, 1024), np.arange(512)]
    asc = np.ascontiguousarray
    f32 = np.float32

    x = np.asarray(inputs["L0_attn_x"])
    xk = x[b, q * TC:(q + 1) * TC, :]
    d = {"x_fm": asc(xk.T[perm])}

    Wd = np.asarray(inputs["L0_attn_Wd"])
    d["wd_t"] = asc(Wd[np.ix_(perm, perm)].T)
    d["wbc_t"] = asc(np.asarray(inputs["L0_attn_Wbc"])[:, perm].T)
    d["w1_t"] = asc(np.asarray(inputs["L0_attn_W1"])[:, perm].T)
    d["w2_t"] = asc(np.asarray(inputs["L0_attn_W2"])[perm].T)
    d["rw_t"] = asc(np.asarray(inputs["L0_ffn_router_weight"])[:, perm].T)
    d["wg_e"] = asc(np.asarray(inputs["L0_ffn_w_gate"])[:, perm, :])
    d["wu_e"] = asc(np.asarray(inputs["L0_ffn_w_up"])[:, perm, :])
    d["wdn_e"] = asc(np.asarray(inputs["L0_ffn_w_down"])[:, :, perm])
    d["swg"] = asc(np.asarray(inputs["L0_ffn_sw_gate"])[perm])
    d["swu"] = asc(np.asarray(inputs["L0_ffn_sw_up"])[perm])
    d["swd"] = asc(np.asarray(inputs["L0_ffn_sw_down"])[:, perm])

    def col8(vv):
        return asc(np.asarray(vv)[perm].reshape(8, 128).T.astype(f32))

    d["ln1_g"] = col8(inputs["L0_attn_ln1_g"])
    d["ln1_b"] = col8(inputs["L0_attn_ln1_b"])
    d["ln2_g"] = col8(inputs["L0_attn_ln2_g"])
    d["ln2_b"] = col8(inputs["L0_attn_ln2_b"])
    d["bd_c"] = col8(inputs["L0_attn_bd"])
    d["b2_c"] = col8(inputs["L0_attn_b2"])
    d["d_c"] = col8(inputs["L0_attn_D"])
    d["b1_c"] = asc(np.asarray(inputs["L0_attn_b1"]).reshape(32, 128).T
                    .astype(f32))
    d["bbc_c"] = asc(np.asarray(inputs["L0_attn_bbc"]).reshape(32, 1)
                     .astype(f32))
    alog = np.asarray(inputs["L0_attn_A_log"])[perm[:HOH]]
    a2 = np.empty((128, 64), f32)
    for tile in range(64):
        for hl in range(8):
            for n in range(NS):
                a2[hl * NS + n, tile] = alog[tile * 8 + hl, n]
    d["alog2"] = asc(a2)
    es = np.zeros((128, 16 * 128), f32)
    for ktile in range(16):
        for p in range(128):
            es[p, ktile * 128 + 8 * ktile + p // NS] = 1.0
    d["esel_i"] = asc(es)
    cb = np.asarray(inputs["L0_ffn_correction_bias"]).astype(f32)
    d["cbt_i"] = asc(np.tile(np.tile(cb, 64), (32, 1)))

    gi = np.zeros((128, 20), np.uint32)
    for sh in range(2):
        src = 2 * b + sh
        for c in range(4):
            p = np.arange(128)
            i = c * 128 + p
            gl = q * 512 + i
            pos = gl if sh == 0 else (gl + 512) % H
            gi[:, sh * 4 + c] = src * AG1R + pos
            gi[:, 8 + sh * 4 + c] = src * H + pos
        gi[0:32, 16 + sh] = src * AG1R + H + np.arange(32)
    d["gidx"] = gi
    yi = np.zeros((128, 8), np.uint32)
    for hc in range(8):
        p = np.arange(128)
        i = hc * 128 + p
        row = np.where(i < 512,
                       q * 8 * HOH + k * HOH + i,
                       q * 8 * HOH + (k ^ 1) * HOH + (i - 512))
        yi[:, hc] = row * 4
    d["yidx"] = yi
    d = {kk: (vv if vv.dtype == np.uint32 else vv.astype(np.float32))
         for kk, vv in d.items()}
    return d, perm


LAST_RESULTS = None


def kernel(**inputs):
    global LAST_RESULTS
    in_maps = []
    perms = []
    for k in range(8):
        m, perm = _prep_core(inputs, k)
        in_maps.append(m)
        perms.append(perm)
    nc = build_program(dbg=KDBG)
    res = run_bass_kernel_spmd(nc, in_maps, core_ids=list(range(8)))
    LAST_RESULTS = res.results
    out = np.empty((4, L, H), np.float32)
    for k in range(8):
        q, b = k % 2, k // 2
        o_fm = res.results[k]["out"]
        o_nat = np.empty_like(o_fm)
        o_nat[perms[k]] = o_fm
        out[b, q * TC:(q + 1) * TC, :] = o_nat.T
    return out


if __name__ == "__main__":
    nc = build_program()
    print("built OK")
